# version 01
# baseline (speedup 1.0000x reference)
import contextlib
import math
import numpy as np

import concourse.bass as bass
import concourse.mybir as mybir
from concourse.bass_utils import run_bass_kernel_spmd

F32 = mybir.dt.float32
BF16 = mybir.dt.bfloat16
I32 = mybir.dt.int32
AF = mybir.ActivationFunctionType
ALU = mybir.AluOpType
AX = mybir.AxisListType.X

D = 2048
HEADS = 16
WIN = 256
FFN_H = 8192
NC8 = 8
CH = 512

INV_2PI = float(1.0 / (2.0 * math.pi))
TWO_PI_HI = 6.28125
TWO_PI_LO = float(2.0 * math.pi - 6.28125)
LN_THETA = float(-math.log(10000.0) / 64.0)
SCALE_A = float(1.0 / math.sqrt(128.0))
SCALE_B = 0.125


class Prog:
    def __init__(self):
        self.ops = {e: [] for e in ("sync", "scalar", "vector", "tensor", "gpsimd")}
        self.sems = {}
        self.frozen = False
        self.phase = 0
        self.max_phase = 99
        self.relay = None

    def ckpt(self):
        self.phase += 1
        if self.phase >= self.max_phase:
            self.frozen = True

    def add_sem(self, name, handle):
        self.sems[name] = [handle, 0]

    def sem(self, name):
        return self.sems[name][0]

    def count(self, name):
        return self.sems[name][1]

    def emit(self, eng, fn):
        if self.frozen:
            return
        self.ops[eng].append(fn)

    def wait(self, eng, name, val=None):
        h, c = self.sems[name]
        v = c if val is None else val
        if v > 0:
            self.emit(eng, lambda E, h=h, v=v: E.wait_ge(h, v))

    def do(self, eng, fn, inc=None):
        if self.frozen:
            return
        if inc is None:
            self.emit(eng, fn)
        else:
            name, n = inc
            rec = self.sems[name]
            rec[1] += n
            h = rec[0]
            self.emit(eng, lambda E, fn=fn, h=h, n=n: fn(E).then_inc(h, n))

    def dov(self, fn):
        self.do("vector", fn, inc=("sVE", 1))
        self.wait("vector", "sVE")

    def barrier(self):
        for eng in ("sync", "scalar", "vector", "tensor"):
            for name in self.sems:
                self.wait(eng, name)
        if self.relay is not None:
            self.relay()
            for name in ("sIN", "sK", "sGP", "sW", "sD0", "sD1", "sST",
                         "sXT", "sKV0", "sKV1"):
                self.wait("gpsimd", name)
        else:
            for name in self.sems:
                self.wait("gpsimd", name)


class Arena:
    def __init__(self, base_ap, nbytes):
        self.base = base_ap
        self.nbytes = nbytes
        self.cur = 0

    def reset(self):
        self.cur = 0

    def take(self, nfree, dtype):
        sz = mybir.dt.size(dtype)
        nb = nfree * sz
        self.cur = (self.cur + 63) // 64 * 64
        off = self.cur
        self.cur += nb
        assert self.cur <= self.nbytes, f"arena overflow: {self.cur} > {self.nbytes}"
        v = self.base[:, off // 4:(off + nb) // 4]
        if dtype == F32:
            return v
        return v.bitcast(dtype)


def chunks_of(ntok):
    out = []
    t = 0
    while t < ntok:
        n = min(4, (ntok - t) // 128)
        out.append((t, n))
        t += n * 128
    return out


def build_program(S, debug=False, max_phase=99):
    NCH = S // CH
    M = S // 128 // NC8
    OWN = M * 128
    nc = bass.Bass("TRN2", target_bir_lowering=False)

    x0 = nc.dram_tensor("x0", [S, D], F32, kind="ExternalInput").ap()
    win = {}
    for nm in ["wqa", "wka", "wva", "woa", "wqb", "wkb", "wvb", "wob"]:
        win[nm] = nc.dram_tensor(nm, [D, D], F32, kind="ExternalInput").ap()
    lam0 = nc.dram_tensor("lam0", [1, HEADS], F32, kind="ExternalInput").ap()
    wup = nc.dram_tensor("wup", [FFN_H, D], F32, kind="ExternalInput").ap()
    wdn = nc.dram_tensor("wdn", [D, FFN_H], F32, kind="ExternalInput").ap()
    scup = nc.dram_tensor("scup", [128, FFN_H // 128], F32,
                          kind="ExternalInput").ap()
    scdn = nc.dram_tensor("scdn", [1, D], F32, kind="ExternalInput").ap()
    scout = nc.dram_tensor("scout", [1, 1], F32, kind="ExternalInput").ap()
    core_slot = nc.dram_tensor("core_slot", [1, 1], I32, kind="ExternalInput").ap()
    dmask = nc.dram_tensor("dmask", [8 * 128, 128], BF16, kind="ExternalInput").ap()
    rampi = nc.dram_tensor("rampi", [128, CH], F32, kind="ExternalInput").ap()
    y = nc.dram_tensor("y", [OWN, D], F32, kind="ExternalOutput").ap()

    wbf = {}
    for nm in ["wqa", "wka", "wva", "woa", "wqb", "wkb", "wvb", "wob"]:
        wbf[nm] = nc.dram_tensor(nm + "_b", [D, D], BF16, kind="Internal").ap()
    upT = nc.dram_tensor("upT", [D, FFN_H], BF16, kind="Internal").ap()
    dnT = nc.dram_tensor("dnT", [FFN_H, D], BF16, kind="Internal").ap()
    ln1T_buf = nc.dram_tensor("ln1T_buf", [D, S], BF16, kind="Internal").ap()
    qa_buf = nc.dram_tensor("qa_buf", [D, S], BF16, kind="Internal").ap()
    ka_buf = nc.dram_tensor("ka_buf", [D, S], BF16, kind="Internal").ap()
    va_buf = nc.dram_tensor("va_buf", [S, D], BF16, kind="Internal").ap()
    x1_buf = nc.dram_tensor("x1_buf", [S, D], F32, kind="Internal").ap()
    ln2T_buf = nc.dram_tensor("ln2T_buf", [D, S], BF16, kind="Internal").ap()
    x2_buf = nc.dram_tensor("x2_buf", [S, D], F32, kind="Internal").ap()
    ktb_buf = nc.dram_tensor("ktb_buf", [D, S], BF16, kind="Internal").ap()
    vb_buf = nc.dram_tensor("vb_buf", [S, D], BF16, kind="Internal").ap()
    x2own_buf = nc.dram_tensor("x2own_buf", [OWN, D], F32, kind="Internal").ap()
    ln3oT_buf = nc.dram_tensor("ln3oT_buf", [D, OWN], BF16, kind="Internal").ap()
    x3own_buf = nc.dram_tensor("x3own_buf", [OWN, D], F32, kind="Internal").ap()
    ln4T_buf = nc.dram_tensor("ln4T_buf", [D, OWN], BF16, kind="Internal").ap()
    x4own_buf = nc.dram_tensor("x4own_buf", [OWN, D], F32, kind="Internal").ap()

    dbg = {}
    if debug:
        for nm, shp, dt in [
            ("d_ln1T", [D, S], BF16), ("d_qa", [D, S], BF16), ("d_ka", [D, S], BF16),
            ("d_va", [S, D], BF16), ("d_x1", [S, D], F32), ("d_x2", [S, D], F32),
            ("d_ktb", [D, S], BF16), ("d_vb", [S, D], BF16),
            ("d_upT", [D, FFN_H], BF16), ("d_dnT", [FFN_H, D], BF16),
            ("d_x2own", [OWN, D], F32), ("d_x3own", [OWN, D], F32),
        ]:
            dbg[nm] = nc.dram_tensor(nm, shp, dt, kind="ExternalOutput").ap()

    P = Prog()
    P.max_phase = max_phase

    with contextlib.ExitStack() as ctx:
        def sb(name, shape, dt):
            return ctx.enter_context(nc.sbuf_tensor(name, shape, dt))[:]

        ident = sb("ident", [128, 128], BF16)
        pmat = sb("pmat", [128, 128], BF16)
        onesb = sb("onesb", [128, 1], BF16)
        ones2 = sb("ones2", [128, 2], BF16)
        onerow = sb("onerow", [1, 128], BF16)
        zrow = sb("zrow", [1, 512], BF16)
        invf2 = sb("invf2", [128, 1], F32)
        dsrow = sb("dsrow", [128, D], F32)
        upsc = sb("upsc", [128, FFN_H // 128], F32)
        lams = sb("lams", [1, HEADS], F32)
        maskw = sb("maskw", [128, 2 * WIN], BF16)
        dmsk = sb("dmsk", [128, 8 * 128], BF16)
        core_sb = sb("core_sb", [1, 1], I32)
        ramp = sb("ramp", [128, CH], F32)
        stat = sb("stat", [128, 32], F32)
        srow = sb("srow", [1, 2048], F32)
        srow2 = sb("srow2", [1, 2048], F32)
        srowb = sb("srowb", [1, 2048], BF16)
        bcolq = sb("bcolq", [1, 2 * HEADS], F32)
        bcolk = sb("bcolk", [1, 2 * HEADS], F32)
        bcols = sb("bcols", [128, 2 * HEADS], F32)

        ARENA_B = 172 * 1024
        arena = Arena(sb("arena", [128, ARENA_B // 4], F32), ARENA_B)

        psb = [ctx.enter_context(nc.psum_tensor(f"ps{i}", [128, 512], F32))[:]
               for i in range(8)]
        pst = psb[7].bitcast(BF16)
        pst2 = psb[6].bitcast(BF16)

        for s_ in ["sIN", "sSC", "sVE", "sMM", "sGP", "sK", "sGC",
            "sW", "sD0", "sD1", "sST", "sXT", "sKV0", "sKV1"]:
            P.add_sem(s_, ctx.enter_context(nc.semaphore(s_)))

        blk = ctx.enter_context(nc.Block())

        def r_in(dram2d, a):
            return dram2d.rearrange("(a p) c -> p a c", p=128)

        def r_sb(sb2d, a):
            return sb2d.rearrange("p (a c) -> p a c", a=a)

        for nm in ["wqa", "wka", "wva", "woa", "wqb", "wkb", "wvb", "wob"]:
            P.do("gpsimd", lambda E, s=win[nm], d=wbf[nm]: E.dma_start(
                out=d, in_=s, max_dma_last_dim=8192), inc=("sGP", 16))
        wcast_done = P.count("sGP")

        P.do("gpsimd", lambda E: E.memset(ident, 0.0), inc=("sGC", 1))
        P.wait("gpsimd", "sGC")
        P.do("gpsimd", lambda E: E.affine_select(
            out=ident, in_=ident, compare_op=ALU.not_equal, fill=1.0, base=0,
            pattern=[[-1, 128]], channel_multiplier=1))
        P.do("gpsimd", lambda E: E.memset(pmat, 0.0), inc=("sGC", 1))
        P.wait("gpsimd", "sGC")
        P.do("gpsimd", lambda E: E.affine_select(
            out=pmat, in_=pmat, compare_op=ALU.not_equal, fill=-1.0, base=-64,
            pattern=[[-1, 128]], channel_multiplier=1), inc=("sGC", 1))
        P.wait("gpsimd", "sGC")
        P.do("gpsimd", lambda E: E.affine_select(
            out=pmat, in_=pmat, compare_op=ALU.not_equal, fill=1.0, base=64,
            pattern=[[-1, 128]], channel_multiplier=1), inc=("sGC", 1))
        P.wait("gpsimd", "sGC")
        P.do("gpsimd", lambda E: E.memset(onesb, 1.0))
        P.do("gpsimd", lambda E: E.memset(onerow, 1.0))
        P.do("gpsimd", lambda E: E.memset(zrow, 0.0))
        P.do("gpsimd", lambda E: E.memset(ones2, 0.0), inc=("sGC", 1))
        P.wait("gpsimd", "sGC")
        P.do("gpsimd", lambda E: E.memset(ones2[0:64, 0:1], 1.0))
        P.do("gpsimd", lambda E: E.memset(ones2[64:128, 1:2], 1.0))
        P.do("gpsimd", lambda E: E.memset(maskw, 1.0), inc=("sGC", 1))
        P.wait("gpsimd", "sGC")
        P.do("gpsimd", lambda E: E.affine_select(
            out=maskw[:, 0:WIN], in_=maskw[:, 0:WIN], compare_op=ALU.is_ge,
            fill=0.0, base=0, pattern=[[1, WIN]], channel_multiplier=-1))
        P.do("gpsimd", lambda E: E.affine_select(
            out=maskw[:, WIN:2 * WIN], in_=maskw[:, WIN:2 * WIN],
            compare_op=ALU.is_ge, fill=0.0, base=-128, pattern=[[1, WIN]],
            channel_multiplier=-1), inc=("sGC", 1))
        P.do("gpsimd", lambda E: E.iota(stat[:, 0:1], pattern=[[0, 1]], base=0,
                                        channel_multiplier=1,
                                        allow_small_or_imprecise_dtypes=True),
             inc=("sGC", 1))
        P.wait("vector", "sGC")
        P.dov(lambda E: E.tensor_single_scalar(invf2, stat[:, 0:1], 63.5,
                                               op=ALU.is_gt))
        P.dov(lambda E: E.tensor_scalar(invf2, invf2, -64.0, None, op0=ALU.mult))
        P.do("vector", lambda E: E.tensor_add(invf2, invf2, stat[:, 0:1]),
             inc=("sVE", 1))
        P.wait("scalar", "sVE")
        P.do("scalar", lambda E: E.activation(invf2, invf2, AF.Exp, scale=LN_THETA),
             inc=("sSC", 1))
        P.do("sync", lambda E: E.dma_start(out=r_sb(dmsk, 8),
                                           in_=r_in(dmask, 8)), inc=("sIN", 16))
        P.do("sync", lambda E: E.dma_start(out=core_sb, in_=core_slot), inc=("sIN", 16))

        def _relay():
            P.wait("sync", "sK")
            P.do("sync", lambda E: E.dma_start(out=core_sb, in_=core_slot),
                 inc=("sK", 16))
        P.relay = _relay
        P.do("sync", lambda E: E.dma_start(out=upsc, in_=scup), inc=("sIN", 16))
        P.do("sync", lambda E: E.dma_start(out=srow[0:1, 0:D], in_=scdn),
             inc=("sIN", 16))
        P.do("sync", lambda E: E.dma_start(out=srow2[0:1, 0:1], in_=scout),
             inc=("sIN", 16))
        P.do("sync", lambda E: E.dma_start(out=lams, in_=lam0), inc=("sIN", 16))
        P.do("sync", lambda E: E.dma_start(out=ramp, in_=rampi), inc=("sIN", 16))
        sm_in = P.count("sIN")
        sm_in2 = sm_in
        P.wait("scalar", "sIN", sm_in)
        P.do("scalar", lambda E: E.activation(lams, lams, AF.Sigmoid), inc=("sSC", 1))
        P.wait("vector", "sIN", sm_in)
        P.do("vector", lambda E: E.tensor_scalar(srow[0:1, 0:D], srow[0:1, 0:D],
                                                 srow2[0:1, 0:1], None, op0=ALU.mult),
             inc=("sVE", 1))
        P.wait("scalar", "sVE")
        P.do("scalar", lambda E: E.copy(srowb[0:1, 0:D], srow[0:1, 0:D]),
             inc=("sSC", 1))
        P.wait("tensor", "sSC")
        P.wait("tensor", "sGC")
        for i in range(4):
            P.do("tensor", lambda E, i=i: E.matmul(
                psb[5][:, 0:512], onerow, srowb[0:1, i * 512:(i + 1) * 512],
                start=True, stop=True), inc=("sMM", 1))
            P.wait("vector", "sMM")
            P.do("vector", lambda E, i=i: E.tensor_copy(
                dsrow[:, i * 512:(i + 1) * 512], psb[5][:, 0:512]), inc=("sVE", 1))
            P.wait("tensor", "sVE")

        arena.reset()
        wst32 = arena.take(2048, F32)
        wstbf = arena.take(2048, BF16)
        upTs = arena.take(16 * 1024, BF16)
        dnTs = arena.take(8 * 2048, BF16)

        for sup in range(FFN_H // 1024):
            for strip in range(8):
                h0 = sup * 1024 + strip * 128
                P.wait("sync", "sSC")
                P.do("sync", lambda E, h0=h0: E.dma_start(
                    out=wst32, in_=wup[h0:h0 + 128, :]), inc=("sW", 16))
                P.wait("scalar", "sW")
                P.wait("scalar", "sMM")
                P.do("scalar", lambda E: E.sign(wstbf, wst32), inc=("sSC", 1))
                P.wait("tensor", "sSC")
                for g in range(2):
                    bank = pst if g == 0 else pst2
                    P.wait("tensor", "sVE")
                    for t8 in range(8):
                        d16 = g * 8 + t8
                        P.do("tensor", lambda E, d16=d16, t8=t8, bank=bank:
                             E.transpose(bank[:, t8 * 128:(t8 + 1) * 128],
                                         wstbf[:, d16 * 128:(d16 + 1) * 128], ident),
                             inc=("sMM", 1) if t8 == 7 else None)
                    P.wait("vector", "sMM")
                    for t8 in range(8):
                        d16 = g * 8 + t8
                        P.do("vector", lambda E, d16=d16, t8=t8, strip=strip,
                             bank=bank: E.tensor_copy(
                                 upTs[:, d16 * 1024 + strip * 128:
                                      d16 * 1024 + strip * 128 + 128],
                                 bank[:, t8 * 128:(t8 + 1) * 128]),
                             inc=("sVE", 1) if t8 == 7 else None)
            P.wait("sync", "sVE")
            P.wait("sync", "sST")
            P.do("sync", lambda E, sup=sup: E.dma_start(
                out=r_in(upT[:, sup * 1024:(sup + 1) * 1024], 16),
                in_=r_sb(upTs, 16)), inc=("sST", 16))
            P.wait("vector", "sST")
        up_staged = P.count("sST")

        for hw in range(FFN_H // 1024):
            for dstrip in range(16):
                P.wait("sync", "sSC")
                P.do("sync", lambda E, hw=hw, dstrip=dstrip: E.dma_start(
                    out=wst32[:, 0:1024],
                    in_=wdn[dstrip * 128:(dstrip + 1) * 128,
                            hw * 1024:(hw + 1) * 1024]), inc=("sW", 16))
                P.wait("scalar", "sW")
                P.wait("scalar", "sMM")
                P.do("scalar", lambda E: E.sign(wstbf[:, 0:1024], wst32[:, 0:1024]),
                     inc=("sSC", 1))
                P.wait("tensor", "sSC")
                P.wait("tensor", "sVE")
                for h8 in range(8):
                    P.do("tensor", lambda E, h8=h8: E.transpose(
                        pst[:, h8 * 128:(h8 + 1) * 128],
                        wstbf[:, h8 * 128:(h8 + 1) * 128], ident),
                        inc=("sMM", 1) if h8 == 7 else None)
                P.wait("vector", "sMM")
                for h8 in range(8):
                    P.do("vector", lambda E, h8=h8, dstrip=dstrip: E.tensor_copy(
                        dnTs[:, h8 * 2048 + dstrip * 128:
                             h8 * 2048 + dstrip * 128 + 128],
                        pst[:, h8 * 128:(h8 + 1) * 128]),
                        inc=("sVE", 1) if h8 == 7 else None)
            P.wait("sync", "sVE")
            P.wait("sync", "sST")
            P.do("sync", lambda E, hw=hw: E.dma_start(
                out=r_in(dnT[hw * 1024:(hw + 1) * 1024, :], 8),
                in_=r_sb(dnTs, 8)), inc=("sST", 16))
            P.wait("vector", "sST")
        if debug:
            for s_ in ("sIN", "sST", "sW", "sD0", "sD1"):
                P.wait("gpsimd", s_)
            P.do("gpsimd", lambda E: E.dma_start(out=dbg["d_upT"], in_=upT),
                 inc=("sGP", 16))
            P.do("gpsimd", lambda E: E.dma_start(out=dbg["d_dnT"], in_=dnT),
                 inc=("sGP", 16))
        P.barrier()
        P.ckpt()

        def emit_ln_chunk(xc, lnc, sq, nt):
            for t in range(nt):
                P.do("vector", lambda E, t=t: E.reduce_sum(
                    stat[:, t:t + 1], xc[:, t * D:(t + 1) * D], axis=AX),
                    inc=("sVE", 1) if t == nt - 1 else None)
            for t in range(nt):
                P.do("scalar", lambda E, t=t: E.activation(
                    sq[:, t * D:(t + 1) * D], xc[:, t * D:(t + 1) * D], AF.Square),
                    inc=("sSC", 1) if t == nt - 1 else None)
            P.wait("vector", "sSC")
            for t in range(nt):
                P.do("vector", lambda E, t=t: E.reduce_sum(
                    stat[:, 4 + t:5 + t], sq[:, t * D:(t + 1) * D], axis=AX),
                    inc=("sVE", 1) if t == nt - 1 else None)
            P.wait("vector", "sVE")
            P.dov(lambda E, nt=nt: E.tensor_scalar(
                stat[:, 8:8 + nt], stat[:, 0:nt], 1.0 / D, None, op0=ALU.mult))
            P.dov(lambda E, nt=nt: E.tensor_mul(
                stat[:, 12:12 + nt], stat[:, 8:8 + nt], stat[:, 8:8 + nt]))
            P.dov(lambda E, nt=nt: E.tensor_scalar(
                stat[:, 16:16 + nt], stat[:, 4:4 + nt], 1.0 / D, None, op0=ALU.mult))
            P.dov(lambda E, nt=nt: E.tensor_sub(
                stat[:, 16:16 + nt], stat[:, 16:16 + nt], stat[:, 12:12 + nt]))
            P.do("vector", lambda E, nt=nt: E.tensor_scalar(
                stat[:, 16:16 + nt], stat[:, 16:16 + nt], 1.0, 1e-5, op0=ALU.mult,
                op1=ALU.add), inc=("sVE", 1))
            P.wait("scalar", "sVE")
            P.do("scalar", lambda E, nt=nt: E.activation(
                stat[:, 20:20 + nt], stat[:, 16:16 + nt], AF.Sqrt), inc=("sSC", 1))
            P.wait("vector", "sSC")
            P.dov(lambda E, nt=nt: E.reciprocal(
                stat[:, 24:24 + nt], stat[:, 20:20 + nt]))
            P.dov(lambda E, nt=nt: E.tensor_mul(
                stat[:, 28:28 + nt], stat[:, 8:8 + nt], stat[:, 24:24 + nt]))
            P.dov(lambda E, nt=nt: E.tensor_scalar(
                stat[:, 28:28 + nt], stat[:, 28:28 + nt], -1.0, None, op0=ALU.mult))
            for t in range(nt):
                P.do("vector", lambda E, t=t: E.tensor_scalar(
                    lnc[:, t * D:(t + 1) * D], xc[:, t * D:(t + 1) * D],
                    stat[:, 24 + t:25 + t], stat[:, 28 + t:29 + t],
                    op0=ALU.mult, op1=ALU.add), inc=("sVE", 1))

        def emit_tr_chunk(src, dst, nt):
            C = nt * 128
            P.wait("tensor", "sSC")
            P.wait("tensor", "sVE")
            for d16 in range(16):
                bank = pst if d16 % 2 == 0 else pst2
                P.wait("tensor", "sVE")
                for t in range(nt):
                    P.do("tensor", lambda E, d16=d16, t=t, bank=bank: E.transpose(
                        bank[:, t * 128:(t + 1) * 128],
                        src[:, t * D + d16 * 128:t * D + d16 * 128 + 128], ident),
                        inc=("sMM", 1) if t == nt - 1 else None)
                P.wait("vector", "sMM")
                P.do("vector", lambda E, d16=d16, C=C, bank=bank: E.tensor_copy(
                    dst[:, d16 * CH:d16 * CH + C], bank[:, 0:C]), inc=("sVE", 1))
            P.wait("tensor", "sVE")

        def load_w16(wdst, wsrc):
            P.wait("sync", "sGP")
            P.do("sync", lambda E, wdst=wdst, wsrc=wsrc: E.dma_start(
                out=r_sb(wdst, 16), in_=r_in(wsrc, 16)), inc=("sW", 16))
            P.wait("tensor", "sW")

        def wt(wsb_, k16):
            return wsb_[:, k16 * D:(k16 + 1) * D]

        def ln_pass(x_src, lnT_dst, ntok):
            arena.reset()
            xc = [arena.take(4 * D, F32), arena.take(4 * D, F32)]
            lnc = arena.take(4 * D, BF16)
            lnT = arena.take(16 * CH, BF16)
            sq = arena.take(4 * D, F32)
            free_at = [0, 0]
            for i, (t0, nt) in enumerate(chunks_of(ntok)):
                buf = xc[i % 2]
                sd = "sD0" if i % 2 == 0 else "sD1"
                P.wait("sync", "sVE", free_at[i % 2])
                P.do("sync", lambda E, t0=t0, nt=nt, buf=buf: E.dma_start(
                    out=r_sb(buf, 4)[:, 0:nt, :],
                    in_=r_in(x_src[t0:t0 + nt * 128, :], nt)), inc=(sd, 16))
                P.wait("vector", sd)
                P.wait("scalar", sd)
                emit_ln_chunk(buf, lnc, sq, nt)
                free_at[i % 2] = P.count("sVE")
                emit_tr_chunk(lnc, lnT, nt)
                P.wait("sync", "sVE")
                P.wait("sync", "sST")
                P.do("sync", lambda E, t0=t0, nt=nt: E.dma_start(
                    out=r_in(lnT_dst[:, t0:t0 + nt * 128], 16),
                    in_=r_sb(lnT, 16)[:, :, 0:nt * 128]), inc=("sST", 16))
                P.wait("vector", "sST")
            P.barrier()

        def rope_tables(t0, ntk, rc, rs, rw, iw, rw2, rw3):
            n = ntk
            P.wait("vector", "sIN", sm_in2)
            P.dov(lambda E, t0=t0: E.tensor_scalar(rw[:, 0:n], ramp[:, 0:n],
                                                   1.0, float(t0), op0=ALU.mult,
                                                   op1=ALU.add))
            P.dov(lambda E: E.tensor_scalar(rw2[:, 0:n], rw[:, 0:n], invf2, None,
                                            op0=ALU.mult))
            P.dov(lambda E: E.tensor_scalar(rw[:, 0:n], rw2[:, 0:n], INV_2PI, 0.5,
                                            op0=ALU.mult, op1=ALU.add))
            P.dov(lambda E: E.tensor_copy(iw[:, 0:n], rw[:, 0:n]))
            P.dov(lambda E: E.tensor_copy(rw[:, 0:n], iw[:, 0:n]))
            P.dov(lambda E: E.tensor_scalar(rw3[:, 0:n], rw[:, 0:n], -TWO_PI_HI,
                                            None, op0=ALU.mult))
            P.dov(lambda E: E.tensor_add(rs[:, 0:n], rw2[:, 0:n], rw3[:, 0:n]))
            P.dov(lambda E: E.tensor_scalar(rw3[:, 0:n], rw[:, 0:n], -TWO_PI_LO,
                                            None, op0=ALU.mult))
            P.dov(lambda E: E.tensor_add(rs[:, 0:n], rs[:, 0:n], rw3[:, 0:n]))
            P.dov(lambda E: E.tensor_single_scalar(rw3[:, 0:n], rs[:, 0:n],
                                                   float(math.pi), op=ALU.is_gt))
            P.dov(lambda E: E.scalar_tensor_tensor(rs[:, 0:n], rw3[:, 0:n],
                                                   float(-2.0 * math.pi),
                                                   rs[:, 0:n], op0=ALU.mult,
                                                   op1=ALU.add))
            P.dov(lambda E: E.tensor_single_scalar(rw3[:, 0:n], rs[:, 0:n],
                                                   float(-math.pi), op=ALU.is_lt))
            P.do("vector", lambda E: E.scalar_tensor_tensor(
                rs[:, 0:n], rw3[:, 0:n], float(2.0 * math.pi), rs[:, 0:n],
                op0=ALU.mult, op1=ALU.add), inc=("sVE", 1))
            P.wait("scalar", "sVE")
            P.do("scalar", lambda E: E.activation(rs[:, 0:n], rs[:, 0:n], AF.Sin),
                 inc=("sSC", 1))
            P.dov(lambda E: E.tensor_scalar(rw[:, 0:n], rw2[:, 0:n], INV_2PI, 0.75,
                                            op0=ALU.mult, op1=ALU.add))
            P.dov(lambda E: E.tensor_copy(iw[:, 0:n], rw[:, 0:n]))
            P.dov(lambda E: E.tensor_copy(rw[:, 0:n], iw[:, 0:n]))
            P.dov(lambda E: E.tensor_scalar(rw3[:, 0:n], rw[:, 0:n], -TWO_PI_HI,
                                            None, op0=ALU.mult))
            P.dov(lambda E: E.tensor_add(rc[:, 0:n], rw2[:, 0:n], rw3[:, 0:n]))
            P.dov(lambda E: E.tensor_scalar(rw3[:, 0:n], rw[:, 0:n], -TWO_PI_LO,
                                            None, op0=ALU.mult))
            P.dov(lambda E: E.tensor_add(rc[:, 0:n], rc[:, 0:n], rw3[:, 0:n]))
            P.dov(lambda E: E.tensor_scalar(rc[:, 0:n], rc[:, 0:n], 1.0,
                                            float(math.pi / 2.0),
                                            op0=ALU.mult, op1=ALU.add))
            P.dov(lambda E: E.tensor_single_scalar(rw3[:, 0:n], rc[:, 0:n],
                                                   float(math.pi), op=ALU.is_gt))
            P.dov(lambda E: E.scalar_tensor_tensor(rc[:, 0:n], rw3[:, 0:n],
                                                   float(-2.0 * math.pi),
                                                   rc[:, 0:n], op0=ALU.mult,
                                                   op1=ALU.add))
            P.dov(lambda E: E.tensor_single_scalar(rw3[:, 0:n], rc[:, 0:n],
                                                   float(-math.pi), op=ALU.is_lt))
            P.do("vector", lambda E: E.scalar_tensor_tensor(
                rc[:, 0:n], rw3[:, 0:n], float(2.0 * math.pi), rc[:, 0:n],
                op0=ALU.mult, op1=ALU.add), inc=("sVE", 1))
            P.wait("scalar", "sVE")
            P.do("scalar", lambda E: E.activation(rc[:, 0:n], rc[:, 0:n], AF.Sin),
                 inc=("sSC", 1))
            P.wait("vector", "sSC")

        def projT_pass(lnT_src, wsrc, dstbuf, rope, ntok):
            arena.reset()
            wsb_ = arena.take(16 * D, BF16)
            lnT = [arena.take(16 * CH, BF16), arena.take(16 * CH, BF16)]
            outT = arena.take(16 * CH, BF16)
            rc = arena.take(CH, F32)
            rs = arena.take(CH, F32)
            rw = arena.take(CH, F32)
            rw2 = arena.take(CH, F32)
            rw3 = arena.take(CH, F32)
            iw = arena.take(CH, I32)
            qnb = arena.take(CH, BF16)
            load_w16(wsb_, wsrc)
            free_at = [0, 0]
            for i, (t0, nt) in enumerate(chunks_of(ntok)):
                ntk = nt * 128
                lt = lnT[i % 2]
                sd = "sD0" if i % 2 == 0 else "sD1"
                P.wait("sync", "sMM", free_at[i % 2])
                P.do("sync", lambda E, t0=t0, ntk=ntk, lt=lt, sd=sd: E.dma_start(
                    out=r_sb(lt, 16)[:, :, 0:ntk],
                    in_=r_in(lnT_src[:, t0:t0 + ntk], 16)), inc=(sd, 16))
                if rope:
                    rope_tables(t0, ntk, rc, rs, rw, iw, rw2, rw3)
                P.wait("tensor", sd)
                for h16 in range(16):
                    bank = psb[h16 % 2]
                    P.wait("tensor", "sVE")
                    for k16 in range(16):
                        P.do("tensor", lambda E, h16=h16, k16=k16, bank=bank, lt=lt,
                             ntk=ntk: E.matmul(
                                 bank[:, 0:ntk],
                                 wt(wsb_, k16)[:, h16 * 128:(h16 + 1) * 128],
                                 lt[:, k16 * CH:k16 * CH + ntk],
                                 start=(k16 == 0), stop=(k16 == 15)),
                             inc=("sMM", 1) if k16 == 15 else None)
                    P.wait("vector", "sMM")
                    if rope:
                        dsl = outT[:, h16 * CH:h16 * CH + ntk]
                        rb = psb[2 + h16 % 2]
                        P.wait("scalar", "sMM")
                        P.do("scalar", lambda E, bank=bank, ntk=ntk: E.copy(
                            qnb[:, 0:ntk], bank[:, 0:ntk]), inc=("sSC", 1))
                        P.wait("tensor", "sSC")
                        P.wait("tensor", "sVE")
                        P.do("tensor", lambda E, rb=rb, ntk=ntk: E.matmul(
                            rb[:, 0:ntk], pmat, qnb[:, 0:ntk], start=True,
                            stop=True), inc=("sMM", 1))
                        P.wait("vector", "sMM")
                        P.do("vector", lambda E, ntk=ntk: E.tensor_mul(
                            rw[:, 0:ntk], qnb[:, 0:ntk], rc[:, 0:ntk]))
                        P.do("vector", lambda E, rb=rb, ntk=ntk: E.tensor_mul(
                            rw2[:, 0:ntk], rb[:, 0:ntk], rs[:, 0:ntk]),
                            inc=("sVE", 1))
                        P.wait("vector", "sVE")
                        P.do("vector", lambda E, dsl=dsl, ntk=ntk: E.tensor_add(
                            dsl, rw[:, 0:ntk], rw2[:, 0:ntk]), inc=("sVE", 1))
                    else:
                        P.do("vector", lambda E, h16=h16, bank=bank, ntk=ntk:
                             E.tensor_copy(outT[:, h16 * CH:h16 * CH + ntk],
                                           bank[:, 0:ntk]), inc=("sVE", 1))
                free_at[i % 2] = P.count("sMM")
                P.wait("sync", "sVE")
                P.wait("sync", "sST")
                P.do("sync", lambda E, t0=t0, ntk=ntk: E.dma_start(
                    out=r_in(dstbuf[:, t0:t0 + ntk], 16),
                    in_=r_sb(outT, 16)[:, :, 0:ntk]), inc=("sST", 16))
                P.wait("vector", "sST")
            P.barrier()

        def projN_pass(lnT_src, wsrc, dstbuf, ntok):
            arena.reset()
            wsb_ = arena.take(16 * D, BF16)
            lnT = [arena.take(16 * CH, BF16), arena.take(16 * CH, BF16)]
            outN = arena.take(4 * D, BF16)
            load_w16(wsb_, wsrc)
            free_at = [0, 0]
            for i, (t0, nt) in enumerate(chunks_of(ntok)):
                ntk = nt * 128
                lt = lnT[i % 2]
                sd = "sD0" if i % 2 == 0 else "sD1"
                P.wait("sync", "sMM", free_at[i % 2])
                P.do("sync", lambda E, t0=t0, ntk=ntk, lt=lt, sd=sd: E.dma_start(
                    out=r_sb(lt, 16)[:, :, 0:ntk],
                    in_=r_in(lnT_src[:, t0:t0 + ntk], 16)), inc=(sd, 16))
                P.wait("tensor", sd)
                for t in range(nt):
                    for n4 in range(4):
                        bank = psb[(t * 4 + n4) % 2]
                        P.wait("tensor", "sVE")
                        for k16 in range(16):
                            P.do("tensor", lambda E, t=t, n4=n4, k16=k16, bank=bank,
                                 lt=lt: E.matmul(
                                     bank[:, 0:512],
                                     lt[:, k16 * CH + t * 128:k16 * CH + t * 128 + 128],
                                     wt(wsb_, k16)[:, n4 * 512:(n4 + 1) * 512],
                                     start=(k16 == 0), stop=(k16 == 15)),
                                 inc=("sMM", 1) if k16 == 15 else None)
                        P.wait("vector", "sMM")
                        P.do("vector", lambda E, t=t, n4=n4, bank=bank: E.tensor_copy(
                            outN[:, t * D + n4 * 512:t * D + (n4 + 1) * 512],
                            bank[:, 0:512]), inc=("sVE", 1))
                free_at[i % 2] = P.count("sMM")
                P.wait("sync", "sVE")
                P.wait("sync", "sST")
                P.do("sync", lambda E, t0=t0, nt=nt: E.dma_start(
                    out=r_in(dstbuf[t0:t0 + nt * 128, :], nt),
                    in_=r_sb(outN, 4)[:, 0:nt, :]), inc=("sST", 16))
                P.wait("vector", "sST")
            P.barrier()

        ln_pass(x0, ln1T_buf, S)
        if debug:
            P.wait("gpsimd", "sST")
            P.do("gpsimd", lambda E: E.dma_start(out=dbg["d_ln1T"], in_=ln1T_buf),
                 inc=("sGP", 16))
        P.ckpt()
        projT_pass(ln1T_buf, wbf["wqa"], qa_buf, True, S)
        P.ckpt()
        projT_pass(ln1T_buf, wbf["wka"], ka_buf, True, S)
        P.ckpt()
        projN_pass(ln1T_buf, wbf["wva"], va_buf, S)
        if debug:
            P.wait("gpsimd", "sST")
            P.do("gpsimd", lambda E: E.dma_start(out=dbg["d_qa"], in_=qa_buf),
                 inc=("sGP", 16))
            P.do("gpsimd", lambda E: E.dma_start(out=dbg["d_ka"], in_=ka_buf),
                 inc=("sGP", 16))
            P.do("gpsimd", lambda E: E.dma_start(out=dbg["d_va"], in_=va_buf),
                 inc=("sGP", 16))

        P.ckpt()
        arena.reset()
        wsb_o = arena.take(16 * D, BF16)
        qtc = arena.take(16 * CH, BF16)
        KW = WIN + CH
        ktc = arena.take(16 * KW, BF16)
        vc = arena.take(6 * D, BF16)
        otc = arena.take(16 * CH, BF16)
        xtile = arena.take(D, F32)
        pb = arena.take(4 * WIN, BF16)
        sqa = arena.take(CH + KW, BF16)
        rwa = arena.take(WIN, F32)

        load_w16(wsb_o, wbf["woa"])

        def ktt(h16, c0, cn):
            return ktc[:, h16 * KW + c0:h16 * KW + c0 + cn]

        import os as _os
        _p4n = int(_os.environ.get("P4N", "99"))
        _p4h = int(_os.environ.get("P4H", "99"))
        _p4w = int(_os.environ.get("P4W", "99"))
        for ch in range(min(NCH, _p4n)):
            P.wait("sync", "sMM")
            P.do("sync", lambda E, ch=ch: E.dma_start(
                out=r_sb(qtc, 16), in_=r_in(qa_buf[:, ch * CH:(ch + 1) * CH], 16)),
                inc=("sD0", 16))
            lo = max(ch * CH - WIN, 0)
            halo = ch * CH - lo
            P.do("sync", lambda E, ch=ch, lo=lo, halo=halo: E.dma_start(
                out=r_sb(ktc, 16)[:, :, WIN - halo:KW],
                in_=r_in(ka_buf[:, lo:(ch + 1) * CH], 16)), inc=("sD0", 16))
            a0 = (WIN - halo) // 128
            P.do("sync", lambda E, ch=ch, lo=lo, a0=a0: E.dma_start(
                out=r_sb(vc, 6)[:, a0:6, :],
                in_=r_in(va_buf[lo:(ch + 1) * CH, :], 6 - a0)), inc=("sD0", 16))
            if ch == 0:
                P.do("gpsimd", lambda E: E.memset(r_sb(ktc, 16)[:, :, 0:WIN], 0.0),
                     inc=("sGC", 1))
                P.do("gpsimd", lambda E: E.memset(vc[:, 0:2 * D], 0.0),
                     inc=("sGC", 1))
                P.wait("tensor", "sGC")
                P.wait("vector", "sGC")
            P.wait("tensor", "sD0")
            P.wait("vector", "sD0")

            for h16 in range(min(HEADS, _p4h)):
                P.do("vector", lambda E, h16=h16: E.tensor_mul(
                    sqa[:, 0:CH], qtc[:, h16 * CH:(h16 + 1) * CH],
                    qtc[:, h16 * CH:(h16 + 1) * CH]))
                P.do("vector", lambda E, h16=h16: E.tensor_mul(
                    sqa[:, CH:CH + KW], ktt(h16, 0, KW), ktt(h16, 0, KW)),
                    inc=("sVE", 1))
                P.wait("tensor", "sVE")
                P.do("tensor", lambda E: E.matmul(psb[5][0:1, 0:CH], onesb,
                                                  sqa[:, 0:CH], start=True, stop=True))
                P.do("tensor", lambda E: E.matmul(psb[4][0:1, 0:512], onesb,
                                                  sqa[:, CH:CH + 512],
                                                  start=True, stop=True))
                P.do("tensor", lambda E: E.matmul(psb[3][0:1, 0:KW - 512], onesb,
                                                  sqa[:, CH + 512:CH + KW],
                                                  start=True, stop=True),
                     inc=("sMM", 1))
                P.wait("vector", "sMM")
                P.do("vector", lambda E: E.reduce_max(srow[0:1, 0:1],
                                                      psb[5][0:1, 0:CH], axis=AX))
                P.do("vector", lambda E: E.reduce_max(srow[0:1, 1:2],
                                                      psb[4][0:1, 0:512], axis=AX))
                P.dov(lambda E: E.reduce_max(srow[0:1, 3:4],
                                             psb[3][0:1, 0:KW - 512], axis=AX))
                P.dov(lambda E: E.tensor_max(srow[0:1, 1:2], srow[0:1, 1:2],
                                             srow[0:1, 3:4]))
                P.dov(lambda E: E.tensor_add(srow[0:1, 0:1], srow[0:1, 0:1],
                                             srow[0:1, 1:2]))
                P.do("vector", lambda E: E.tensor_scalar(
                    srow[0:1, 2:3], srow[0:1, 0:1], -0.5 * SCALE_A, None,
                    op0=ALU.mult), inc=("sVE", 1))
                P.wait("scalar", "sVE")
                P.do("scalar", lambda E: E.copy(srowb[0:1, 0:1], srow[0:1, 2:3]),
                     inc=("sSC", 1))
                P.wait("tensor", "sSC")
                P.do("tensor", lambda E: E.matmul(psb[5][:, 0:1], onerow,
                                                  srowb[0:1, 0:1], start=True,
                                                  stop=True), inc=("sMM", 1))
                P.wait("scalar", "sMM")
                P.do("scalar", lambda E: E.copy(stat[:, 8:9], psb[5][:, 0:1]),
                     inc=("sSC", 1))

                for w2 in range(2 if _p4w >= 2 else 0):
                    q0 = w2 * WIN
                    P.wait("tensor", "sSC")
                    P.wait("tensor", "sVE")
                    for kb in range(4):
                        bank = psb[2] if kb < 2 else psb[3]
                        P.do("tensor", lambda E, h16=h16, q0=q0, kb=kb, bank=bank:
                             E.matmul(bank[:, (kb % 2) * WIN:(kb % 2 + 1) * WIN],
                                      ktt(h16, q0 + kb * 128, 128),
                                      qtc[:, h16 * CH + q0:h16 * CH + q0 + WIN],
                                      start=True, stop=True),
                             inc=("sMM", 1) if kb == 3 else None)
                    P.wait("scalar", "sMM")
                    P.do("scalar", lambda E: E.activation(
                        pb[:, 0:2 * WIN], psb[2][:, 0:2 * WIN], AF.Exp,
                        bias=stat[:, 8:9], scale=SCALE_A))
                    P.do("scalar", lambda E: E.activation(
                        pb[:, 2 * WIN:4 * WIN], psb[3][:, 0:2 * WIN], AF.Exp,
                        bias=stat[:, 8:9], scale=SCALE_A), inc=("sSC", 1))
                    P.wait("vector", "sSC")
                    P.do("vector", lambda E: E.tensor_mul(
                        pb[:, 2 * WIN:3 * WIN], pb[:, 2 * WIN:3 * WIN],
                        maskw[:, 0:WIN]))
                    P.do("vector", lambda E: E.tensor_mul(
                        pb[:, 3 * WIN:4 * WIN], pb[:, 3 * WIN:4 * WIN],
                        maskw[:, WIN:2 * WIN]), inc=("sVE", 1))
                    if ch == 0 and w2 == 0:
                        P.do("vector", lambda E: E.memset(pb[:, 0:2 * WIN], 0.0),
                             inc=("sVE", 1))
                        P.wait("tensor", "sVE")
                    P.wait("tensor", "sVE")
                    if _p4w < 3:
                        continue
                    P.do("tensor", lambda E: E.matmul(
                        psb[5][0:1, 0:2 * WIN], onesb, pb[:, 0:2 * WIN],
                        start=True, stop=True))
                    P.do("tensor", lambda E: E.matmul(
                        psb[6][0:1, 0:2 * WIN], onesb, pb[:, 2 * WIN:4 * WIN],
                        start=True, stop=True), inc=("sMM", 1))
                    for kb in range(4):
                        r0 = (q0 + kb * 128) // 128
                        P.do("tensor", lambda E, h16=h16, kb=kb, r0=r0: E.matmul(
                            psb[4][:, 0:WIN],
                            vc[:, r0 * D + h16 * 128:r0 * D + h16 * 128 + 128],
                            pb[:, kb * WIN:(kb + 1) * WIN],
                            start=(kb == 0), stop=(kb == 3)),
                            inc=("sMM", 1) if kb == 3 else None)
                    P.wait("vector", "sMM")
                    if _p4w < 4:
                        continue
                    P.dov(lambda E: E.tensor_copy(
                        srow[0:1, 0:WIN], psb[5][0:1, 0:WIN]))
                    P.dov(lambda E: E.tensor_add(
                        srow[0:1, 0:WIN], srow[0:1, 0:WIN],
                        psb[5][0:1, WIN:2 * WIN]))
                    P.dov(lambda E: E.tensor_add(
                        srow[0:1, 0:WIN], srow[0:1, 0:WIN], psb[6][0:1, 0:WIN]))
                    P.dov(lambda E: E.tensor_add(
                        srow[0:1, 0:WIN], srow[0:1, 0:WIN],
                        psb[6][0:1, WIN:2 * WIN]))
                    P.do("vector", lambda E: E.reciprocal(
                        srow2[0:1, 0:WIN], srow[0:1, 0:WIN]), inc=("sVE", 1))
                    P.wait("scalar", "sVE")
                    P.do("scalar", lambda E: E.copy(srowb[0:1, 0:WIN],
                                                    srow2[0:1, 0:WIN]),
                         inc=("sSC", 1))
                    P.wait("tensor", "sSC")
                    P.do("tensor", lambda E: E.matmul(
                        psb[5][:, 0:WIN], onerow, srowb[0:1, 0:WIN],
                        start=True, stop=True), inc=("sMM", 1))
                    P.wait("vector", "sMM")
                    P.dov(lambda E: E.tensor_copy(rwa, psb[5][:, 0:WIN]))
                    P.do("vector", lambda E, h16=h16, q0=q0: E.tensor_mul(
                        otc[:, h16 * CH + q0:h16 * CH + q0 + WIN],
                        psb[4][:, 0:WIN], rwa), inc=("sVE", 1))
                    P.wait("tensor", "sVE")

            for t4 in range(4 if _p4w >= 5 else 0):
                P.wait("sync", "sST")
                P.wait("sync", "sVE")
                P.do("sync", lambda E, ch=ch, t4=t4: E.dma_start(
                    out=xtile,
                    in_=x0[ch * CH + t4 * 128:ch * CH + (t4 + 1) * 128, :]),
                    inc=("sXT", 16))
                P.wait("vector", "sXT")
                for n4 in range(4 if _p4w >= 6 else 0):
                    bank = psb[n4 % 2]
                    P.wait("tensor", "sVE")
                    for k16 in range(16):
                        P.do("tensor", lambda E, t4=t4, n4=n4, k16=k16, bank=bank:
                             E.matmul(bank[:, 0:512],
                                      otc[:, k16 * CH + t4 * 128:
                                          k16 * CH + t4 * 128 + 128],
                                      wt(wsb_o, k16)[:, n4 * 512:(n4 + 1) * 512],
                                      start=(k16 == 0), stop=(k16 == 15)),
                             inc=("sMM", 1) if k16 == 15 else None)
                    P.wait("vector", "sMM")
                    _p4x = _os.environ.get("P4X", "add")
                    if _p4x == "add":
                        P.do("vector", lambda E, n4=n4, bank=bank: E.tensor_add(
                            xtile[:, n4 * 512:(n4 + 1) * 512],
                            xtile[:, n4 * 512:(n4 + 1) * 512], bank[:, 0:512]),
                            inc=("sVE", 1))
                    elif _p4x == "cp":
                        P.do("vector", lambda E, n4=n4, bank=bank: E.tensor_copy(
                            xtile[:, n4 * 512:(n4 + 1) * 512], bank[:, 0:512]),
                            inc=("sVE", 1))
                    else:
                        P.do("vector", lambda E: E.engine_nop(), inc=("sVE", 1))
                if _p4w < 7:
                    continue
                P.wait("sync", "sVE")
                P.do("sync", lambda E, ch=ch, t4=t4: E.dma_start(
                    out=x1_buf[ch * CH + t4 * 128:ch * CH + (t4 + 1) * 128, :],
                    in_=xtile), inc=("sST", 16))
        P.barrier()
        if debug:
            P.wait("gpsimd", "sST")
            P.do("gpsimd", lambda E: E.dma_start(out=dbg["d_x1"], in_=x1_buf),
                 inc=("sGP", 16))

        P.ckpt()
        def ffn_pass(lnT_src, x_src, x_dst, ntok):
            arena.reset()
            wsl = arena.take(16 * 1024, BF16)
            lnT = [arena.take(16 * CH, BF16), arena.take(16 * CH, BF16)]
            hid = arena.take(64 * CH, BF16)
            xc = arena.take(4 * D, F32)
            ac = arena.take(512, F32)
            free_at = [0, 0]
            wsl_free = 0
            for i, (t0, nt) in enumerate(chunks_of(ntok)):
                ntk = nt * 128
                lt = lnT[i % 2]
                sd = "sD0" if i % 2 == 0 else "sD1"
                P.wait("sync", "sMM", free_at[i % 2])
                P.do("sync", lambda E, t0=t0, ntk=ntk, lt=lt, sd=sd: E.dma_start(
                    out=r_sb(lt, 16)[:, :, 0:ntk],
                    in_=r_in(lnT_src[:, t0:t0 + ntk], 16)), inc=(sd, 16))
                P.wait("sync", "sST")
                P.do("sync", lambda E, t0=t0, nt=nt: E.dma_start(
                    out=r_sb(xc, 4)[:, 0:nt, :],
                    in_=r_in(x_src[t0:t0 + nt * 128, :], nt)), inc=("sXT", 16))
                xc_in = P.count("sXT")
                P.wait("tensor", sd)
                P.wait("tensor", "sVE")
                for s8 in range(8):
                    P.wait("sync", "sMM", wsl_free)
                    P.do("sync", lambda E, s8=s8: E.dma_start(
                        out=r_sb(wsl, 16),
                        in_=r_in(upT[:, s8 * 1024:(s8 + 1) * 1024], 16)),
                        inc=("sW", 16))
                    P.wait("tensor", "sW")
                    for h8 in range(8):
                        hg = s8 * 8 + h8
                        bank = psb[h8 % 2]
                        P.wait("tensor", "sSC")
                        for k16 in range(16):
                            P.do("tensor", lambda E, h8=h8, k16=k16, bank=bank,
                                 lt=lt, ntk=ntk: E.matmul(
                                     bank[:, 0:ntk],
                                     wsl[:, k16 * 1024 + h8 * 128:
                                         k16 * 1024 + (h8 + 1) * 128],
                                     lt[:, k16 * CH:k16 * CH + ntk],
                                     start=(k16 == 0), stop=(k16 == 15)),
                                 inc=("sMM", 1) if k16 == 15 else None)
                        P.wait("scalar", "sMM")
                        P.do("scalar", lambda E, hg=hg, bank=bank, ntk=ntk:
                             E.activation(hid[:, hg * CH:hg * CH + ntk],
                                          bank[:, 0:ntk], AF.Relu,
                                          scale=upsc[:, hg:hg + 1]), inc=("sSC", 1))
                    wsl_free = P.count("sMM")
                P.wait("vector", "sXT", xc_in)
                for half in range(2):
                    for s8 in range(8):
                        P.wait("sync", "sMM", wsl_free)
                        P.do("sync", lambda E, s8=s8, half=half: E.dma_start(
                            out=wsl[:, 0:8 * 1024]
                            .rearrange("p (a c) -> p a c", a=8),
                            in_=r_in(dnT[s8 * 1024:(s8 + 1) * 1024,
                                         half * 1024:(half + 1) * 1024], 8)),
                            inc=("sW", 16))
                        P.wait("tensor", "sW")
                        P.wait("tensor", "sSC")
                        if s8 == 0:
                            P.wait("tensor", "sVE")
                        for t in range(nt):
                            for n2 in range(2):
                                bank = psb[t * 2 + n2]
                                for k8 in range(8):
                                    hg = s8 * 8 + k8
                                    P.do("tensor", lambda E, t=t, n2=n2, k8=k8,
                                         hg=hg, bank=bank, s8=s8: E.matmul(
                                             bank[:, 0:512],
                                             hid[:, hg * CH + t * 128:
                                                 hg * CH + t * 128 + 128],
                                             wsl[:, k8 * 1024 + n2 * 512:
                                                 k8 * 1024 + (n2 + 1) * 512],
                                             start=(s8 == 0 and k8 == 0),
                                             stop=(s8 == 7 and k8 == 7)),
                                         inc=("sMM", 1)
                                         if (k8 == 7) else None)
                        wsl_free = P.count("sMM")
                    P.wait("vector", "sMM")
                    for t in range(nt):
                        for n2 in range(2):
                            bank = psb[t * 2 + n2]
                            d0 = half * 1024 + n2 * 512
                            P.wait("vector", "sVE")
                            P.dov(lambda E, bank=bank, d0=d0: E.tensor_mul(
                                ac, bank[:, 0:512], dsrow[:, d0:d0 + 512]))
                            P.do("vector", lambda E, t=t, d0=d0: E.tensor_add(
                                xc[:, t * D + d0:t * D + d0 + 512],
                                xc[:, t * D + d0:t * D + d0 + 512], ac),
                                inc=("sVE", 1))
                free_at[i % 2] = P.count("sMM")
                P.wait("sync", "sVE")
                P.do("sync", lambda E, t0=t0, nt=nt: E.dma_start(
                    out=r_in(x_dst[t0:t0 + nt * 128, :], nt),
                    in_=r_sb(xc, 4)[:, 0:nt, :]), inc=("sST", 16))
            P.barrier()

        ln_pass(x1_buf, ln2T_buf, S)
        ffn_pass(ln2T_buf, x1_buf, x2_buf, S)
        if debug:
            P.wait("gpsimd", "sST")
            P.do("gpsimd", lambda E: E.dma_start(out=dbg["d_x2"], in_=x2_buf),
                 inc=("sGP", 16))

        P.ckpt()
        ln3T_buf = ln1T_buf
        ln_pass(x2_buf, ln3T_buf, S)
        projT_pass(ln3T_buf, wbf["wkb"], ktb_buf, False, S)
        projN_pass(ln3T_buf, wbf["wvb"], vb_buf, S)

        arena.reset()
        ktc2 = arena.take(16 * CH, BF16)
        sqk = arena.take(16 * CH, BF16)
        P.do("gpsimd", lambda E: E.memset(bcolk, 0.0), inc=("sGC", 1))
        P.do("gpsimd", lambda E: E.memset(bcolq, 0.0), inc=("sGC", 1))
        P.wait("vector", "sGC")
        for ch in range(NCH):
            P.wait("sync", "sVE")
            P.do("sync", lambda E, ch=ch: E.dma_start(
                out=r_sb(ktc2, 16), in_=r_in(ktb_buf[:, ch * CH:(ch + 1) * CH], 16)),
                inc=("sD0", 16))
            P.wait("vector", "sD0")
            for h16 in range(16):
                P.do("vector", lambda E, h16=h16: E.tensor_mul(
                    sqk[:, h16 * CH:(h16 + 1) * CH],
                    ktc2[:, h16 * CH:(h16 + 1) * CH],
                    ktc2[:, h16 * CH:(h16 + 1) * CH]),
                    inc=("sVE", 1) if h16 == 15 else None)
            P.wait("tensor", "sVE")
            for h16 in range(16):
                bank = psb[h16 % 2]
                bank2 = psb[2 + h16 % 2]
                P.do("tensor", lambda E, h16=h16, bank=bank: E.matmul(
                    bank[0:1, 0:CH], onesb[0:64, :],
                    sqk[0:64, h16 * CH:(h16 + 1) * CH],
                    start=True, stop=True))
                P.do("tensor", lambda E, h16=h16, bank2=bank2: E.matmul(
                    bank2[0:1, 0:CH], onesb[64:128, :],
                    sqk[64:128, h16 * CH:(h16 + 1) * CH],
                    start=True, stop=True), inc=("sMM", 1))
                P.wait("vector", "sMM")
                P.wait("vector", "sVE")
                P.do("vector", lambda E, h16=h16, bank=bank: E.reduce_max(
                    srow[0:1, 4:5], bank[0:1, 0:CH], axis=AX))
                P.dov(lambda E, h16=h16, bank2=bank2: E.reduce_max(
                    srow[0:1, 5:6], bank2[0:1, 0:CH], axis=AX))
                P.dov(lambda E, h16=h16: E.tensor_max(
                    bcolk[0:1, 2 * h16:2 * h16 + 1],
                    bcolk[0:1, 2 * h16:2 * h16 + 1], srow[0:1, 4:5]))
                P.do("vector", lambda E, h16=h16: E.tensor_max(
                    bcolk[0:1, 2 * h16 + 1:2 * h16 + 2],
                    bcolk[0:1, 2 * h16 + 1:2 * h16 + 2], srow[0:1, 5:6]),
                    inc=("sVE", 1))
                if h16 % 2 == 1:
                    P.wait("tensor", "sVE")
        P.barrier()
        if debug:
            P.wait("gpsimd", "sST")
            P.do("gpsimd", lambda E: E.dma_start(out=dbg["d_ktb"], in_=ktb_buf),
                 inc=("sGP", 16))
            P.do("gpsimd", lambda E: E.dma_start(out=dbg["d_vb"], in_=vb_buf),
                 inc=("sGP", 16))
            P.wait("gpsimd", "sGP")

        P.ckpt()
        def gather_switch(E):
            v = bass.Bass.values_load(nc, core_sb, engines=[mybir.EngineType.Pool],
                                      skip_runtime_bounds_check=True,
                                      min_val=0, max_val=7)
            reg = E.snap(v)
            for j in nc.Switch(engines=[E], index=[reg], n=NC8):
                for mm_ in range(M):
                    T = NC8 * mm_ + j
                    E.dma_start(out=x2own_buf[mm_ * 128:(mm_ + 1) * 128, :],
                                in_=x2_buf[T * 128:(T + 1) * 128, :]
                                ).then_inc(P.sem("sGP"), 16)

        if not P.frozen:
            P.sems["sGP"][1] += 16 * M
            P.emit("gpsimd", gather_switch)
        P.barrier()
        if debug:
            P.wait("gpsimd", "sST")
            P.do("gpsimd", lambda E: E.dma_start(out=dbg["d_x2own"], in_=x2own_buf),
                 inc=("sGP", 16))

        ln_pass(x2own_buf, ln3oT_buf, OWN)

        arena.reset()
        wsb_q = arena.take(16 * D, BF16)
        qto = arena.take(16 * OWN, BF16)
        lnTq = arena.take(16 * CH, BF16)
        load_w16(wsb_q, wbf["wqb"])
        for i, (t0, nt) in enumerate(chunks_of(OWN)):
            ntk = nt * 128
            P.wait("sync", "sMM")
            P.do("sync", lambda E, t0=t0, ntk=ntk: E.dma_start(
                out=r_sb(lnTq, 16)[:, :, 0:ntk],
                in_=r_in(ln3oT_buf[:, t0:t0 + ntk], 16)), inc=("sD0", 16))
            P.wait("tensor", "sD0")
            for h16 in range(16):
                bank = psb[h16 % 2]
                P.wait("tensor", "sVE")
                for k16 in range(16):
                    P.do("tensor", lambda E, h16=h16, k16=k16, bank=bank, ntk=ntk:
                         E.matmul(bank[:, 0:ntk],
                                  wt(wsb_q, k16)[:, h16 * 128:(h16 + 1) * 128],
                                  lnTq[:, k16 * CH:k16 * CH + ntk],
                                  start=(k16 == 0), stop=(k16 == 15)),
                         inc=("sMM", 1) if k16 == 15 else None)
                P.wait("vector", "sMM")
                P.do("vector", lambda E, h16=h16, t0=t0, ntk=ntk, bank=bank:
                     E.tensor_copy(qto[:, h16 * OWN + t0:h16 * OWN + t0 + ntk],
                                   bank[:, 0:ntk]), inc=("sVE", 1))
        sqq = lnTq
        P.wait("vector", "sVE")
        for i, (t0, nt) in enumerate(chunks_of(OWN)):
            ntk = nt * 128
            for h16 in range(16):
                P.do("vector", lambda E, h16=h16, t0=t0, ntk=ntk: E.tensor_mul(
                    sqq[:, h16 * CH:h16 * CH + ntk],
                    qto[:, h16 * OWN + t0:h16 * OWN + t0 + ntk],
                    qto[:, h16 * OWN + t0:h16 * OWN + t0 + ntk]),
                    inc=("sVE", 1) if h16 == 15 else None)
            P.wait("tensor", "sVE")
            for h16 in range(16):
                bank = psb[h16 % 2]
                bank2 = psb[2 + h16 % 2]
                P.do("tensor", lambda E, h16=h16, ntk=ntk, bank=bank: E.matmul(
                    bank[0:1, 0:ntk], onesb[0:64, :],
                    sqq[0:64, h16 * CH:h16 * CH + ntk],
                    start=True, stop=True))
                P.do("tensor", lambda E, h16=h16, ntk=ntk, bank2=bank2: E.matmul(
                    bank2[0:1, 0:ntk], onesb[64:128, :],
                    sqq[64:128, h16 * CH:h16 * CH + ntk],
                    start=True, stop=True), inc=("sMM", 1))
                P.wait("vector", "sMM")
                P.wait("vector", "sVE")
                P.do("vector", lambda E, ntk=ntk, bank=bank: E.reduce_max(
                    srow[0:1, 4:5], bank[0:1, 0:ntk], axis=AX))
                P.dov(lambda E, ntk=ntk, bank2=bank2: E.reduce_max(
                    srow[0:1, 5:6], bank2[0:1, 0:ntk], axis=AX))
                P.dov(lambda E, h16=h16: E.tensor_max(
                    bcolq[0:1, 2 * h16:2 * h16 + 1],
                    bcolq[0:1, 2 * h16:2 * h16 + 1], srow[0:1, 4:5]))
                P.do("vector", lambda E, h16=h16: E.tensor_max(
                    bcolq[0:1, 2 * h16 + 1:2 * h16 + 2],
                    bcolq[0:1, 2 * h16 + 1:2 * h16 + 2], srow[0:1, 5:6]),
                    inc=("sVE", 1))
                if h16 % 2 == 1:
                    P.wait("tensor", "sVE")
        P.wait("vector", "sVE")
        P.dov(lambda E: E.tensor_add(srow[0:1, 8:40], bcolq[0:1, 0:32],
                                     bcolk[0:1, 0:32]))
        P.do("vector", lambda E: E.tensor_scalar(
            srow[0:1, 8:40], srow[0:1, 8:40], -0.5 * SCALE_B, None, op0=ALU.mult),
            inc=("sVE", 1))
        P.wait("scalar", "sVE")
        P.do("scalar", lambda E: E.copy(srowb[0:1, 0:32], srow[0:1, 8:40]),
             inc=("sSC", 1))
        P.wait("tensor", "sSC")
        P.do("tensor", lambda E: E.matmul(psb[2][:, 0:32], onerow, srowb[0:1, 0:32],
                                          start=True, stop=True), inc=("sMM", 1))
        P.wait("scalar", "sMM")
        P.do("scalar", lambda E: E.copy(bcols, psb[2][:, 0:32]), inc=("sSC", 1))
        P.wait("tensor", "sSC")
        P.wait("vector", "sSC")

        P.ckpt()
        ktk = [arena.take(4 * 512, BF16), arena.take(4 * 512, BF16)]
        vtk = [arena.take(4 * 512, BF16), arena.take(4 * 512, BF16)]
        pb2 = arena.take(512, BF16)
        pb3 = arena.take(512, BF16)
        otall = arena.take(16 * OWN, BF16)
        rr = arena.take(256, F32)
        rwa2 = arena.take(128, F32)
        rwa3 = arena.take(128, F32)

        kv_free = [0, 0]
        for m in range(M):
            for hg in range(4):
                P.wait("tensor", "sVE")
                P.wait("tensor", "sSC")
                for b in range(2, 8):
                    P.do("tensor", lambda E, b=b: E.matmul(
                        psb[b][:, 0:512], onerow, zrow, start=True, stop=False),
                        inc=("sMM", 1) if b == 7 else None)
                nkc = m + 1
                for ikc in range(nkc):
                    kc = m - ikc
                    for hf in range(2):
                        it = ikc * 2 + hf
                        bi = it % 2
                        kt = ktk[bi]
                        vt = vtk[bi]
                        skv = "sKV0" if bi == 0 else "sKV1"
                        P.do("sync", lambda E, v=kv_free[bi]: E.wait_ge(
                            P.sem("sMM"), v))
                        P.do("sync", lambda E, hg=hg, kc=kc, hf=hf, kt=kt:
                             E.dma_start(
                                 out=r_sb(kt, 4),
                                 in_=r_in(ktb_buf[hg * 512:(hg + 1) * 512,
                                                  kc * 1024 + hf * 512:
                                                  kc * 1024 + (hf + 1) * 512], 4)),
                             inc=(skv, 16))
                        P.do("sync", lambda E, hg=hg, kc=kc, hf=hf, vt=vt:
                             E.dma_start(
                                 out=r_sb(vt, 4),
                                 in_=r_in(vb_buf[kc * 1024 + hf * 512:
                                                 kc * 1024 + (hf + 1) * 512,
                                                 hg * 512:(hg + 1) * 512], 4)),
                             inc=(skv, 16))
                        P.wait("tensor", skv)
                        last_it = (ikc == nkc - 1 and hf == 1)
                        for hl in range(4):
                            h = hg * 4 + hl
                            P.wait("tensor", "sSC")
                            P.wait("tensor", "sVE")
                            for kb in range(4):
                                for st_ in range(2):
                                    P.do("tensor", lambda E, hl=hl, kb=kb, st_=st_,
                                         kt=kt, m=m, h=h: E.matmul(
                                             psb[st_][:, kb * 128:(kb + 1) * 128],
                                             kt[st_ * 64:(st_ + 1) * 64,
                                                hl * 512 + kb * 128:
                                                hl * 512 + (kb + 1) * 128],
                                             qto[st_ * 64:(st_ + 1) * 64,
                                                 h * OWN + m * 128:
                                                 h * OWN + (m + 1) * 128],
                                             start=True, stop=True,
                                             tile_position=(st_ * 64, 0)),
                                         inc=("sMM", 1)
                                         if (kb == 3 and st_ == 1) else None)
                            P.wait("scalar", "sMM")
                            P.do("scalar", lambda E, h=h: E.activation(
                                pb2, psb[0][:, 0:512], AF.Exp,
                                bias=bcols[:, 2 * h:2 * h + 1], scale=SCALE_B))
                            P.do("scalar", lambda E, h=h: E.activation(
                                pb3, psb[1][:, 0:512], AF.Exp,
                                bias=bcols[:, 2 * h + 1:2 * h + 2], scale=SCALE_B),
                                inc=("sSC", 1))
                            if kc == m:
                                P.wait("vector", "sSC")
                                P.do("vector", lambda E, hf=hf: E.tensor_mul(
                                    pb2, pb2, dmsk[:, hf * 512:(hf + 1) * 512]))
                                P.do("vector", lambda E, hf=hf: E.tensor_mul(
                                    pb3, pb3, dmsk[:, hf * 512:(hf + 1) * 512]),
                                    inc=("sVE", 1))
                                P.wait("tensor", "sVE")
                            else:
                                P.wait("tensor", "sSC")
                            P.do("tensor", lambda E, hl=hl: E.matmul(
                                psb[4 + hl][0:1, 0:512], onesb, pb2,
                                start=False, stop=False))
                            P.do("tensor", lambda E, hl=hl: E.matmul(
                                psb[4 + hl][32:33, 0:512], onesb, pb3,
                                start=False, stop=False, tile_position=(0, 32)))
                            ob = psb[2 + hl // 2]
                            oc = (hl % 2) * 256
                            for kb in range(4):
                                P.do("tensor", lambda E, hl=hl, kb=kb, vt=vt, ob=ob,
                                     oc=oc: E.matmul(
                                         ob[:, oc:oc + 128],
                                         vt[:, kb * 512 + hl * 128:
                                            kb * 512 + hl * 128 + 128],
                                         pb2[:, kb * 128:(kb + 1) * 128],
                                         start=False, stop=False))
                                P.do("tensor", lambda E, hl=hl, kb=kb, vt=vt, ob=ob,
                                     oc=oc: E.matmul(
                                         ob[:, oc + 128:oc + 256],
                                         vt[:, kb * 512 + hl * 128:
                                            kb * 512 + hl * 128 + 128],
                                         pb3[:, kb * 128:(kb + 1) * 128],
                                         start=False, stop=False),
                                     inc=("sMM", 1) if kb == 3 else None)
                        kv_free[bi] = P.count("sMM")
                for b in range(2, 8):
                    P.do("tensor", lambda E, b=b: E.matmul(
                        psb[b][:, 0:512], onerow, zrow, start=False, stop=True),
                        inc=("sMM", 1) if b == 7 else None)
                for hl in range(4):
                    h = hg * 4 + hl
                    P.wait("vector", "sMM")
                    P.do("vector", lambda E, hl=hl: E.tensor_copy(
                        srow[0:1, 512:1024], psb[4 + hl][0:1, 0:512]))
                    P.dov(lambda E, hl=hl: E.tensor_copy(
                        srow2[0:1, 512:1024], psb[4 + hl][32:33, 0:512]))
                    P.do("vector", lambda E: E.tensor_add(
                        srow[0:1, 0:128], srow[0:1, 512:640], srow[0:1, 640:768]))
                    P.dov(lambda E: E.tensor_add(
                        srow[0:1, 128:256], srow[0:1, 768:896],
                        srow[0:1, 896:1024]))
                    P.dov(lambda E: E.tensor_add(
                        srow[0:1, 0:128], srow[0:1, 0:128], srow[0:1, 128:256]))
                    P.do("vector", lambda E: E.tensor_add(
                        srow2[0:1, 0:128], srow2[0:1, 512:640], srow2[0:1, 640:768]))
                    P.dov(lambda E: E.tensor_add(
                        srow2[0:1, 128:256], srow2[0:1, 768:896],
                        srow2[0:1, 896:1024]))
                    P.dov(lambda E: E.tensor_add(
                        srow[0:1, 128:256], srow2[0:1, 0:128],
                        srow2[0:1, 128:256]))
                    P.dov(lambda E: E.reciprocal(srow2[0:1, 0:256],
                                                 srow[0:1, 0:256]))
                    P.do("vector", lambda E, h=h: E.tensor_scalar(
                        srow2[0:1, 128:256], srow2[0:1, 128:256],
                        lams[0:1, h:h + 1], None, op0=ALU.mult), inc=("sVE", 1))
                    P.wait("scalar", "sVE")
                    P.do("scalar", lambda E: E.copy(srowb[0:1, 0:256],
                                                    srow2[0:1, 0:256]),
                         inc=("sSC", 1))
                    P.wait("tensor", "sSC")
                    P.do("tensor", lambda E: E.matmul(
                        psb[0][:, 0:256], onerow, srowb[0:1, 0:256],
                        start=True, stop=True), inc=("sMM", 1))
                    P.wait("vector", "sMM")
                    P.dov(lambda E: E.tensor_copy(rr, psb[0][:, 0:256]))
                    ob = psb[2 + hl // 2]
                    oc = (hl % 2) * 256
                    P.do("vector", lambda E, ob=ob, oc=oc: E.tensor_mul(
                        rwa2, ob[:, oc:oc + 128], rr[:, 0:128]))
                    P.do("vector", lambda E, ob=ob, oc=oc: E.tensor_mul(
                        rwa3, ob[:, oc + 128:oc + 256], rr[:, 128:256]),
                        inc=("sVE", 1))
                    P.wait("vector", "sVE")
                    P.do("vector", lambda E, h=h, m=m: E.tensor_sub(
                        otall[:, h * OWN + m * 128:h * OWN + (m + 1) * 128],
                        rwa2, rwa3), inc=("sVE", 1))
                    P.wait("tensor", "sVE")
        P.barrier()

        P.ckpt()
        wsb_ob = wsb_q
        xt2 = lnTq.bitcast(F32)
        load_w16(wsb_ob, wbf["wob"])
        for m in range(M):
            P.wait("sync", "sST")
            P.wait("sync", "sVE")
            P.do("sync", lambda E, m=m: E.dma_start(
                out=xt2[:, 0:D], in_=x2own_buf[m * 128:(m + 1) * 128, :]),
                inc=("sXT", 16))
            P.wait("vector", "sXT")
            P.wait("tensor", "sXT")
            for n4 in range(4):
                bank = psb[n4 % 2]
                P.wait("tensor", "sVE")
                for k16 in range(16):
                    P.do("tensor", lambda E, m=m, n4=n4, k16=k16, bank=bank:
                         E.matmul(bank[:, 0:512],
                                  otall[:, k16 * OWN + m * 128:
                                        k16 * OWN + m * 128 + 128],
                                  wt(wsb_ob, k16)[:, n4 * 512:(n4 + 1) * 512],
                                  start=(k16 == 0), stop=(k16 == 15)),
                         inc=("sMM", 1) if k16 == 15 else None)
                P.wait("vector", "sMM")
                P.do("vector", lambda E, n4=n4, bank=bank: E.tensor_add(
                    xt2[:, n4 * 512:(n4 + 1) * 512],
                    xt2[:, n4 * 512:(n4 + 1) * 512], bank[:, 0:512]),
                    inc=("sVE", 1))
            P.wait("sync", "sVE")
            P.do("sync", lambda E, m=m: E.dma_start(
                out=x3own_buf[m * 128:(m + 1) * 128, :], in_=xt2[:, 0:D]),
                inc=("sST", 16))
        P.barrier()
        if debug:
            P.wait("gpsimd", "sST")
            P.do("gpsimd", lambda E: E.dma_start(out=dbg["d_x3own"], in_=x3own_buf),
                 inc=("sGP", 16))

        P.ckpt()
        ln_pass(x3own_buf, ln4T_buf, OWN)
        ffn_pass(ln4T_buf, x3own_buf, x4own_buf, OWN)
        P.do("gpsimd", lambda E: E.dma_start(out=y, in_=x4own_buf), inc=("sGP", 16))
        P.wait("gpsimd", "sGP")
        P.barrier()

        @blk.sync
        def _(E):
            for f in P.ops["sync"]:
                f(E)

        @blk.scalar
        def _(E):
            for f in P.ops["scalar"]:
                f(E)

        @blk.vector
        def _(E):
            for f in P.ops["vector"]:
                f(E)

        @blk.tensor
        def _(E):
            for f in P.ops["tensor"]:
                f(E)

        @blk.gpsimd
        def _(E):
            for f in P.ops["gpsimd"]:
                f(E)

    return nc


def make_dmask(j):
    import ml_dtypes
    kb = np.arange(8)[:, None, None]
    p = np.arange(128)[None, :, None]
    w = np.arange(128)[None, None, :]
    m = (kb * 128 + p <= 128 * j + w).astype(np.float32)
    return np.ascontiguousarray(m.reshape(8 * 128, 128).astype(ml_dtypes.bfloat16))


def _prep_inputs(kw):
    x = np.ascontiguousarray(np.asarray(kw["L0_attn_a_hidden_states"],
                                        np.float32).reshape(-1, D))
    base = {
        "x0": x,
        "wqa": np.ascontiguousarray(np.asarray(kw["L0_attn_a_Wq"], np.float32)),
        "wka": np.ascontiguousarray(np.asarray(kw["L0_attn_a_Wk"], np.float32)),
        "wva": np.ascontiguousarray(np.asarray(kw["L0_attn_a_Wv"], np.float32)),
        "woa": np.ascontiguousarray(np.asarray(kw["L0_attn_a_Wo"], np.float32)),
        "wqb": np.ascontiguousarray(np.asarray(kw["L0_attn_b_Wq"], np.float32)),
        "wkb": np.ascontiguousarray(np.asarray(kw["L0_attn_b_Wk"], np.float32)),
        "wvb": np.ascontiguousarray(np.asarray(kw["L0_attn_b_Wv"], np.float32)),
        "wob": np.ascontiguousarray(np.asarray(kw["L0_attn_b_Wo"], np.float32)),
        "lam0": np.ascontiguousarray(np.asarray(kw["L0_attn_b_lambda_init"],
                                                np.float32).reshape(1, HEADS)),
        "wup": np.ascontiguousarray(np.asarray(kw["L0_ffn_up_weight"], np.float32)),
        "wdn": np.ascontiguousarray(np.asarray(kw["L0_ffn_down_weight"],
                                               np.float32)),
        "scup": np.ascontiguousarray(np.asarray(kw["L0_ffn_up_scale"], np.float32)
                                     .reshape(FFN_H // 128, 128).T),
        "scdn": np.ascontiguousarray(np.asarray(kw["L0_ffn_down_scale"],
                                                np.float32).reshape(1, D)),
        "scout": np.ascontiguousarray(np.asarray(kw["L0_ffn_output_scale"],
                                                 np.float32).reshape(1, 1)),
    }
    return x.shape[0], base


def kernel(**kw):
    S, base = _prep_inputs(kw)
    M = S // 128 // NC8
    nc = build_program(S, debug=False)
    in_maps = []
    for j in range(NC8):
        mdict = dict(base)
        mdict["core_slot"] = np.array([[j]], np.int32)
        mdict["dmask"] = make_dmask(j)
        mdict["rampi"] = np.broadcast_to(np.arange(CH, dtype=np.float32),
                                         (128, CH)).copy()
        in_maps.append(mdict)
    res = run_bass_kernel_spmd(nc, in_maps, core_ids=list(range(NC8)))
    out = np.empty((S, D), np.float32)
    for j in range(NC8):
        yj = res.results[j]["y"]
        for mm_ in range(M):
            T = NC8 * mm_ + j
            out[T * 128:(T + 1) * 128, :] = yj[mm_ * 128:(mm_ + 1) * 128, :]
    return out.reshape(1, S, D)


# version 02
# speedup vs baseline: 1.1115x; 1.1115x over previous
import contextlib
import math
import numpy as np

import concourse.bass as bass
import concourse.mybir as mybir
from concourse.bass_utils import run_bass_kernel_spmd

F32 = mybir.dt.float32
BF16 = mybir.dt.bfloat16
I32 = mybir.dt.int32
AF = mybir.ActivationFunctionType
ALU = mybir.AluOpType
AX = mybir.AxisListType.X

D = 2048
HEADS = 16
WIN = 256
FFN_H = 8192
NC8 = 8
CH = 512

INV_2PI = float(1.0 / (2.0 * math.pi))
TWO_PI_HI = 6.28125
TWO_PI_LO = float(2.0 * math.pi - 6.28125)
LN_THETA = float(-math.log(10000.0) / 64.0)
SCALE_A = float(1.0 / math.sqrt(128.0))
SCALE_B = 0.125


class Prog:
    def __init__(self):
        self.ops = {e: [] for e in ("sync", "scalar", "vector", "tensor", "gpsimd")}
        self.sems = {}
        self.frozen = False
        self.phase = 0
        self.max_phase = 99
        self.relay = None

    def ckpt(self):
        self.phase += 1
        if self.phase >= self.max_phase:
            self.frozen = True

    def add_sem(self, name, handle):
        self.sems[name] = [handle, 0]

    def sem(self, name):
        return self.sems[name][0]

    def count(self, name):
        return self.sems[name][1]

    def emit(self, eng, fn):
        if self.frozen:
            return
        self.ops[eng].append(fn)

    def wait(self, eng, name, val=None):
        h, c = self.sems[name]
        v = c if val is None else val
        if v > 0:
            self.emit(eng, lambda E, h=h, v=v: E.wait_ge(h, v))

    def do(self, eng, fn, inc=None):
        if self.frozen:
            return
        if inc is None:
            self.emit(eng, fn)
        else:
            name, n = inc
            rec = self.sems[name]
            rec[1] += n
            h = rec[0]
            self.emit(eng, lambda E, fn=fn, h=h, n=n: fn(E).then_inc(h, n))

    def dov(self, fn):
        self.do("vector", fn, inc=("sVE", 1))
        self.wait("vector", "sVE")

    def barrier(self):
        for eng in ("sync", "scalar", "vector", "tensor"):
            for name in self.sems:
                self.wait(eng, name)
        if self.relay is not None:
            self.relay()
            for name in ("sIN", "sK", "sGP", "sW", "sD0", "sD1", "sST",
                         "sXT", "sKV0", "sKV1"):
                self.wait("gpsimd", name)
        else:
            for name in self.sems:
                self.wait("gpsimd", name)


class Arena:
    def __init__(self, base_ap, nbytes):
        self.base = base_ap
        self.nbytes = nbytes
        self.cur = 0

    def reset(self):
        self.cur = 0

    def take(self, nfree, dtype):
        sz = mybir.dt.size(dtype)
        nb = nfree * sz
        self.cur = (self.cur + 63) // 64 * 64
        off = self.cur
        self.cur += nb
        assert self.cur <= self.nbytes, f"arena overflow: {self.cur} > {self.nbytes}"
        v = self.base[:, off // 4:(off + nb) // 4]
        if dtype == F32:
            return v
        return v.bitcast(dtype)


def chunks_of(ntok):
    out = []
    t = 0
    while t < ntok:
        n = min(4, (ntok - t) // 128)
        out.append((t, n))
        t += n * 128
    return out


def build_program(S, debug=False, max_phase=99):
    NCH = S // CH
    M = S // 128 // NC8
    OWN = M * 128
    nc = bass.Bass("TRN2", target_bir_lowering=False)

    x0 = nc.dram_tensor("x0", [S, D], F32, kind="ExternalInput").ap()
    win = {}
    for nm in ["wqa", "wka", "wva", "woa", "wqb", "wkb", "wvb", "wob"]:
        win[nm] = nc.dram_tensor(nm, [D, D], F32, kind="ExternalInput").ap()
    lam0 = nc.dram_tensor("lam0", [1, HEADS], F32, kind="ExternalInput").ap()
    wup = nc.dram_tensor("wup", [FFN_H, D], F32, kind="ExternalInput").ap()
    wdn = nc.dram_tensor("wdn", [D, FFN_H], F32, kind="ExternalInput").ap()
    scup = nc.dram_tensor("scup", [128, FFN_H // 128], F32,
                          kind="ExternalInput").ap()
    scdn = nc.dram_tensor("scdn", [1, D], F32, kind="ExternalInput").ap()
    scout = nc.dram_tensor("scout", [1, 1], F32, kind="ExternalInput").ap()
    core_slot = nc.dram_tensor("core_slot", [1, 1], I32, kind="ExternalInput").ap()
    dmask = nc.dram_tensor("dmask", [8 * 128, 128], BF16, kind="ExternalInput").ap()
    rampi = nc.dram_tensor("rampi", [128, CH], F32, kind="ExternalInput").ap()
    y = nc.dram_tensor("y", [OWN, D], F32, kind="ExternalOutput").ap()

    wbf = {}
    for nm in ["wqa", "wka", "wva", "woa", "wqb", "wkb", "wvb", "wob"]:
        wbf[nm] = nc.dram_tensor(nm + "_b", [D, D], BF16, kind="Internal").ap()
    upT = nc.dram_tensor("upT", [D, FFN_H], BF16, kind="Internal").ap()
    dnT = nc.dram_tensor("dnT", [FFN_H, D], BF16, kind="Internal").ap()
    ln1T_buf = nc.dram_tensor("ln1T_buf", [D, S], BF16, kind="Internal").ap()
    qa_buf = nc.dram_tensor("qa_buf", [D, S], BF16, kind="Internal").ap()
    ka_buf = nc.dram_tensor("ka_buf", [D, S], BF16, kind="Internal").ap()
    va_buf = nc.dram_tensor("va_buf", [S, D], BF16, kind="Internal").ap()
    x1_buf = nc.dram_tensor("x1_buf", [S, D], F32, kind="Internal").ap()
    ln2T_buf = nc.dram_tensor("ln2T_buf", [D, S], BF16, kind="Internal").ap()
    x2_buf = nc.dram_tensor("x2_buf", [S, D], F32, kind="Internal").ap()
    ktb_buf = nc.dram_tensor("ktb_buf", [D, S], BF16, kind="Internal").ap()
    vb_buf = nc.dram_tensor("vb_buf", [S, D], BF16, kind="Internal").ap()
    x2own_buf = nc.dram_tensor("x2own_buf", [OWN, D], F32, kind="Internal").ap()
    ln3oT_buf = nc.dram_tensor("ln3oT_buf", [D, OWN], BF16, kind="Internal").ap()
    x3own_buf = nc.dram_tensor("x3own_buf", [OWN, D], F32, kind="Internal").ap()
    ln4T_buf = nc.dram_tensor("ln4T_buf", [D, OWN], BF16, kind="Internal").ap()
    x4own_buf = nc.dram_tensor("x4own_buf", [OWN, D], F32, kind="Internal").ap()

    dbg = {}
    if debug:
        for nm, shp, dt in [
            ("d_ln1T", [D, S], BF16), ("d_qa", [D, S], BF16), ("d_ka", [D, S], BF16),
            ("d_va", [S, D], BF16), ("d_x1", [S, D], F32), ("d_x2", [S, D], F32),
            ("d_ktb", [D, S], BF16), ("d_vb", [S, D], BF16),
            ("d_upT", [D, FFN_H], BF16), ("d_dnT", [FFN_H, D], BF16),
            ("d_x2own", [OWN, D], F32), ("d_x3own", [OWN, D], F32),
        ]:
            dbg[nm] = nc.dram_tensor(nm, shp, dt, kind="ExternalOutput").ap()

    P = Prog()
    P.max_phase = max_phase

    with contextlib.ExitStack() as ctx:
        def sb(name, shape, dt):
            return ctx.enter_context(nc.sbuf_tensor(name, shape, dt))[:]

        ident = sb("ident", [128, 128], BF16)
        pmat = sb("pmat", [128, 128], BF16)
        onesb = sb("onesb", [128, 1], BF16)
        ones2 = sb("ones2", [128, 2], BF16)
        onerow = sb("onerow", [1, 128], BF16)
        zrow = sb("zrow", [1, 512], BF16)
        invf2 = sb("invf2", [128, 1], F32)
        dsrow = sb("dsrow", [128, D], F32)
        upsc = sb("upsc", [128, FFN_H // 128], F32)
        lams = sb("lams", [1, HEADS], F32)
        maskw = sb("maskw", [128, 2 * WIN], BF16)
        dmsk = sb("dmsk", [128, 8 * 128], BF16)
        core_sb = sb("core_sb", [1, 1], I32)
        ramp = sb("ramp", [128, CH], F32)
        stat = sb("stat", [128, 32], F32)
        srow = sb("srow", [1, 2048], F32)
        srow2 = sb("srow2", [1, 2048], F32)
        srowb = sb("srowb", [1, 2048], BF16)
        bcolq = sb("bcolq", [1, 2 * HEADS], F32)
        bcolk = sb("bcolk", [1, 2 * HEADS], F32)
        bcols = sb("bcols", [128, 2 * HEADS], F32)

        ARENA_B = 172 * 1024
        arena = Arena(sb("arena", [128, ARENA_B // 4], F32), ARENA_B)

        psb = [ctx.enter_context(nc.psum_tensor(f"ps{i}", [128, 512], F32))[:]
               for i in range(8)]
        pst = psb[7].bitcast(BF16)
        pst2 = psb[6].bitcast(BF16)

        for s_ in ["sIN", "sSC", "sVE", "sMM", "sGP", "sK", "sGC",
            "sW", "sD0", "sD1", "sST", "sXT", "sKV0", "sKV1"]:
            P.add_sem(s_, ctx.enter_context(nc.semaphore(s_)))

        blk = ctx.enter_context(nc.Block())

        def r_in(dram2d, a):
            return dram2d.rearrange("(a p) c -> p a c", p=128)

        def r_sb(sb2d, a):
            return sb2d.rearrange("p (a c) -> p a c", a=a)

        for nm in ["wqa", "wka", "wva", "woa", "wqb", "wkb", "wvb", "wob"]:
            P.do("gpsimd", lambda E, s=win[nm], d=wbf[nm]: E.dma_start(
                out=d, in_=s, max_dma_last_dim=8192), inc=("sGP", 16))
        wcast_done = P.count("sGP")

        P.do("gpsimd", lambda E: E.memset(ident, 0.0), inc=("sGC", 1))
        P.wait("gpsimd", "sGC")
        P.do("gpsimd", lambda E: E.affine_select(
            out=ident, in_=ident, compare_op=ALU.not_equal, fill=1.0, base=0,
            pattern=[[-1, 128]], channel_multiplier=1))
        P.do("gpsimd", lambda E: E.memset(pmat, 0.0), inc=("sGC", 1))
        P.wait("gpsimd", "sGC")
        P.do("gpsimd", lambda E: E.affine_select(
            out=pmat, in_=pmat, compare_op=ALU.not_equal, fill=-1.0, base=-64,
            pattern=[[-1, 128]], channel_multiplier=1), inc=("sGC", 1))
        P.wait("gpsimd", "sGC")
        P.do("gpsimd", lambda E: E.affine_select(
            out=pmat, in_=pmat, compare_op=ALU.not_equal, fill=1.0, base=64,
            pattern=[[-1, 128]], channel_multiplier=1), inc=("sGC", 1))
        P.wait("gpsimd", "sGC")
        P.do("gpsimd", lambda E: E.memset(onesb, 1.0))
        P.do("gpsimd", lambda E: E.memset(onerow, 1.0))
        P.do("gpsimd", lambda E: E.memset(zrow, 0.0))
        P.do("gpsimd", lambda E: E.memset(ones2, 0.0), inc=("sGC", 1))
        P.wait("gpsimd", "sGC")
        P.do("gpsimd", lambda E: E.memset(ones2[0:64, 0:1], 1.0))
        P.do("gpsimd", lambda E: E.memset(ones2[64:128, 1:2], 1.0))
        P.do("gpsimd", lambda E: E.memset(maskw, 1.0), inc=("sGC", 1))
        P.wait("gpsimd", "sGC")
        P.do("gpsimd", lambda E: E.affine_select(
            out=maskw[:, 0:WIN], in_=maskw[:, 0:WIN], compare_op=ALU.is_ge,
            fill=0.0, base=0, pattern=[[1, WIN]], channel_multiplier=-1))
        P.do("gpsimd", lambda E: E.affine_select(
            out=maskw[:, WIN:2 * WIN], in_=maskw[:, WIN:2 * WIN],
            compare_op=ALU.is_ge, fill=0.0, base=-128, pattern=[[1, WIN]],
            channel_multiplier=-1), inc=("sGC", 1))
        P.do("gpsimd", lambda E: E.iota(stat[:, 0:1], pattern=[[0, 1]], base=0,
                                        channel_multiplier=1,
                                        allow_small_or_imprecise_dtypes=True),
             inc=("sGC", 1))
        P.wait("vector", "sGC")
        P.dov(lambda E: E.tensor_single_scalar(invf2, stat[:, 0:1], 63.5,
                                               op=ALU.is_gt))
        P.dov(lambda E: E.tensor_scalar(invf2, invf2, -64.0, None, op0=ALU.mult))
        P.do("vector", lambda E: E.tensor_add(invf2, invf2, stat[:, 0:1]),
             inc=("sVE", 1))
        P.wait("scalar", "sVE")
        P.do("scalar", lambda E: E.activation(invf2, invf2, AF.Exp, scale=LN_THETA),
             inc=("sSC", 1))
        P.do("sync", lambda E: E.dma_start(out=r_sb(dmsk, 8),
                                           in_=r_in(dmask, 8)), inc=("sIN", 16))
        P.do("sync", lambda E: E.dma_start(out=core_sb, in_=core_slot), inc=("sIN", 16))

        def _relay():
            P.wait("sync", "sK")
            P.do("sync", lambda E: E.dma_start(out=core_sb, in_=core_slot),
                 inc=("sK", 16))
        P.relay = _relay
        P.do("sync", lambda E: E.dma_start(out=upsc, in_=scup), inc=("sIN", 16))
        P.do("sync", lambda E: E.dma_start(out=srow[0:1, 0:D], in_=scdn),
             inc=("sIN", 16))
        P.do("sync", lambda E: E.dma_start(out=srow2[0:1, 0:1], in_=scout),
             inc=("sIN", 16))
        P.do("sync", lambda E: E.dma_start(out=lams, in_=lam0), inc=("sIN", 16))
        P.do("sync", lambda E: E.dma_start(out=ramp, in_=rampi), inc=("sIN", 16))
        sm_in = P.count("sIN")
        sm_in2 = sm_in
        P.wait("scalar", "sIN", sm_in)
        P.do("scalar", lambda E: E.activation(lams, lams, AF.Sigmoid), inc=("sSC", 1))
        P.wait("vector", "sIN", sm_in)
        P.do("vector", lambda E: E.tensor_scalar(srow[0:1, 0:D], srow[0:1, 0:D],
                                                 srow2[0:1, 0:1], None, op0=ALU.mult),
             inc=("sVE", 1))
        P.wait("scalar", "sVE")
        P.do("scalar", lambda E: E.copy(srowb[0:1, 0:D], srow[0:1, 0:D]),
             inc=("sSC", 1))
        P.wait("tensor", "sSC")
        P.wait("tensor", "sGC")
        for i in range(4):
            P.do("tensor", lambda E, i=i: E.matmul(
                psb[5][:, 0:512], onerow, srowb[0:1, i * 512:(i + 1) * 512],
                start=True, stop=True), inc=("sMM", 1))
            P.wait("vector", "sMM")
            P.do("vector", lambda E, i=i: E.tensor_copy(
                dsrow[:, i * 512:(i + 1) * 512], psb[5][:, 0:512]), inc=("sVE", 1))
            P.wait("tensor", "sVE")

        arena.reset()
        wst32 = arena.take(2048, F32)
        wstbf = arena.take(2048, BF16)
        upTs = arena.take(16 * 1024, BF16)
        dnTs = arena.take(8 * 2048, BF16)

        for sup in range(FFN_H // 1024):
            for strip in range(8):
                h0 = sup * 1024 + strip * 128
                P.wait("sync", "sSC")
                P.do("sync", lambda E, h0=h0: E.dma_start(
                    out=wst32, in_=wup[h0:h0 + 128, :]), inc=("sW", 16))
                P.wait("scalar", "sW")
                P.wait("scalar", "sMM")
                P.do("scalar", lambda E: E.sign(wstbf, wst32), inc=("sSC", 1))
                P.wait("tensor", "sSC")
                for g in range(2):
                    bank = pst if g == 0 else pst2
                    P.wait("tensor", "sVE")
                    for t8 in range(8):
                        d16 = g * 8 + t8
                        P.do("tensor", lambda E, d16=d16, t8=t8, bank=bank:
                             E.transpose(bank[:, t8 * 128:(t8 + 1) * 128],
                                         wstbf[:, d16 * 128:(d16 + 1) * 128], ident),
                             inc=("sMM", 1) if t8 == 7 else None)
                    P.wait("vector", "sMM")
                    for t8 in range(8):
                        d16 = g * 8 + t8
                        P.do("vector", lambda E, d16=d16, t8=t8, strip=strip,
                             bank=bank: E.tensor_copy(
                                 upTs[:, d16 * 1024 + strip * 128:
                                      d16 * 1024 + strip * 128 + 128],
                                 bank[:, t8 * 128:(t8 + 1) * 128]),
                             inc=("sVE", 1) if t8 == 7 else None)
            P.wait("sync", "sVE")
            P.wait("sync", "sST")
            P.do("sync", lambda E, sup=sup: E.dma_start(
                out=r_in(upT[:, sup * 1024:(sup + 1) * 1024], 16),
                in_=r_sb(upTs, 16)), inc=("sST", 16))
            P.wait("vector", "sST")
        up_staged = P.count("sST")

        for hw in range(FFN_H // 1024):
            for dstrip in range(16):
                P.wait("sync", "sSC")
                P.do("sync", lambda E, hw=hw, dstrip=dstrip: E.dma_start(
                    out=wst32[:, 0:1024],
                    in_=wdn[dstrip * 128:(dstrip + 1) * 128,
                            hw * 1024:(hw + 1) * 1024]), inc=("sW", 16))
                P.wait("scalar", "sW")
                P.wait("scalar", "sMM")
                P.do("scalar", lambda E: E.sign(wstbf[:, 0:1024], wst32[:, 0:1024]),
                     inc=("sSC", 1))
                P.wait("tensor", "sSC")
                P.wait("tensor", "sVE")
                for h8 in range(8):
                    P.do("tensor", lambda E, h8=h8: E.transpose(
                        pst[:, h8 * 128:(h8 + 1) * 128],
                        wstbf[:, h8 * 128:(h8 + 1) * 128], ident),
                        inc=("sMM", 1) if h8 == 7 else None)
                P.wait("vector", "sMM")
                for h8 in range(8):
                    P.do("vector", lambda E, h8=h8, dstrip=dstrip: E.tensor_copy(
                        dnTs[:, h8 * 2048 + dstrip * 128:
                             h8 * 2048 + dstrip * 128 + 128],
                        pst[:, h8 * 128:(h8 + 1) * 128]),
                        inc=("sVE", 1) if h8 == 7 else None)
            P.wait("sync", "sVE")
            P.wait("sync", "sST")
            P.do("sync", lambda E, hw=hw: E.dma_start(
                out=r_in(dnT[hw * 1024:(hw + 1) * 1024, :], 8),
                in_=r_sb(dnTs, 8)), inc=("sST", 16))
            P.wait("vector", "sST")
        if debug:
            for s_ in ("sIN", "sST", "sW", "sD0", "sD1"):
                P.wait("gpsimd", s_)
            P.do("gpsimd", lambda E: E.dma_start(out=dbg["d_upT"], in_=upT),
                 inc=("sGP", 16))
            P.do("gpsimd", lambda E: E.dma_start(out=dbg["d_dnT"], in_=dnT),
                 inc=("sGP", 16))
        P.barrier()
        P.ckpt()

        def emit_ln_chunk(xc, lnc, sq, nt):
            for t in range(nt):
                P.do("vector", lambda E, t=t: E.reduce_sum(
                    stat[:, t:t + 1], xc[:, t * D:(t + 1) * D], axis=AX),
                    inc=("sVE", 1) if t == nt - 1 else None)
            for t in range(nt):
                P.do("scalar", lambda E, t=t: E.activation(
                    sq[:, t * D:(t + 1) * D], xc[:, t * D:(t + 1) * D], AF.Square),
                    inc=("sSC", 1) if t == nt - 1 else None)
            P.wait("vector", "sSC")
            for t in range(nt):
                P.do("vector", lambda E, t=t: E.reduce_sum(
                    stat[:, 4 + t:5 + t], sq[:, t * D:(t + 1) * D], axis=AX),
                    inc=("sVE", 1) if t == nt - 1 else None)
            P.wait("vector", "sVE")
            P.dov(lambda E, nt=nt: E.tensor_scalar(
                stat[:, 8:8 + nt], stat[:, 0:nt], 1.0 / D, None, op0=ALU.mult))
            P.dov(lambda E, nt=nt: E.tensor_mul(
                stat[:, 12:12 + nt], stat[:, 8:8 + nt], stat[:, 8:8 + nt]))
            P.dov(lambda E, nt=nt: E.tensor_scalar(
                stat[:, 16:16 + nt], stat[:, 4:4 + nt], 1.0 / D, None, op0=ALU.mult))
            P.dov(lambda E, nt=nt: E.tensor_sub(
                stat[:, 16:16 + nt], stat[:, 16:16 + nt], stat[:, 12:12 + nt]))
            P.do("vector", lambda E, nt=nt: E.tensor_scalar(
                stat[:, 16:16 + nt], stat[:, 16:16 + nt], 1.0, 1e-5, op0=ALU.mult,
                op1=ALU.add), inc=("sVE", 1))
            P.wait("scalar", "sVE")
            P.do("scalar", lambda E, nt=nt: E.activation(
                stat[:, 20:20 + nt], stat[:, 16:16 + nt], AF.Sqrt), inc=("sSC", 1))
            P.wait("vector", "sSC")
            P.dov(lambda E, nt=nt: E.reciprocal(
                stat[:, 24:24 + nt], stat[:, 20:20 + nt]))
            P.dov(lambda E, nt=nt: E.tensor_mul(
                stat[:, 28:28 + nt], stat[:, 8:8 + nt], stat[:, 24:24 + nt]))
            P.dov(lambda E, nt=nt: E.tensor_scalar(
                stat[:, 28:28 + nt], stat[:, 28:28 + nt], -1.0, None, op0=ALU.mult))
            for t in range(nt):
                P.do("vector", lambda E, t=t: E.tensor_scalar(
                    lnc[:, t * D:(t + 1) * D], xc[:, t * D:(t + 1) * D],
                    stat[:, 24 + t:25 + t], stat[:, 28 + t:29 + t],
                    op0=ALU.mult, op1=ALU.add), inc=("sVE", 1))

        def emit_tr_chunk(src, dst, nt):
            C = nt * 128
            P.wait("tensor", "sSC")
            P.wait("tensor", "sVE")
            for d16 in range(16):
                bank = pst if d16 % 2 == 0 else pst2
                P.wait("tensor", "sVE")
                for t in range(nt):
                    P.do("tensor", lambda E, d16=d16, t=t, bank=bank: E.transpose(
                        bank[:, t * 128:(t + 1) * 128],
                        src[:, t * D + d16 * 128:t * D + d16 * 128 + 128], ident),
                        inc=("sMM", 1) if t == nt - 1 else None)
                P.wait("vector", "sMM")
                P.do("vector", lambda E, d16=d16, C=C, bank=bank: E.tensor_copy(
                    dst[:, d16 * CH:d16 * CH + C], bank[:, 0:C]), inc=("sVE", 1))
            P.wait("tensor", "sVE")

        def load_w16(wdst, wsrc):
            P.wait("sync", "sGP")
            P.do("sync", lambda E, wdst=wdst, wsrc=wsrc: E.dma_start(
                out=r_sb(wdst, 16), in_=r_in(wsrc, 16)), inc=("sW", 16))
            P.wait("tensor", "sW")

        def wt(wsb_, k16):
            return wsb_[:, k16 * D:(k16 + 1) * D]

        def ln_pass(x_src, lnT_dst, ntok):
            arena.reset()
            xc = [arena.take(4 * D, F32), arena.take(4 * D, F32)]
            lnc = arena.take(4 * D, BF16)
            lnT = arena.take(16 * CH, BF16)
            sq = arena.take(4 * D, F32)
            free_at = [0, 0]
            for i, (t0, nt) in enumerate(chunks_of(ntok)):
                buf = xc[i % 2]
                sd = "sD0" if i % 2 == 0 else "sD1"
                P.wait("sync", "sVE", free_at[i % 2])
                P.do("sync", lambda E, t0=t0, nt=nt, buf=buf: E.dma_start(
                    out=r_sb(buf, 4)[:, 0:nt, :],
                    in_=r_in(x_src[t0:t0 + nt * 128, :], nt)), inc=(sd, 16))
                P.wait("vector", sd)
                P.wait("scalar", sd)
                emit_ln_chunk(buf, lnc, sq, nt)
                free_at[i % 2] = P.count("sVE")
                emit_tr_chunk(lnc, lnT, nt)
                P.wait("sync", "sVE")
                P.wait("sync", "sST")
                P.do("sync", lambda E, t0=t0, nt=nt: E.dma_start(
                    out=r_in(lnT_dst[:, t0:t0 + nt * 128], 16),
                    in_=r_sb(lnT, 16)[:, :, 0:nt * 128]), inc=("sST", 16))
                P.wait("vector", "sST")
            P.barrier()

        def rope_tables(t0, ntk, rc, rs, rw, iw, rw2, rw3):
            n = ntk
            P.wait("vector", "sIN", sm_in2)
            P.dov(lambda E, t0=t0: E.tensor_scalar(rw[:, 0:n], ramp[:, 0:n],
                                                   1.0, float(t0), op0=ALU.mult,
                                                   op1=ALU.add))
            P.dov(lambda E: E.tensor_scalar(rw2[:, 0:n], rw[:, 0:n], invf2, None,
                                            op0=ALU.mult))
            P.dov(lambda E: E.tensor_scalar(rw[:, 0:n], rw2[:, 0:n], INV_2PI, 0.5,
                                            op0=ALU.mult, op1=ALU.add))
            P.dov(lambda E: E.tensor_copy(iw[:, 0:n], rw[:, 0:n]))
            P.dov(lambda E: E.tensor_copy(rw[:, 0:n], iw[:, 0:n]))
            P.dov(lambda E: E.tensor_scalar(rw3[:, 0:n], rw[:, 0:n], -TWO_PI_HI,
                                            None, op0=ALU.mult))
            P.dov(lambda E: E.tensor_add(rs[:, 0:n], rw2[:, 0:n], rw3[:, 0:n]))
            P.dov(lambda E: E.tensor_scalar(rw3[:, 0:n], rw[:, 0:n], -TWO_PI_LO,
                                            None, op0=ALU.mult))
            P.dov(lambda E: E.tensor_add(rs[:, 0:n], rs[:, 0:n], rw3[:, 0:n]))
            P.dov(lambda E: E.tensor_single_scalar(rw3[:, 0:n], rs[:, 0:n],
                                                   float(math.pi), op=ALU.is_gt))
            P.dov(lambda E: E.scalar_tensor_tensor(rs[:, 0:n], rw3[:, 0:n],
                                                   float(-2.0 * math.pi),
                                                   rs[:, 0:n], op0=ALU.mult,
                                                   op1=ALU.add))
            P.dov(lambda E: E.tensor_single_scalar(rw3[:, 0:n], rs[:, 0:n],
                                                   float(-math.pi), op=ALU.is_lt))
            P.do("vector", lambda E: E.scalar_tensor_tensor(
                rs[:, 0:n], rw3[:, 0:n], float(2.0 * math.pi), rs[:, 0:n],
                op0=ALU.mult, op1=ALU.add), inc=("sVE", 1))
            P.wait("scalar", "sVE")
            P.do("scalar", lambda E: E.activation(rs[:, 0:n], rs[:, 0:n], AF.Sin),
                 inc=("sSC", 1))
            P.dov(lambda E: E.tensor_scalar(rw[:, 0:n], rw2[:, 0:n], INV_2PI, 0.75,
                                            op0=ALU.mult, op1=ALU.add))
            P.dov(lambda E: E.tensor_copy(iw[:, 0:n], rw[:, 0:n]))
            P.dov(lambda E: E.tensor_copy(rw[:, 0:n], iw[:, 0:n]))
            P.dov(lambda E: E.tensor_scalar(rw3[:, 0:n], rw[:, 0:n], -TWO_PI_HI,
                                            None, op0=ALU.mult))
            P.dov(lambda E: E.tensor_add(rc[:, 0:n], rw2[:, 0:n], rw3[:, 0:n]))
            P.dov(lambda E: E.tensor_scalar(rw3[:, 0:n], rw[:, 0:n], -TWO_PI_LO,
                                            None, op0=ALU.mult))
            P.dov(lambda E: E.tensor_add(rc[:, 0:n], rc[:, 0:n], rw3[:, 0:n]))
            P.dov(lambda E: E.tensor_scalar(rc[:, 0:n], rc[:, 0:n], 1.0,
                                            float(math.pi / 2.0),
                                            op0=ALU.mult, op1=ALU.add))
            P.dov(lambda E: E.tensor_single_scalar(rw3[:, 0:n], rc[:, 0:n],
                                                   float(math.pi), op=ALU.is_gt))
            P.dov(lambda E: E.scalar_tensor_tensor(rc[:, 0:n], rw3[:, 0:n],
                                                   float(-2.0 * math.pi),
                                                   rc[:, 0:n], op0=ALU.mult,
                                                   op1=ALU.add))
            P.dov(lambda E: E.tensor_single_scalar(rw3[:, 0:n], rc[:, 0:n],
                                                   float(-math.pi), op=ALU.is_lt))
            P.do("vector", lambda E: E.scalar_tensor_tensor(
                rc[:, 0:n], rw3[:, 0:n], float(2.0 * math.pi), rc[:, 0:n],
                op0=ALU.mult, op1=ALU.add), inc=("sVE", 1))
            P.wait("scalar", "sVE")
            P.do("scalar", lambda E: E.activation(rc[:, 0:n], rc[:, 0:n], AF.Sin),
                 inc=("sSC", 1))
            P.wait("vector", "sSC")

        def projT_pass(lnT_src, wsrc, dstbuf, rope, ntok):
            arena.reset()
            wsb_ = arena.take(16 * D, BF16)
            lnT = [arena.take(16 * CH, BF16), arena.take(16 * CH, BF16)]
            outT = arena.take(16 * CH, BF16)
            rc = arena.take(CH, F32)
            rs = arena.take(CH, F32)
            rw = arena.take(CH, F32)
            rw2 = arena.take(CH, F32)
            rw3 = arena.take(CH, F32)
            iw = arena.take(CH, I32)
            qnb = arena.take(CH, BF16)
            load_w16(wsb_, wsrc)
            free_at = [0, 0]
            for i, (t0, nt) in enumerate(chunks_of(ntok)):
                ntk = nt * 128
                lt = lnT[i % 2]
                sd = "sD0" if i % 2 == 0 else "sD1"
                P.wait("sync", "sMM", free_at[i % 2])
                P.do("sync", lambda E, t0=t0, ntk=ntk, lt=lt, sd=sd: E.dma_start(
                    out=r_sb(lt, 16)[:, :, 0:ntk],
                    in_=r_in(lnT_src[:, t0:t0 + ntk], 16)), inc=(sd, 16))
                if rope:
                    rope_tables(t0, ntk, rc, rs, rw, iw, rw2, rw3)
                P.wait("tensor", sd)
                for h16 in range(16):
                    bank = psb[h16 % 2]
                    P.wait("tensor", "sVE")
                    for k16 in range(16):
                        P.do("tensor", lambda E, h16=h16, k16=k16, bank=bank, lt=lt,
                             ntk=ntk: E.matmul(
                                 bank[:, 0:ntk],
                                 wt(wsb_, k16)[:, h16 * 128:(h16 + 1) * 128],
                                 lt[:, k16 * CH:k16 * CH + ntk],
                                 start=(k16 == 0), stop=(k16 == 15)),
                             inc=("sMM", 1) if k16 == 15 else None)
                    P.wait("vector", "sMM")
                    if rope:
                        dsl = outT[:, h16 * CH:h16 * CH + ntk]
                        rb = psb[2 + h16 % 2]
                        P.wait("scalar", "sMM")
                        P.do("scalar", lambda E, bank=bank, ntk=ntk: E.copy(
                            qnb[:, 0:ntk], bank[:, 0:ntk]), inc=("sSC", 1))
                        P.wait("tensor", "sSC")
                        P.wait("tensor", "sVE")
                        P.do("tensor", lambda E, rb=rb, ntk=ntk: E.matmul(
                            rb[:, 0:ntk], pmat, qnb[:, 0:ntk], start=True,
                            stop=True), inc=("sMM", 1))
                        P.wait("vector", "sMM")
                        P.do("vector", lambda E, ntk=ntk: E.tensor_mul(
                            rw[:, 0:ntk], qnb[:, 0:ntk], rc[:, 0:ntk]))
                        P.do("vector", lambda E, rb=rb, ntk=ntk: E.tensor_mul(
                            rw2[:, 0:ntk], rb[:, 0:ntk], rs[:, 0:ntk]),
                            inc=("sVE", 1))
                        P.wait("vector", "sVE")
                        P.do("vector", lambda E, dsl=dsl, ntk=ntk: E.tensor_add(
                            dsl, rw[:, 0:ntk], rw2[:, 0:ntk]), inc=("sVE", 1))
                    else:
                        P.do("vector", lambda E, h16=h16, bank=bank, ntk=ntk:
                             E.tensor_copy(outT[:, h16 * CH:h16 * CH + ntk],
                                           bank[:, 0:ntk]), inc=("sVE", 1))
                free_at[i % 2] = P.count("sMM")
                P.wait("sync", "sVE")
                P.wait("sync", "sST")
                P.do("sync", lambda E, t0=t0, ntk=ntk: E.dma_start(
                    out=r_in(dstbuf[:, t0:t0 + ntk], 16),
                    in_=r_sb(outT, 16)[:, :, 0:ntk]), inc=("sST", 16))
                P.wait("vector", "sST")
            P.barrier()

        def projN_pass(lnT_src, wsrc, dstbuf, ntok):
            arena.reset()
            wsb_ = arena.take(16 * D, BF16)
            lnT = [arena.take(16 * CH, BF16), arena.take(16 * CH, BF16)]
            outN = arena.take(4 * D, BF16)
            load_w16(wsb_, wsrc)
            free_at = [0, 0]
            for i, (t0, nt) in enumerate(chunks_of(ntok)):
                ntk = nt * 128
                lt = lnT[i % 2]
                sd = "sD0" if i % 2 == 0 else "sD1"
                P.wait("sync", "sMM", free_at[i % 2])
                P.do("sync", lambda E, t0=t0, ntk=ntk, lt=lt, sd=sd: E.dma_start(
                    out=r_sb(lt, 16)[:, :, 0:ntk],
                    in_=r_in(lnT_src[:, t0:t0 + ntk], 16)), inc=(sd, 16))
                P.wait("tensor", sd)
                for t in range(nt):
                    for n4 in range(4):
                        bank = psb[(t * 4 + n4) % 2]
                        P.wait("tensor", "sVE")
                        for k16 in range(16):
                            P.do("tensor", lambda E, t=t, n4=n4, k16=k16, bank=bank,
                                 lt=lt: E.matmul(
                                     bank[:, 0:512],
                                     lt[:, k16 * CH + t * 128:k16 * CH + t * 128 + 128],
                                     wt(wsb_, k16)[:, n4 * 512:(n4 + 1) * 512],
                                     start=(k16 == 0), stop=(k16 == 15)),
                                 inc=("sMM", 1) if k16 == 15 else None)
                        P.wait("vector", "sMM")
                        P.do("vector", lambda E, t=t, n4=n4, bank=bank: E.tensor_copy(
                            outN[:, t * D + n4 * 512:t * D + (n4 + 1) * 512],
                            bank[:, 0:512]), inc=("sVE", 1))
                free_at[i % 2] = P.count("sMM")
                P.wait("sync", "sVE")
                P.wait("sync", "sST")
                P.do("sync", lambda E, t0=t0, nt=nt: E.dma_start(
                    out=r_in(dstbuf[t0:t0 + nt * 128, :], nt),
                    in_=r_sb(outN, 4)[:, 0:nt, :]), inc=("sST", 16))
                P.wait("vector", "sST")
            P.barrier()

        ln_pass(x0, ln1T_buf, S)
        if debug:
            P.wait("gpsimd", "sST")
            P.do("gpsimd", lambda E: E.dma_start(out=dbg["d_ln1T"], in_=ln1T_buf),
                 inc=("sGP", 16))
        P.ckpt()
        projT_pass(ln1T_buf, wbf["wqa"], qa_buf, True, S)
        P.ckpt()
        projT_pass(ln1T_buf, wbf["wka"], ka_buf, True, S)
        P.ckpt()
        projN_pass(ln1T_buf, wbf["wva"], va_buf, S)
        if debug:
            P.wait("gpsimd", "sST")
            P.do("gpsimd", lambda E: E.dma_start(out=dbg["d_qa"], in_=qa_buf),
                 inc=("sGP", 16))
            P.do("gpsimd", lambda E: E.dma_start(out=dbg["d_ka"], in_=ka_buf),
                 inc=("sGP", 16))
            P.do("gpsimd", lambda E: E.dma_start(out=dbg["d_va"], in_=va_buf),
                 inc=("sGP", 16))

        P.ckpt()
        arena.reset()
        wsb_o = arena.take(16 * D, BF16)
        qtc = arena.take(16 * CH, BF16)
        KW = WIN + CH
        ktc = arena.take(16 * KW, BF16)
        vc = arena.take(6 * D, BF16)
        otc = arena.take(16 * CH, BF16)
        xtile = arena.take(D, F32)
        pb = arena.take(4 * WIN, BF16)
        sqa = arena.take(CH + KW, BF16)
        rwa = arena.take(WIN, F32)

        load_w16(wsb_o, wbf["woa"])

        def ktt(h16, c0, cn):
            return ktc[:, h16 * KW + c0:h16 * KW + c0 + cn]

        import os as _os
        _p4n = int(_os.environ.get("P4N", "99"))
        _p4h = int(_os.environ.get("P4H", "99"))
        _p4w = int(_os.environ.get("P4W", "99"))
        for ch in range(min(NCH, _p4n)):
            P.wait("sync", "sMM")
            P.do("sync", lambda E, ch=ch: E.dma_start(
                out=r_sb(qtc, 16), in_=r_in(qa_buf[:, ch * CH:(ch + 1) * CH], 16)),
                inc=("sD0", 16))
            lo = max(ch * CH - WIN, 0)
            halo = ch * CH - lo
            P.do("sync", lambda E, ch=ch, lo=lo, halo=halo: E.dma_start(
                out=r_sb(ktc, 16)[:, :, WIN - halo:KW],
                in_=r_in(ka_buf[:, lo:(ch + 1) * CH], 16)), inc=("sD0", 16))
            a0 = (WIN - halo) // 128
            P.do("sync", lambda E, ch=ch, lo=lo, a0=a0: E.dma_start(
                out=r_sb(vc, 6)[:, a0:6, :],
                in_=r_in(va_buf[lo:(ch + 1) * CH, :], 6 - a0)), inc=("sD0", 16))
            if ch == 0:
                P.do("gpsimd", lambda E: E.memset(r_sb(ktc, 16)[:, :, 0:WIN], 0.0),
                     inc=("sGC", 1))
                P.do("gpsimd", lambda E: E.memset(vc[:, 0:2 * D], 0.0),
                     inc=("sGC", 1))
                P.wait("tensor", "sGC")
                P.wait("vector", "sGC")
            P.wait("tensor", "sD0")
            P.wait("vector", "sD0")

            for h16 in range(min(HEADS, _p4h)):
                P.do("vector", lambda E, h16=h16: E.tensor_mul(
                    sqa[:, 0:CH], qtc[:, h16 * CH:(h16 + 1) * CH],
                    qtc[:, h16 * CH:(h16 + 1) * CH]))
                P.do("vector", lambda E, h16=h16: E.tensor_mul(
                    sqa[:, CH:CH + KW], ktt(h16, 0, KW), ktt(h16, 0, KW)),
                    inc=("sVE", 1))
                P.wait("tensor", "sVE")
                P.do("tensor", lambda E: E.matmul(psb[5][0:1, 0:CH], onesb,
                                                  sqa[:, 0:CH], start=True, stop=True))
                P.do("tensor", lambda E: E.matmul(psb[4][0:1, 0:512], onesb,
                                                  sqa[:, CH:CH + 512],
                                                  start=True, stop=True))
                P.do("tensor", lambda E: E.matmul(psb[3][0:1, 0:KW - 512], onesb,
                                                  sqa[:, CH + 512:CH + KW],
                                                  start=True, stop=True),
                     inc=("sMM", 1))
                P.wait("vector", "sMM")
                P.do("vector", lambda E: E.reduce_max(srow[0:1, 0:1],
                                                      psb[5][0:1, 0:CH], axis=AX))
                P.do("vector", lambda E: E.reduce_max(srow[0:1, 1:2],
                                                      psb[4][0:1, 0:512], axis=AX))
                P.dov(lambda E: E.reduce_max(srow[0:1, 3:4],
                                             psb[3][0:1, 0:KW - 512], axis=AX))
                P.dov(lambda E: E.tensor_max(srow[0:1, 1:2], srow[0:1, 1:2],
                                             srow[0:1, 3:4]))
                P.dov(lambda E: E.tensor_add(srow[0:1, 0:1], srow[0:1, 0:1],
                                             srow[0:1, 1:2]))
                P.do("vector", lambda E: E.tensor_scalar(
                    srow[0:1, 2:3], srow[0:1, 0:1], -0.5 * SCALE_A, None,
                    op0=ALU.mult), inc=("sVE", 1))
                P.wait("scalar", "sVE")
                P.do("scalar", lambda E: E.copy(srowb[0:1, 0:1], srow[0:1, 2:3]),
                     inc=("sSC", 1))
                P.wait("tensor", "sSC")
                P.do("tensor", lambda E: E.matmul(psb[5][:, 0:1], onerow,
                                                  srowb[0:1, 0:1], start=True,
                                                  stop=True), inc=("sMM", 1))
                P.wait("scalar", "sMM")
                P.do("scalar", lambda E: E.copy(stat[:, 8:9], psb[5][:, 0:1]),
                     inc=("sSC", 1))

                for w2 in range(2 if _p4w >= 2 else 0):
                    q0 = w2 * WIN
                    P.wait("tensor", "sSC")
                    P.wait("tensor", "sVE")
                    for kb in range(4):
                        bank = psb[2] if kb < 2 else psb[3]
                        P.do("tensor", lambda E, h16=h16, q0=q0, kb=kb, bank=bank:
                             E.matmul(bank[:, (kb % 2) * WIN:(kb % 2 + 1) * WIN],
                                      ktt(h16, q0 + kb * 128, 128),
                                      qtc[:, h16 * CH + q0:h16 * CH + q0 + WIN],
                                      start=True, stop=True),
                             inc=("sMM", 1) if kb == 3 else None)
                    P.wait("scalar", "sMM")
                    P.do("scalar", lambda E: E.activation(
                        pb[:, 0:2 * WIN], psb[2][:, 0:2 * WIN], AF.Exp,
                        bias=stat[:, 8:9], scale=SCALE_A))
                    P.do("scalar", lambda E: E.activation(
                        pb[:, 2 * WIN:4 * WIN], psb[3][:, 0:2 * WIN], AF.Exp,
                        bias=stat[:, 8:9], scale=SCALE_A), inc=("sSC", 1))
                    P.wait("vector", "sSC")
                    P.do("vector", lambda E: E.tensor_mul(
                        pb[:, 2 * WIN:3 * WIN], pb[:, 2 * WIN:3 * WIN],
                        maskw[:, 0:WIN]))
                    P.do("vector", lambda E: E.tensor_mul(
                        pb[:, 3 * WIN:4 * WIN], pb[:, 3 * WIN:4 * WIN],
                        maskw[:, WIN:2 * WIN]), inc=("sVE", 1))
                    if ch == 0 and w2 == 0:
                        P.do("vector", lambda E: E.memset(pb[:, 0:2 * WIN], 0.0),
                             inc=("sVE", 1))
                        P.wait("tensor", "sVE")
                    P.wait("tensor", "sVE")
                    if _p4w < 3:
                        continue
                    P.do("tensor", lambda E: E.matmul(
                        psb[5][0:1, 0:2 * WIN], onesb, pb[:, 0:2 * WIN],
                        start=True, stop=True))
                    P.do("tensor", lambda E: E.matmul(
                        psb[6][0:1, 0:2 * WIN], onesb, pb[:, 2 * WIN:4 * WIN],
                        start=True, stop=True), inc=("sMM", 1))
                    for kb in range(4):
                        r0 = (q0 + kb * 128) // 128
                        P.do("tensor", lambda E, h16=h16, kb=kb, r0=r0: E.matmul(
                            psb[4][:, 0:WIN],
                            vc[:, r0 * D + h16 * 128:r0 * D + h16 * 128 + 128],
                            pb[:, kb * WIN:(kb + 1) * WIN],
                            start=(kb == 0), stop=(kb == 3)),
                            inc=("sMM", 1) if kb == 3 else None)
                    P.wait("vector", "sMM")
                    if _p4w < 4:
                        continue
                    P.dov(lambda E: E.tensor_copy(
                        srow[0:1, 0:WIN], psb[5][0:1, 0:WIN]))
                    P.dov(lambda E: E.tensor_add(
                        srow[0:1, 0:WIN], srow[0:1, 0:WIN],
                        psb[5][0:1, WIN:2 * WIN]))
                    P.dov(lambda E: E.tensor_add(
                        srow[0:1, 0:WIN], srow[0:1, 0:WIN], psb[6][0:1, 0:WIN]))
                    P.dov(lambda E: E.tensor_add(
                        srow[0:1, 0:WIN], srow[0:1, 0:WIN],
                        psb[6][0:1, WIN:2 * WIN]))
                    P.do("vector", lambda E: E.reciprocal(
                        srow2[0:1, 0:WIN], srow[0:1, 0:WIN]), inc=("sVE", 1))
                    P.wait("scalar", "sVE")
                    P.do("scalar", lambda E: E.copy(srowb[0:1, 0:WIN],
                                                    srow2[0:1, 0:WIN]),
                         inc=("sSC", 1))
                    P.wait("tensor", "sSC")
                    P.do("tensor", lambda E: E.matmul(
                        psb[5][:, 0:WIN], onerow, srowb[0:1, 0:WIN],
                        start=True, stop=True), inc=("sMM", 1))
                    P.wait("vector", "sMM")
                    P.dov(lambda E: E.tensor_copy(rwa, psb[5][:, 0:WIN]))
                    P.do("vector", lambda E, h16=h16, q0=q0: E.tensor_mul(
                        otc[:, h16 * CH + q0:h16 * CH + q0 + WIN],
                        psb[4][:, 0:WIN], rwa), inc=("sVE", 1))
                    P.wait("tensor", "sVE")

            for t4 in range(4 if _p4w >= 5 else 0):
                P.wait("sync", "sST")
                P.wait("sync", "sVE")
                P.do("sync", lambda E, ch=ch, t4=t4: E.dma_start(
                    out=xtile,
                    in_=x0[ch * CH + t4 * 128:ch * CH + (t4 + 1) * 128, :]),
                    inc=("sXT", 16))
                P.wait("vector", "sXT")
                for n4 in range(4 if _p4w >= 6 else 0):
                    bank = psb[n4 % 2]
                    P.wait("tensor", "sVE")
                    for k16 in range(16):
                        P.do("tensor", lambda E, t4=t4, n4=n4, k16=k16, bank=bank:
                             E.matmul(bank[:, 0:512],
                                      otc[:, k16 * CH + t4 * 128:
                                          k16 * CH + t4 * 128 + 128],
                                      wt(wsb_o, k16)[:, n4 * 512:(n4 + 1) * 512],
                                      start=(k16 == 0), stop=(k16 == 15)),
                             inc=("sMM", 1) if k16 == 15 else None)
                    P.wait("vector", "sMM")
                    _p4x = _os.environ.get("P4X", "add")
                    if _p4x == "add":
                        P.do("vector", lambda E, n4=n4, bank=bank: E.tensor_add(
                            xtile[:, n4 * 512:(n4 + 1) * 512],
                            xtile[:, n4 * 512:(n4 + 1) * 512], bank[:, 0:512]),
                            inc=("sVE", 1))
                    elif _p4x == "cp":
                        P.do("vector", lambda E, n4=n4, bank=bank: E.tensor_copy(
                            xtile[:, n4 * 512:(n4 + 1) * 512], bank[:, 0:512]),
                            inc=("sVE", 1))
                    else:
                        P.do("vector", lambda E: E.engine_nop(), inc=("sVE", 1))
                if _p4w < 7:
                    continue
                P.wait("sync", "sVE")
                P.do("sync", lambda E, ch=ch, t4=t4: E.dma_start(
                    out=x1_buf[ch * CH + t4 * 128:ch * CH + (t4 + 1) * 128, :],
                    in_=xtile), inc=("sST", 16))
        P.barrier()
        if debug:
            P.wait("gpsimd", "sST")
            P.do("gpsimd", lambda E: E.dma_start(out=dbg["d_x1"], in_=x1_buf),
                 inc=("sGP", 16))

        P.ckpt()
        def ffn_pass(lnT_src, x_src, x_dst, ntok):
            arena.reset()
            wsl2 = [arena.take(16 * 512, BF16), arena.take(16 * 512, BF16)]
            lnT = [arena.take(16 * CH, BF16), arena.take(16 * CH, BF16)]
            hid = arena.take(64 * CH, BF16)
            xc = arena.take(4 * D, F32)
            ac = arena.take(512, F32)
            free_at = [0, 0]
            wsl_free = [0, 0]
            wsl_done = [0, 0]
            for i, (t0, nt) in enumerate(chunks_of(ntok)):
                ntk = nt * 128
                lt = lnT[i % 2]
                sd = "sD0" if i % 2 == 0 else "sD1"
                P.wait("sync", "sMM", free_at[i % 2])
                P.do("sync", lambda E, t0=t0, ntk=ntk, lt=lt, sd=sd: E.dma_start(
                    out=r_sb(lt, 16)[:, :, 0:ntk],
                    in_=r_in(lnT_src[:, t0:t0 + ntk], 16)), inc=(sd, 16))
                P.wait("sync", "sST")
                P.do("sync", lambda E, t0=t0, nt=nt: E.dma_start(
                    out=r_sb(xc, 4)[:, 0:nt, :],
                    in_=r_in(x_src[t0:t0 + nt * 128, :], nt)), inc=("sXT", 16))
                xc_in = P.count("sXT")
                P.wait("tensor", sd)
                P.wait("tensor", "sVE")

                def up_dma(s):
                    bi = s % 2
                    P.wait("sync", "sMM", wsl_free[bi])
                    P.do("sync", lambda E, s=s, b=wsl2[bi]: E.dma_start(
                        out=b.rearrange("p (a c) -> p a c", a=16),
                        in_=r_in(upT[:, s * 512:(s + 1) * 512], 16)),
                        inc=("sW", 16))
                    wsl_done[bi] = P.count("sW")
                up_dma(0)
                for s in range(16):
                    bi = s % 2
                    if s + 1 < 16:
                        up_dma(s + 1)
                    P.wait("tensor", "sW", wsl_done[bi])
                    for h4 in range(4):
                        hg = s * 4 + h4
                        bank = psb[h4 % 2]
                        P.wait("tensor", "sSC")
                        for k16 in range(16):
                            P.do("tensor", lambda E, h4=h4, k16=k16, bank=bank,
                                 lt=lt, ntk=ntk, bi=bi: E.matmul(
                                     bank[:, 0:ntk],
                                     wsl2[bi][:, k16 * 512 + h4 * 128:
                                              k16 * 512 + (h4 + 1) * 128],
                                     lt[:, k16 * CH:k16 * CH + ntk],
                                     start=(k16 == 0), stop=(k16 == 15)),
                                 inc=("sMM", 1) if k16 == 15 else None)
                        P.wait("scalar", "sMM")
                        P.do("scalar", lambda E, hg=hg, bank=bank, ntk=ntk:
                             E.activation(hid[:, hg * CH:hg * CH + ntk],
                                          bank[:, 0:ntk], AF.Relu,
                                          scale=upsc[:, hg:hg + 1]), inc=("sSC", 1))
                    wsl_free[bi] = P.count("sMM")
                P.wait("vector", "sXT", xc_in)
                for half in range(2):
                    def dn_dma(s, half=half):
                        bi = s % 2
                        P.wait("sync", "sMM", wsl_free[bi])
                        P.do("sync", lambda E, s=s, half=half, b=wsl2[bi]:
                             E.dma_start(
                                 out=b[:, 0:4 * 1024]
                                 .rearrange("p (a c) -> p a c", a=4),
                                 in_=r_in(dnT[s * 512:(s + 1) * 512,
                                              half * 1024:(half + 1) * 1024], 4)),
                             inc=("sW", 16))
                        wsl_done[bi] = P.count("sW")
                    dn_dma(0)
                    for s in range(16):
                        bi = s % 2
                        if s + 1 < 16:
                            dn_dma(s + 1)
                        P.wait("tensor", "sW", wsl_done[bi])
                        P.wait("tensor", "sSC")
                        if s == 0:
                            P.wait("tensor", "sVE")
                        for t in range(nt):
                            for n2 in range(2):
                                bank = psb[t * 2 + n2]
                                for k4 in range(4):
                                    hg = s * 4 + k4
                                    P.do("tensor", lambda E, t=t, n2=n2, k4=k4,
                                         hg=hg, bank=bank, s=s, bi=bi: E.matmul(
                                             bank[:, 0:512],
                                             hid[:, hg * CH + t * 128:
                                                 hg * CH + t * 128 + 128],
                                             wsl2[bi][:, k4 * 1024 + n2 * 512:
                                                 k4 * 1024 + (n2 + 1) * 512],
                                             start=(s == 0 and k4 == 0),
                                             stop=(s == 15 and k4 == 3)),
                                         inc=("sMM", 1)
                                         if (k4 == 3) else None)
                        wsl_free[bi] = P.count("sMM")
                    P.wait("vector", "sMM")
                    for t in range(nt):
                        for n2 in range(2):
                            bank = psb[t * 2 + n2]
                            d0 = half * 1024 + n2 * 512
                            P.wait("vector", "sVE")
                            P.dov(lambda E, bank=bank, d0=d0: E.tensor_mul(
                                ac, bank[:, 0:512], dsrow[:, d0:d0 + 512]))
                            P.do("vector", lambda E, t=t, d0=d0: E.tensor_add(
                                xc[:, t * D + d0:t * D + d0 + 512],
                                xc[:, t * D + d0:t * D + d0 + 512], ac),
                                inc=("sVE", 1))
                free_at[i % 2] = P.count("sMM")
                P.wait("sync", "sVE")
                P.do("sync", lambda E, t0=t0, nt=nt: E.dma_start(
                    out=r_in(x_dst[t0:t0 + nt * 128, :], nt),
                    in_=r_sb(xc, 4)[:, 0:nt, :]), inc=("sST", 16))
            P.barrier()

        ln_pass(x1_buf, ln2T_buf, S)
        ffn_pass(ln2T_buf, x1_buf, x2_buf, S)
        if debug:
            P.wait("gpsimd", "sST")
            P.do("gpsimd", lambda E: E.dma_start(out=dbg["d_x2"], in_=x2_buf),
                 inc=("sGP", 16))

        P.ckpt()
        ln3T_buf = ln1T_buf
        ln_pass(x2_buf, ln3T_buf, S)
        projT_pass(ln3T_buf, wbf["wkb"], ktb_buf, False, S)
        projN_pass(ln3T_buf, wbf["wvb"], vb_buf, S)

        arena.reset()
        ktc2 = arena.take(16 * CH, BF16)
        sqk = arena.take(16 * CH, BF16)
        P.do("gpsimd", lambda E: E.memset(bcolk, 0.0), inc=("sGC", 1))
        P.do("gpsimd", lambda E: E.memset(bcolq, 0.0), inc=("sGC", 1))
        P.wait("vector", "sGC")
        for ch in range(NCH):
            P.wait("sync", "sVE")
            P.do("sync", lambda E, ch=ch: E.dma_start(
                out=r_sb(ktc2, 16), in_=r_in(ktb_buf[:, ch * CH:(ch + 1) * CH], 16)),
                inc=("sD0", 16))
            P.wait("vector", "sD0")
            for h16 in range(16):
                P.do("vector", lambda E, h16=h16: E.tensor_mul(
                    sqk[:, h16 * CH:(h16 + 1) * CH],
                    ktc2[:, h16 * CH:(h16 + 1) * CH],
                    ktc2[:, h16 * CH:(h16 + 1) * CH]),
                    inc=("sVE", 1) if h16 == 15 else None)
            P.wait("tensor", "sVE")
            for h16 in range(16):
                bank = psb[h16 % 2]
                bank2 = psb[2 + h16 % 2]
                P.do("tensor", lambda E, h16=h16, bank=bank: E.matmul(
                    bank[0:1, 0:CH], onesb[0:64, :],
                    sqk[0:64, h16 * CH:(h16 + 1) * CH],
                    start=True, stop=True))
                P.do("tensor", lambda E, h16=h16, bank2=bank2: E.matmul(
                    bank2[0:1, 0:CH], onesb[64:128, :],
                    sqk[64:128, h16 * CH:(h16 + 1) * CH],
                    start=True, stop=True), inc=("sMM", 1))
                P.wait("vector", "sMM")
                P.wait("vector", "sVE")
                P.do("vector", lambda E, h16=h16, bank=bank: E.reduce_max(
                    srow[0:1, 4:5], bank[0:1, 0:CH], axis=AX))
                P.dov(lambda E, h16=h16, bank2=bank2: E.reduce_max(
                    srow[0:1, 5:6], bank2[0:1, 0:CH], axis=AX))
                P.dov(lambda E, h16=h16: E.tensor_max(
                    bcolk[0:1, 2 * h16:2 * h16 + 1],
                    bcolk[0:1, 2 * h16:2 * h16 + 1], srow[0:1, 4:5]))
                P.do("vector", lambda E, h16=h16: E.tensor_max(
                    bcolk[0:1, 2 * h16 + 1:2 * h16 + 2],
                    bcolk[0:1, 2 * h16 + 1:2 * h16 + 2], srow[0:1, 5:6]),
                    inc=("sVE", 1))
                if h16 % 2 == 1:
                    P.wait("tensor", "sVE")
        P.barrier()
        if debug:
            P.wait("gpsimd", "sST")
            P.do("gpsimd", lambda E: E.dma_start(out=dbg["d_ktb"], in_=ktb_buf),
                 inc=("sGP", 16))
            P.do("gpsimd", lambda E: E.dma_start(out=dbg["d_vb"], in_=vb_buf),
                 inc=("sGP", 16))
            P.wait("gpsimd", "sGP")

        P.ckpt()
        def gather_switch(E):
            v = bass.Bass.values_load(nc, core_sb, engines=[mybir.EngineType.Pool],
                                      skip_runtime_bounds_check=True,
                                      min_val=0, max_val=7)
            reg = E.snap(v)
            for j in nc.Switch(engines=[E], index=[reg], n=NC8):
                for mm_ in range(M):
                    T = NC8 * mm_ + j
                    E.dma_start(out=x2own_buf[mm_ * 128:(mm_ + 1) * 128, :],
                                in_=x2_buf[T * 128:(T + 1) * 128, :]
                                ).then_inc(P.sem("sGP"), 16)

        if not P.frozen:
            P.sems["sGP"][1] += 16 * M
            P.emit("gpsimd", gather_switch)
        P.barrier()
        if debug:
            P.wait("gpsimd", "sST")
            P.do("gpsimd", lambda E: E.dma_start(out=dbg["d_x2own"], in_=x2own_buf),
                 inc=("sGP", 16))

        ln_pass(x2own_buf, ln3oT_buf, OWN)

        arena.reset()
        wsb_q = arena.take(16 * D, BF16)
        qto = arena.take(16 * OWN, BF16)
        lnTq = arena.take(16 * CH, BF16)
        load_w16(wsb_q, wbf["wqb"])
        for i, (t0, nt) in enumerate(chunks_of(OWN)):
            ntk = nt * 128
            P.wait("sync", "sMM")
            P.do("sync", lambda E, t0=t0, ntk=ntk: E.dma_start(
                out=r_sb(lnTq, 16)[:, :, 0:ntk],
                in_=r_in(ln3oT_buf[:, t0:t0 + ntk], 16)), inc=("sD0", 16))
            P.wait("tensor", "sD0")
            for h16 in range(16):
                bank = psb[h16 % 2]
                P.wait("tensor", "sVE")
                for k16 in range(16):
                    P.do("tensor", lambda E, h16=h16, k16=k16, bank=bank, ntk=ntk:
                         E.matmul(bank[:, 0:ntk],
                                  wt(wsb_q, k16)[:, h16 * 128:(h16 + 1) * 128],
                                  lnTq[:, k16 * CH:k16 * CH + ntk],
                                  start=(k16 == 0), stop=(k16 == 15)),
                         inc=("sMM", 1) if k16 == 15 else None)
                P.wait("vector", "sMM")
                P.do("vector", lambda E, h16=h16, t0=t0, ntk=ntk, bank=bank:
                     E.tensor_copy(qto[:, h16 * OWN + t0:h16 * OWN + t0 + ntk],
                                   bank[:, 0:ntk]), inc=("sVE", 1))
        sqq = lnTq
        P.wait("vector", "sVE")
        for i, (t0, nt) in enumerate(chunks_of(OWN)):
            ntk = nt * 128
            for h16 in range(16):
                P.do("vector", lambda E, h16=h16, t0=t0, ntk=ntk: E.tensor_mul(
                    sqq[:, h16 * CH:h16 * CH + ntk],
                    qto[:, h16 * OWN + t0:h16 * OWN + t0 + ntk],
                    qto[:, h16 * OWN + t0:h16 * OWN + t0 + ntk]),
                    inc=("sVE", 1) if h16 == 15 else None)
            P.wait("tensor", "sVE")
            for h16 in range(16):
                bank = psb[h16 % 2]
                bank2 = psb[2 + h16 % 2]
                P.do("tensor", lambda E, h16=h16, ntk=ntk, bank=bank: E.matmul(
                    bank[0:1, 0:ntk], onesb[0:64, :],
                    sqq[0:64, h16 * CH:h16 * CH + ntk],
                    start=True, stop=True))
                P.do("tensor", lambda E, h16=h16, ntk=ntk, bank2=bank2: E.matmul(
                    bank2[0:1, 0:ntk], onesb[64:128, :],
                    sqq[64:128, h16 * CH:h16 * CH + ntk],
                    start=True, stop=True), inc=("sMM", 1))
                P.wait("vector", "sMM")
                P.wait("vector", "sVE")
                P.do("vector", lambda E, ntk=ntk, bank=bank: E.reduce_max(
                    srow[0:1, 4:5], bank[0:1, 0:ntk], axis=AX))
                P.dov(lambda E, ntk=ntk, bank2=bank2: E.reduce_max(
                    srow[0:1, 5:6], bank2[0:1, 0:ntk], axis=AX))
                P.dov(lambda E, h16=h16: E.tensor_max(
                    bcolq[0:1, 2 * h16:2 * h16 + 1],
                    bcolq[0:1, 2 * h16:2 * h16 + 1], srow[0:1, 4:5]))
                P.do("vector", lambda E, h16=h16: E.tensor_max(
                    bcolq[0:1, 2 * h16 + 1:2 * h16 + 2],
                    bcolq[0:1, 2 * h16 + 1:2 * h16 + 2], srow[0:1, 5:6]),
                    inc=("sVE", 1))
                if h16 % 2 == 1:
                    P.wait("tensor", "sVE")
        P.wait("vector", "sVE")
        P.dov(lambda E: E.tensor_add(srow[0:1, 8:40], bcolq[0:1, 0:32],
                                     bcolk[0:1, 0:32]))
        P.do("vector", lambda E: E.tensor_scalar(
            srow[0:1, 8:40], srow[0:1, 8:40], -0.5 * SCALE_B, None, op0=ALU.mult),
            inc=("sVE", 1))
        P.wait("scalar", "sVE")
        P.do("scalar", lambda E: E.copy(srowb[0:1, 0:32], srow[0:1, 8:40]),
             inc=("sSC", 1))
        P.wait("tensor", "sSC")
        P.do("tensor", lambda E: E.matmul(psb[2][:, 0:32], onerow, srowb[0:1, 0:32],
                                          start=True, stop=True), inc=("sMM", 1))
        P.wait("scalar", "sMM")
        P.do("scalar", lambda E: E.copy(bcols, psb[2][:, 0:32]), inc=("sSC", 1))
        P.wait("tensor", "sSC")
        P.wait("vector", "sSC")

        P.ckpt()
        ktk = [arena.take(4 * 512, BF16), arena.take(4 * 512, BF16)]
        vtk = [arena.take(4 * 512, BF16), arena.take(4 * 512, BF16)]
        pb2 = arena.take(512, BF16)
        pb3 = arena.take(512, BF16)
        otall = arena.take(16 * OWN, BF16)
        rr = arena.take(256, F32)
        rwa2 = arena.take(128, F32)
        rwa3 = arena.take(128, F32)

        kv_free = [0, 0]
        for m in range(M):
            for hg in range(4):
                P.wait("tensor", "sVE")
                P.wait("tensor", "sSC")
                for b in range(2, 8):
                    P.do("tensor", lambda E, b=b: E.matmul(
                        psb[b][:, 0:512], onerow, zrow, start=True, stop=False),
                        inc=("sMM", 1) if b == 7 else None)
                nkc = m + 1
                for ikc in range(nkc):
                    kc = m - ikc
                    for hf in range(2):
                        it = ikc * 2 + hf
                        bi = it % 2
                        kt = ktk[bi]
                        vt = vtk[bi]
                        skv = "sKV0" if bi == 0 else "sKV1"
                        P.do("sync", lambda E, v=kv_free[bi]: E.wait_ge(
                            P.sem("sMM"), v))
                        P.do("sync", lambda E, hg=hg, kc=kc, hf=hf, kt=kt:
                             E.dma_start(
                                 out=r_sb(kt, 4),
                                 in_=r_in(ktb_buf[hg * 512:(hg + 1) * 512,
                                                  kc * 1024 + hf * 512:
                                                  kc * 1024 + (hf + 1) * 512], 4)),
                             inc=(skv, 16))
                        P.do("sync", lambda E, hg=hg, kc=kc, hf=hf, vt=vt:
                             E.dma_start(
                                 out=r_sb(vt, 4),
                                 in_=r_in(vb_buf[kc * 1024 + hf * 512:
                                                 kc * 1024 + (hf + 1) * 512,
                                                 hg * 512:(hg + 1) * 512], 4)),
                             inc=(skv, 16))
                        P.wait("tensor", skv)
                        last_it = (ikc == nkc - 1 and hf == 1)
                        for hl in range(4):
                            h = hg * 4 + hl
                            P.wait("tensor", "sSC")
                            P.wait("tensor", "sVE")
                            for kb in range(4):
                                for st_ in range(2):
                                    P.do("tensor", lambda E, hl=hl, kb=kb, st_=st_,
                                         kt=kt, m=m, h=h: E.matmul(
                                             psb[st_][:, kb * 128:(kb + 1) * 128],
                                             kt[st_ * 64:(st_ + 1) * 64,
                                                hl * 512 + kb * 128:
                                                hl * 512 + (kb + 1) * 128],
                                             qto[st_ * 64:(st_ + 1) * 64,
                                                 h * OWN + m * 128:
                                                 h * OWN + (m + 1) * 128],
                                             start=True, stop=True,
                                             tile_position=(st_ * 64, 0)),
                                         inc=("sMM", 1)
                                         if (kb == 3 and st_ == 1) else None)
                            P.wait("scalar", "sMM")
                            P.do("scalar", lambda E, h=h: E.activation(
                                pb2, psb[0][:, 0:512], AF.Exp,
                                bias=bcols[:, 2 * h:2 * h + 1], scale=SCALE_B))
                            P.do("scalar", lambda E, h=h: E.activation(
                                pb3, psb[1][:, 0:512], AF.Exp,
                                bias=bcols[:, 2 * h + 1:2 * h + 2], scale=SCALE_B),
                                inc=("sSC", 1))
                            if kc == m:
                                P.wait("vector", "sSC")
                                P.do("vector", lambda E, hf=hf: E.tensor_mul(
                                    pb2, pb2, dmsk[:, hf * 512:(hf + 1) * 512]))
                                P.do("vector", lambda E, hf=hf: E.tensor_mul(
                                    pb3, pb3, dmsk[:, hf * 512:(hf + 1) * 512]),
                                    inc=("sVE", 1))
                                P.wait("tensor", "sVE")
                            else:
                                P.wait("tensor", "sSC")
                            P.do("tensor", lambda E, hl=hl: E.matmul(
                                psb[4 + hl][0:1, 0:512], onesb, pb2,
                                start=False, stop=False))
                            P.do("tensor", lambda E, hl=hl: E.matmul(
                                psb[4 + hl][32:33, 0:512], onesb, pb3,
                                start=False, stop=False, tile_position=(0, 32)))
                            ob = psb[2 + hl // 2]
                            oc = (hl % 2) * 256
                            for kb in range(4):
                                P.do("tensor", lambda E, hl=hl, kb=kb, vt=vt, ob=ob,
                                     oc=oc: E.matmul(
                                         ob[:, oc:oc + 128],
                                         vt[:, kb * 512 + hl * 128:
                                            kb * 512 + hl * 128 + 128],
                                         pb2[:, kb * 128:(kb + 1) * 128],
                                         start=False, stop=False))
                                P.do("tensor", lambda E, hl=hl, kb=kb, vt=vt, ob=ob,
                                     oc=oc: E.matmul(
                                         ob[:, oc + 128:oc + 256],
                                         vt[:, kb * 512 + hl * 128:
                                            kb * 512 + hl * 128 + 128],
                                         pb3[:, kb * 128:(kb + 1) * 128],
                                         start=False, stop=False),
                                     inc=("sMM", 1) if kb == 3 else None)
                        kv_free[bi] = P.count("sMM")
                for b in range(2, 8):
                    P.do("tensor", lambda E, b=b: E.matmul(
                        psb[b][:, 0:512], onerow, zrow, start=False, stop=True),
                        inc=("sMM", 1) if b == 7 else None)
                for hl in range(4):
                    h = hg * 4 + hl
                    P.wait("vector", "sMM")
                    P.do("vector", lambda E, hl=hl: E.tensor_copy(
                        srow[0:1, 512:1024], psb[4 + hl][0:1, 0:512]))
                    P.dov(lambda E, hl=hl: E.tensor_copy(
                        srow2[0:1, 512:1024], psb[4 + hl][32:33, 0:512]))
                    P.do("vector", lambda E: E.tensor_add(
                        srow[0:1, 0:128], srow[0:1, 512:640], srow[0:1, 640:768]))
                    P.dov(lambda E: E.tensor_add(
                        srow[0:1, 128:256], srow[0:1, 768:896],
                        srow[0:1, 896:1024]))
                    P.dov(lambda E: E.tensor_add(
                        srow[0:1, 0:128], srow[0:1, 0:128], srow[0:1, 128:256]))
                    P.do("vector", lambda E: E.tensor_add(
                        srow2[0:1, 0:128], srow2[0:1, 512:640], srow2[0:1, 640:768]))
                    P.dov(lambda E: E.tensor_add(
                        srow2[0:1, 128:256], srow2[0:1, 768:896],
                        srow2[0:1, 896:1024]))
                    P.dov(lambda E: E.tensor_add(
                        srow[0:1, 128:256], srow2[0:1, 0:128],
                        srow2[0:1, 128:256]))
                    P.dov(lambda E: E.reciprocal(srow2[0:1, 0:256],
                                                 srow[0:1, 0:256]))
                    P.do("vector", lambda E, h=h: E.tensor_scalar(
                        srow2[0:1, 128:256], srow2[0:1, 128:256],
                        lams[0:1, h:h + 1], None, op0=ALU.mult), inc=("sVE", 1))
                    P.wait("scalar", "sVE")
                    P.do("scalar", lambda E: E.copy(srowb[0:1, 0:256],
                                                    srow2[0:1, 0:256]),
                         inc=("sSC", 1))
                    P.wait("tensor", "sSC")
                    P.do("tensor", lambda E: E.matmul(
                        psb[0][:, 0:256], onerow, srowb[0:1, 0:256],
                        start=True, stop=True), inc=("sMM", 1))
                    P.wait("vector", "sMM")
                    P.dov(lambda E: E.tensor_copy(rr, psb[0][:, 0:256]))
                    ob = psb[2 + hl // 2]
                    oc = (hl % 2) * 256
                    P.do("vector", lambda E, ob=ob, oc=oc: E.tensor_mul(
                        rwa2, ob[:, oc:oc + 128], rr[:, 0:128]))
                    P.do("vector", lambda E, ob=ob, oc=oc: E.tensor_mul(
                        rwa3, ob[:, oc + 128:oc + 256], rr[:, 128:256]),
                        inc=("sVE", 1))
                    P.wait("vector", "sVE")
                    P.do("vector", lambda E, h=h, m=m: E.tensor_sub(
                        otall[:, h * OWN + m * 128:h * OWN + (m + 1) * 128],
                        rwa2, rwa3), inc=("sVE", 1))
                    P.wait("tensor", "sVE")
        P.barrier()

        P.ckpt()
        wsb_ob = wsb_q
        xt2 = lnTq.bitcast(F32)
        load_w16(wsb_ob, wbf["wob"])
        for m in range(M):
            P.wait("sync", "sST")
            P.wait("sync", "sVE")
            P.do("sync", lambda E, m=m: E.dma_start(
                out=xt2[:, 0:D], in_=x2own_buf[m * 128:(m + 1) * 128, :]),
                inc=("sXT", 16))
            P.wait("vector", "sXT")
            P.wait("tensor", "sXT")
            for n4 in range(4):
                bank = psb[n4 % 2]
                P.wait("tensor", "sVE")
                for k16 in range(16):
                    P.do("tensor", lambda E, m=m, n4=n4, k16=k16, bank=bank:
                         E.matmul(bank[:, 0:512],
                                  otall[:, k16 * OWN + m * 128:
                                        k16 * OWN + m * 128 + 128],
                                  wt(wsb_ob, k16)[:, n4 * 512:(n4 + 1) * 512],
                                  start=(k16 == 0), stop=(k16 == 15)),
                         inc=("sMM", 1) if k16 == 15 else None)
                P.wait("vector", "sMM")
                P.do("vector", lambda E, n4=n4, bank=bank: E.tensor_add(
                    xt2[:, n4 * 512:(n4 + 1) * 512],
                    xt2[:, n4 * 512:(n4 + 1) * 512], bank[:, 0:512]),
                    inc=("sVE", 1))
            P.wait("sync", "sVE")
            P.do("sync", lambda E, m=m: E.dma_start(
                out=x3own_buf[m * 128:(m + 1) * 128, :], in_=xt2[:, 0:D]),
                inc=("sST", 16))
        P.barrier()
        if debug:
            P.wait("gpsimd", "sST")
            P.do("gpsimd", lambda E: E.dma_start(out=dbg["d_x3own"], in_=x3own_buf),
                 inc=("sGP", 16))

        P.ckpt()
        ln_pass(x3own_buf, ln4T_buf, OWN)
        ffn_pass(ln4T_buf, x3own_buf, x4own_buf, OWN)
        P.do("gpsimd", lambda E: E.dma_start(out=y, in_=x4own_buf), inc=("sGP", 16))
        P.wait("gpsimd", "sGP")
        P.barrier()

        @blk.sync
        def _(E):
            for f in P.ops["sync"]:
                f(E)

        @blk.scalar
        def _(E):
            for f in P.ops["scalar"]:
                f(E)

        @blk.vector
        def _(E):
            for f in P.ops["vector"]:
                f(E)

        @blk.tensor
        def _(E):
            for f in P.ops["tensor"]:
                f(E)

        @blk.gpsimd
        def _(E):
            for f in P.ops["gpsimd"]:
                f(E)

    return nc


def make_dmask(j):
    import ml_dtypes
    kb = np.arange(8)[:, None, None]
    p = np.arange(128)[None, :, None]
    w = np.arange(128)[None, None, :]
    m = (kb * 128 + p <= 128 * j + w).astype(np.float32)
    return np.ascontiguousarray(m.reshape(8 * 128, 128).astype(ml_dtypes.bfloat16))


def _prep_inputs(kw):
    x = np.ascontiguousarray(np.asarray(kw["L0_attn_a_hidden_states"],
                                        np.float32).reshape(-1, D))
    base = {
        "x0": x,
        "wqa": np.ascontiguousarray(np.asarray(kw["L0_attn_a_Wq"], np.float32)),
        "wka": np.ascontiguousarray(np.asarray(kw["L0_attn_a_Wk"], np.float32)),
        "wva": np.ascontiguousarray(np.asarray(kw["L0_attn_a_Wv"], np.float32)),
        "woa": np.ascontiguousarray(np.asarray(kw["L0_attn_a_Wo"], np.float32)),
        "wqb": np.ascontiguousarray(np.asarray(kw["L0_attn_b_Wq"], np.float32)),
        "wkb": np.ascontiguousarray(np.asarray(kw["L0_attn_b_Wk"], np.float32)),
        "wvb": np.ascontiguousarray(np.asarray(kw["L0_attn_b_Wv"], np.float32)),
        "wob": np.ascontiguousarray(np.asarray(kw["L0_attn_b_Wo"], np.float32)),
        "lam0": np.ascontiguousarray(np.asarray(kw["L0_attn_b_lambda_init"],
                                                np.float32).reshape(1, HEADS)),
        "wup": np.ascontiguousarray(np.asarray(kw["L0_ffn_up_weight"], np.float32)),
        "wdn": np.ascontiguousarray(np.asarray(kw["L0_ffn_down_weight"],
                                               np.float32)),
        "scup": np.ascontiguousarray(np.asarray(kw["L0_ffn_up_scale"], np.float32)
                                     .reshape(FFN_H // 128, 128).T),
        "scdn": np.ascontiguousarray(np.asarray(kw["L0_ffn_down_scale"],
                                                np.float32).reshape(1, D)),
        "scout": np.ascontiguousarray(np.asarray(kw["L0_ffn_output_scale"],
                                                 np.float32).reshape(1, 1)),
    }
    return x.shape[0], base


def kernel(**kw):
    S, base = _prep_inputs(kw)
    M = S // 128 // NC8
    nc = build_program(S, debug=False)
    in_maps = []
    for j in range(NC8):
        mdict = dict(base)
        mdict["core_slot"] = np.array([[j]], np.int32)
        mdict["dmask"] = make_dmask(j)
        mdict["rampi"] = np.broadcast_to(np.arange(CH, dtype=np.float32),
                                         (128, CH)).copy()
        in_maps.append(mdict)
    res = run_bass_kernel_spmd(nc, in_maps, core_ids=list(range(NC8)))
    out = np.empty((S, D), np.float32)
    for j in range(NC8):
        yj = res.results[j]["y"]
        for mm_ in range(M):
            T = NC8 * mm_ + j
            out[T * 128:(T + 1) * 128, :] = yj[mm_ * 128:(mm_ + 1) * 128, :]
    return out.reshape(1, S, D)


# version 03
# speedup vs baseline: 1.1577x; 1.0416x over previous
import contextlib
import math
import numpy as np

import concourse.bass as bass
import concourse.mybir as mybir
from concourse.bass_utils import run_bass_kernel_spmd

F32 = mybir.dt.float32
BF16 = mybir.dt.bfloat16
I32 = mybir.dt.int32
AF = mybir.ActivationFunctionType
ALU = mybir.AluOpType
AX = mybir.AxisListType.X

D = 2048
HEADS = 16
WIN = 256
FFN_H = 8192
NC8 = 8
CH = 512

INV_2PI = float(1.0 / (2.0 * math.pi))
TWO_PI_HI = 6.28125
TWO_PI_LO = float(2.0 * math.pi - 6.28125)
LN_THETA = float(-math.log(10000.0) / 64.0)
SCALE_A = float(1.0 / math.sqrt(128.0))
SCALE_B = 0.125


class Prog:
    def __init__(self):
        self.ops = {e: [] for e in ("sync", "scalar", "vector", "tensor", "gpsimd")}
        self.sems = {}
        self.frozen = False
        self.phase = 0
        self.max_phase = 99
        self.relay = None

    def ckpt(self):
        self.phase += 1
        if self.phase >= self.max_phase:
            self.frozen = True

    def add_sem(self, name, handle):
        self.sems[name] = [handle, 0]

    def sem(self, name):
        return self.sems[name][0]

    def count(self, name):
        return self.sems[name][1]

    def emit(self, eng, fn):
        if self.frozen:
            return
        self.ops[eng].append(fn)

    def wait(self, eng, name, val=None):
        h, c = self.sems[name]
        v = c if val is None else val
        if v > 0:
            self.emit(eng, lambda E, h=h, v=v: E.wait_ge(h, v))

    def do(self, eng, fn, inc=None):
        if self.frozen:
            return
        if inc is None:
            self.emit(eng, fn)
        else:
            name, n = inc
            rec = self.sems[name]
            rec[1] += n
            h = rec[0]
            self.emit(eng, lambda E, fn=fn, h=h, n=n: fn(E).then_inc(h, n))

    def dov(self, fn):
        self.do("vector", fn, inc=("sVE", 1))
        self.wait("vector", "sVE")

    def barrier(self):
        for eng in ("sync", "scalar", "vector", "tensor"):
            for name in self.sems:
                self.wait(eng, name)
        if self.relay is not None:
            self.relay()
            for name in ("sIN", "sK", "sGP", "sW", "sD0", "sD1", "sST",
                         "sXT", "sKV0", "sKV1"):
                self.wait("gpsimd", name)
        else:
            for name in self.sems:
                self.wait("gpsimd", name)


class Arena:
    def __init__(self, base_ap, nbytes):
        self.base = base_ap
        self.nbytes = nbytes
        self.cur = 0

    def reset(self):
        self.cur = 0

    def take(self, nfree, dtype):
        sz = mybir.dt.size(dtype)
        nb = nfree * sz
        self.cur = (self.cur + 63) // 64 * 64
        off = self.cur
        self.cur += nb
        assert self.cur <= self.nbytes, f"arena overflow: {self.cur} > {self.nbytes}"
        v = self.base[:, off // 4:(off + nb) // 4]
        if dtype == F32:
            return v
        return v.bitcast(dtype)


def chunks_of(ntok):
    out = []
    t = 0
    while t < ntok:
        n = min(4, (ntok - t) // 128)
        out.append((t, n))
        t += n * 128
    return out


def build_program(S, debug=False, max_phase=99):
    NCH = S // CH
    M = S // 128 // NC8
    OWN = M * 128
    nc = bass.Bass("TRN2", target_bir_lowering=False)

    x0 = nc.dram_tensor("x0", [S, D], F32, kind="ExternalInput").ap()
    win = {}
    for nm in ["wqa", "wka", "wva", "woa", "wqb", "wkb", "wvb", "wob"]:
        win[nm] = nc.dram_tensor(nm, [D, D], F32, kind="ExternalInput").ap()
    lam0 = nc.dram_tensor("lam0", [1, HEADS], F32, kind="ExternalInput").ap()
    wup = nc.dram_tensor("wup", [FFN_H, D], F32, kind="ExternalInput").ap()
    wdn = nc.dram_tensor("wdn", [D, FFN_H], F32, kind="ExternalInput").ap()
    scup = nc.dram_tensor("scup", [128, FFN_H // 128], F32,
                          kind="ExternalInput").ap()
    scdn = nc.dram_tensor("scdn", [1, D], F32, kind="ExternalInput").ap()
    scout = nc.dram_tensor("scout", [1, 1], F32, kind="ExternalInput").ap()
    core_slot = nc.dram_tensor("core_slot", [1, 1], I32, kind="ExternalInput").ap()
    dmask = nc.dram_tensor("dmask", [8 * 128, 128], BF16, kind="ExternalInput").ap()
    rampi = nc.dram_tensor("rampi", [128, CH], F32, kind="ExternalInput").ap()
    y = nc.dram_tensor("y", [OWN, D], F32, kind="ExternalOutput").ap()

    wbf = {}
    for nm in ["wqa", "wka", "wva", "woa", "wqb", "wkb", "wvb", "wob"]:
        wbf[nm] = nc.dram_tensor(nm + "_b", [D, D], BF16, kind="Internal").ap()
    upT = nc.dram_tensor("upT", [D, FFN_H], BF16, kind="Internal").ap()
    dnT = nc.dram_tensor("dnT", [FFN_H, D], BF16, kind="Internal").ap()
    ln1T_buf = nc.dram_tensor("ln1T_buf", [D, S], BF16, kind="Internal").ap()
    qa_buf = nc.dram_tensor("qa_buf", [D, S], BF16, kind="Internal").ap()
    ka_buf = nc.dram_tensor("ka_buf", [D, S], BF16, kind="Internal").ap()
    va_buf = nc.dram_tensor("va_buf", [S, D], BF16, kind="Internal").ap()
    x1_buf = nc.dram_tensor("x1_buf", [S, D], F32, kind="Internal").ap()
    ln2T_buf = nc.dram_tensor("ln2T_buf", [D, S], BF16, kind="Internal").ap()
    x2_buf = nc.dram_tensor("x2_buf", [S, D], F32, kind="Internal").ap()
    ktb_buf = nc.dram_tensor("ktb_buf", [D, S], BF16, kind="Internal").ap()
    vb_buf = nc.dram_tensor("vb_buf", [S, D], BF16, kind="Internal").ap()
    x2own_buf = nc.dram_tensor("x2own_buf", [OWN, D], F32, kind="Internal").ap()
    ln3oT_buf = nc.dram_tensor("ln3oT_buf", [D, OWN], BF16, kind="Internal").ap()
    x3own_buf = nc.dram_tensor("x3own_buf", [OWN, D], F32, kind="Internal").ap()
    ln4T_buf = nc.dram_tensor("ln4T_buf", [D, OWN], BF16, kind="Internal").ap()
    x4own_buf = nc.dram_tensor("x4own_buf", [OWN, D], F32, kind="Internal").ap()

    dbg = {}
    if debug:
        for nm, shp, dt in [
            ("d_ln1T", [D, S], BF16), ("d_qa", [D, S], BF16), ("d_ka", [D, S], BF16),
            ("d_va", [S, D], BF16), ("d_x1", [S, D], F32), ("d_x2", [S, D], F32),
            ("d_ktb", [D, S], BF16), ("d_vb", [S, D], BF16),
            ("d_upT", [D, FFN_H], BF16), ("d_dnT", [FFN_H, D], BF16),
            ("d_x2own", [OWN, D], F32), ("d_x3own", [OWN, D], F32),
        ]:
            dbg[nm] = nc.dram_tensor(nm, shp, dt, kind="ExternalOutput").ap()

    P = Prog()
    P.max_phase = max_phase

    with contextlib.ExitStack() as ctx:
        def sb(name, shape, dt):
            return ctx.enter_context(nc.sbuf_tensor(name, shape, dt))[:]

        ident = sb("ident", [128, 128], BF16)
        pmat = sb("pmat", [128, 128], BF16)
        onesb = sb("onesb", [128, 1], BF16)
        ones2 = sb("ones2", [128, 2], BF16)
        onerow = sb("onerow", [1, 128], BF16)
        zrow = sb("zrow", [1, 512], BF16)
        invf2 = sb("invf2", [128, 1], F32)
        dsrow = sb("dsrow", [128, D], F32)
        upsc = sb("upsc", [128, FFN_H // 128], F32)
        lams = sb("lams", [1, HEADS], F32)
        maskw = sb("maskw", [128, 2 * WIN], BF16)
        dmsk = sb("dmsk", [128, 8 * 128], BF16)
        core_sb = sb("core_sb", [1, 1], I32)
        ramp = sb("ramp", [128, CH], F32)
        stat = sb("stat", [128, 32], F32)
        srow = sb("srow", [1, 2048], F32)
        srow2 = sb("srow2", [1, 2048], F32)
        srowb = sb("srowb", [1, 2048], BF16)
        bcolq = sb("bcolq", [1, 2 * HEADS], F32)
        bcolk = sb("bcolk", [1, 2 * HEADS], F32)
        bcols = sb("bcols", [128, 2 * HEADS], F32)

        ARENA_B = 172 * 1024
        arena = Arena(sb("arena", [128, ARENA_B // 4], F32), ARENA_B)

        psb = [ctx.enter_context(nc.psum_tensor(f"ps{i}", [128, 512], F32))[:]
               for i in range(8)]
        pst = psb[7].bitcast(BF16)
        pst2 = psb[6].bitcast(BF16)

        for s_ in ["sIN", "sSC", "sVE", "sMM", "sGP", "sK", "sGC",
            "sW", "sD0", "sD1", "sST", "sXT", "sKV0", "sKV1"]:
            P.add_sem(s_, ctx.enter_context(nc.semaphore(s_)))

        blk = ctx.enter_context(nc.Block())

        def r_in(dram2d, a):
            return dram2d.rearrange("(a p) c -> p a c", p=128)

        def r_sb(sb2d, a):
            return sb2d.rearrange("p (a c) -> p a c", a=a)

        for nm in ["wqa", "wka", "wva", "woa", "wqb", "wkb", "wvb", "wob"]:
            P.do("gpsimd", lambda E, s=win[nm], d=wbf[nm]: E.dma_start(
                out=d, in_=s, max_dma_last_dim=8192), inc=("sGP", 16))
        wcast_done = P.count("sGP")

        P.do("gpsimd", lambda E: E.memset(ident, 0.0), inc=("sGC", 1))
        P.wait("gpsimd", "sGC")
        P.do("gpsimd", lambda E: E.affine_select(
            out=ident, in_=ident, compare_op=ALU.not_equal, fill=1.0, base=0,
            pattern=[[-1, 128]], channel_multiplier=1))
        P.do("gpsimd", lambda E: E.memset(pmat, 0.0), inc=("sGC", 1))
        P.wait("gpsimd", "sGC")
        P.do("gpsimd", lambda E: E.affine_select(
            out=pmat, in_=pmat, compare_op=ALU.not_equal, fill=-1.0, base=-64,
            pattern=[[-1, 128]], channel_multiplier=1), inc=("sGC", 1))
        P.wait("gpsimd", "sGC")
        P.do("gpsimd", lambda E: E.affine_select(
            out=pmat, in_=pmat, compare_op=ALU.not_equal, fill=1.0, base=64,
            pattern=[[-1, 128]], channel_multiplier=1), inc=("sGC", 1))
        P.wait("gpsimd", "sGC")
        P.do("gpsimd", lambda E: E.memset(onesb, 1.0))
        P.do("gpsimd", lambda E: E.memset(onerow, 1.0))
        P.do("gpsimd", lambda E: E.memset(zrow, 0.0))
        P.do("gpsimd", lambda E: E.memset(ones2, 0.0), inc=("sGC", 1))
        P.wait("gpsimd", "sGC")
        P.do("gpsimd", lambda E: E.memset(ones2[0:64, 0:1], 1.0))
        P.do("gpsimd", lambda E: E.memset(ones2[64:128, 1:2], 1.0))
        P.do("gpsimd", lambda E: E.memset(maskw, 1.0), inc=("sGC", 1))
        P.wait("gpsimd", "sGC")
        P.do("gpsimd", lambda E: E.affine_select(
            out=maskw[:, 0:WIN], in_=maskw[:, 0:WIN], compare_op=ALU.is_ge,
            fill=0.0, base=0, pattern=[[1, WIN]], channel_multiplier=-1))
        P.do("gpsimd", lambda E: E.affine_select(
            out=maskw[:, WIN:2 * WIN], in_=maskw[:, WIN:2 * WIN],
            compare_op=ALU.is_ge, fill=0.0, base=-128, pattern=[[1, WIN]],
            channel_multiplier=-1), inc=("sGC", 1))
        P.do("gpsimd", lambda E: E.iota(stat[:, 0:1], pattern=[[0, 1]], base=0,
                                        channel_multiplier=1,
                                        allow_small_or_imprecise_dtypes=True),
             inc=("sGC", 1))
        P.wait("vector", "sGC")
        P.dov(lambda E: E.tensor_single_scalar(invf2, stat[:, 0:1], 63.5,
                                               op=ALU.is_gt))
        P.dov(lambda E: E.tensor_scalar(invf2, invf2, -64.0, None, op0=ALU.mult))
        P.do("vector", lambda E: E.tensor_add(invf2, invf2, stat[:, 0:1]),
             inc=("sVE", 1))
        P.wait("scalar", "sVE")
        P.do("scalar", lambda E: E.activation(invf2, invf2, AF.Exp, scale=LN_THETA),
             inc=("sSC", 1))
        P.do("sync", lambda E: E.dma_start(out=r_sb(dmsk, 8),
                                           in_=r_in(dmask, 8)), inc=("sIN", 16))
        P.do("sync", lambda E: E.dma_start(out=core_sb, in_=core_slot), inc=("sIN", 16))

        def _relay():
            P.wait("sync", "sK")
            P.do("sync", lambda E: E.dma_start(out=core_sb, in_=core_slot),
                 inc=("sK", 16))
        P.relay = _relay
        P.do("sync", lambda E: E.dma_start(out=upsc, in_=scup), inc=("sIN", 16))
        P.do("sync", lambda E: E.dma_start(out=srow[0:1, 0:D], in_=scdn),
             inc=("sIN", 16))
        P.do("sync", lambda E: E.dma_start(out=srow2[0:1, 0:1], in_=scout),
             inc=("sIN", 16))
        P.do("sync", lambda E: E.dma_start(out=lams, in_=lam0), inc=("sIN", 16))
        P.do("sync", lambda E: E.dma_start(out=ramp, in_=rampi), inc=("sIN", 16))
        sm_in = P.count("sIN")
        sm_in2 = sm_in
        P.wait("scalar", "sIN", sm_in)
        P.do("scalar", lambda E: E.activation(lams, lams, AF.Sigmoid), inc=("sSC", 1))
        P.wait("vector", "sIN", sm_in)
        P.do("vector", lambda E: E.tensor_scalar(srow[0:1, 0:D], srow[0:1, 0:D],
                                                 srow2[0:1, 0:1], None, op0=ALU.mult),
             inc=("sVE", 1))
        P.wait("scalar", "sVE")
        P.do("scalar", lambda E: E.copy(srowb[0:1, 0:D], srow[0:1, 0:D]),
             inc=("sSC", 1))
        P.wait("tensor", "sSC")
        P.wait("tensor", "sGC")
        for i in range(4):
            P.do("tensor", lambda E, i=i: E.matmul(
                psb[5][:, 0:512], onerow, srowb[0:1, i * 512:(i + 1) * 512],
                start=True, stop=True), inc=("sMM", 1))
            P.wait("vector", "sMM")
            P.do("vector", lambda E, i=i: E.tensor_copy(
                dsrow[:, i * 512:(i + 1) * 512], psb[5][:, 0:512]), inc=("sVE", 1))
            P.wait("tensor", "sVE")

        arena.reset()
        wst32 = arena.take(2048, F32)
        wstbf = arena.take(2048, BF16)
        upTs = arena.take(16 * 1024, BF16)
        dnTs = arena.take(8 * 2048, BF16)

        for sup in range(FFN_H // 1024):
            for strip in range(8):
                h0 = sup * 1024 + strip * 128
                P.wait("sync", "sSC")
                P.do("sync", lambda E, h0=h0: E.dma_start(
                    out=wst32, in_=wup[h0:h0 + 128, :]), inc=("sW", 16))
                P.wait("scalar", "sW")
                P.wait("scalar", "sMM")
                P.do("scalar", lambda E: E.sign(wstbf, wst32), inc=("sSC", 1))
                P.wait("tensor", "sSC")
                for g in range(2):
                    bank = pst if g == 0 else pst2
                    P.wait("tensor", "sVE")
                    for t8 in range(8):
                        d16 = g * 8 + t8
                        P.do("tensor", lambda E, d16=d16, t8=t8, bank=bank:
                             E.transpose(bank[:, t8 * 128:(t8 + 1) * 128],
                                         wstbf[:, d16 * 128:(d16 + 1) * 128], ident),
                             inc=("sMM", 1) if t8 == 7 else None)
                    P.wait("vector", "sMM")
                    for t8 in range(8):
                        d16 = g * 8 + t8
                        P.do("vector", lambda E, d16=d16, t8=t8, strip=strip,
                             bank=bank: E.tensor_copy(
                                 upTs[:, d16 * 1024 + strip * 128:
                                      d16 * 1024 + strip * 128 + 128],
                                 bank[:, t8 * 128:(t8 + 1) * 128]),
                             inc=("sVE", 1) if t8 == 7 else None)
            P.wait("sync", "sVE")
            P.wait("sync", "sST")
            P.do("sync", lambda E, sup=sup: E.dma_start(
                out=r_in(upT[:, sup * 1024:(sup + 1) * 1024], 16),
                in_=r_sb(upTs, 16)), inc=("sST", 16))
            P.wait("vector", "sST")
        up_staged = P.count("sST")

        for hw in range(FFN_H // 1024):
            for dstrip in range(16):
                P.wait("sync", "sSC")
                P.do("sync", lambda E, hw=hw, dstrip=dstrip: E.dma_start(
                    out=wst32[:, 0:1024],
                    in_=wdn[dstrip * 128:(dstrip + 1) * 128,
                            hw * 1024:(hw + 1) * 1024]), inc=("sW", 16))
                P.wait("scalar", "sW")
                P.wait("scalar", "sMM")
                P.do("scalar", lambda E: E.sign(wstbf[:, 0:1024], wst32[:, 0:1024]),
                     inc=("sSC", 1))
                P.wait("tensor", "sSC")
                P.wait("tensor", "sVE")
                for h8 in range(8):
                    P.do("tensor", lambda E, h8=h8: E.transpose(
                        pst[:, h8 * 128:(h8 + 1) * 128],
                        wstbf[:, h8 * 128:(h8 + 1) * 128], ident),
                        inc=("sMM", 1) if h8 == 7 else None)
                P.wait("vector", "sMM")
                for h8 in range(8):
                    P.do("vector", lambda E, h8=h8, dstrip=dstrip: E.tensor_copy(
                        dnTs[:, h8 * 2048 + dstrip * 128:
                             h8 * 2048 + dstrip * 128 + 128],
                        pst[:, h8 * 128:(h8 + 1) * 128]),
                        inc=("sVE", 1) if h8 == 7 else None)
            P.wait("sync", "sVE")
            P.wait("sync", "sST")
            P.do("sync", lambda E, hw=hw: E.dma_start(
                out=r_in(dnT[hw * 1024:(hw + 1) * 1024, :], 8),
                in_=r_sb(dnTs, 8)), inc=("sST", 16))
            P.wait("vector", "sST")
        if debug:
            for s_ in ("sIN", "sST", "sW", "sD0", "sD1"):
                P.wait("gpsimd", s_)
            P.do("gpsimd", lambda E: E.dma_start(out=dbg["d_upT"], in_=upT),
                 inc=("sGP", 16))
            P.do("gpsimd", lambda E: E.dma_start(out=dbg["d_dnT"], in_=dnT),
                 inc=("sGP", 16))
        P.barrier()
        P.ckpt()

        def emit_ln_chunk(xc, lnc, sq, nt):
            for t in range(nt):
                P.do("vector", lambda E, t=t: E.reduce_sum(
                    stat[:, t:t + 1], xc[:, t * D:(t + 1) * D], axis=AX),
                    inc=("sVE", 1) if t == nt - 1 else None)
            for t in range(nt):
                P.do("scalar", lambda E, t=t: E.activation(
                    sq[:, t * D:(t + 1) * D], xc[:, t * D:(t + 1) * D], AF.Square),
                    inc=("sSC", 1) if t == nt - 1 else None)
            P.wait("vector", "sSC")
            for t in range(nt):
                P.do("vector", lambda E, t=t: E.reduce_sum(
                    stat[:, 4 + t:5 + t], sq[:, t * D:(t + 1) * D], axis=AX),
                    inc=("sVE", 1) if t == nt - 1 else None)
            P.wait("vector", "sVE")
            P.dov(lambda E, nt=nt: E.tensor_scalar(
                stat[:, 8:8 + nt], stat[:, 0:nt], 1.0 / D, None, op0=ALU.mult))
            P.dov(lambda E, nt=nt: E.tensor_mul(
                stat[:, 12:12 + nt], stat[:, 8:8 + nt], stat[:, 8:8 + nt]))
            P.dov(lambda E, nt=nt: E.tensor_scalar(
                stat[:, 16:16 + nt], stat[:, 4:4 + nt], 1.0 / D, None, op0=ALU.mult))
            P.dov(lambda E, nt=nt: E.tensor_sub(
                stat[:, 16:16 + nt], stat[:, 16:16 + nt], stat[:, 12:12 + nt]))
            P.do("vector", lambda E, nt=nt: E.tensor_scalar(
                stat[:, 16:16 + nt], stat[:, 16:16 + nt], 1.0, 1e-5, op0=ALU.mult,
                op1=ALU.add), inc=("sVE", 1))
            P.wait("scalar", "sVE")
            P.do("scalar", lambda E, nt=nt: E.activation(
                stat[:, 20:20 + nt], stat[:, 16:16 + nt], AF.Sqrt), inc=("sSC", 1))
            P.wait("vector", "sSC")
            P.dov(lambda E, nt=nt: E.reciprocal(
                stat[:, 24:24 + nt], stat[:, 20:20 + nt]))
            P.dov(lambda E, nt=nt: E.tensor_mul(
                stat[:, 28:28 + nt], stat[:, 8:8 + nt], stat[:, 24:24 + nt]))
            P.dov(lambda E, nt=nt: E.tensor_scalar(
                stat[:, 28:28 + nt], stat[:, 28:28 + nt], -1.0, None, op0=ALU.mult))
            for t in range(nt):
                P.do("vector", lambda E, t=t: E.tensor_scalar(
                    lnc[:, t * D:(t + 1) * D], xc[:, t * D:(t + 1) * D],
                    stat[:, 24 + t:25 + t], stat[:, 28 + t:29 + t],
                    op0=ALU.mult, op1=ALU.add), inc=("sVE", 1))

        def emit_tr_chunk(src, dst, nt):
            C = nt * 128
            P.wait("tensor", "sSC")
            P.wait("tensor", "sVE")
            for d16 in range(16):
                bank = pst if d16 % 2 == 0 else pst2
                P.wait("tensor", "sVE")
                for t in range(nt):
                    P.do("tensor", lambda E, d16=d16, t=t, bank=bank: E.transpose(
                        bank[:, t * 128:(t + 1) * 128],
                        src[:, t * D + d16 * 128:t * D + d16 * 128 + 128], ident),
                        inc=("sMM", 1) if t == nt - 1 else None)
                P.wait("vector", "sMM")
                P.do("vector", lambda E, d16=d16, C=C, bank=bank: E.tensor_copy(
                    dst[:, d16 * CH:d16 * CH + C], bank[:, 0:C]), inc=("sVE", 1))
            P.wait("tensor", "sVE")

        def load_w16(wdst, wsrc):
            P.wait("sync", "sGP")
            P.do("sync", lambda E, wdst=wdst, wsrc=wsrc: E.dma_start(
                out=r_sb(wdst, 16), in_=r_in(wsrc, 16)), inc=("sW", 16))
            P.wait("tensor", "sW")

        def wt(wsb_, k16):
            return wsb_[:, k16 * D:(k16 + 1) * D]

        def ln_pass(x_src, lnT_dst, ntok):
            arena.reset()
            xc = [arena.take(4 * D, F32), arena.take(4 * D, F32)]
            lnc = arena.take(4 * D, BF16)
            lnT = arena.take(16 * CH, BF16)
            sq = arena.take(4 * D, F32)
            free_at = [0, 0]
            for i, (t0, nt) in enumerate(chunks_of(ntok)):
                buf = xc[i % 2]
                sd = "sD0" if i % 2 == 0 else "sD1"
                P.wait("sync", "sVE", free_at[i % 2])
                P.do("sync", lambda E, t0=t0, nt=nt, buf=buf: E.dma_start(
                    out=r_sb(buf, 4)[:, 0:nt, :],
                    in_=r_in(x_src[t0:t0 + nt * 128, :], nt)), inc=(sd, 16))
                P.wait("vector", sd)
                P.wait("scalar", sd)
                emit_ln_chunk(buf, lnc, sq, nt)
                free_at[i % 2] = P.count("sVE")
                emit_tr_chunk(lnc, lnT, nt)
                P.wait("sync", "sVE")
                P.wait("sync", "sST")
                P.do("sync", lambda E, t0=t0, nt=nt: E.dma_start(
                    out=r_in(lnT_dst[:, t0:t0 + nt * 128], 16),
                    in_=r_sb(lnT, 16)[:, :, 0:nt * 128]), inc=("sST", 16))
                P.wait("vector", "sST")
            P.barrier()

        def rope_tables(t0, ntk, rc, rs, rw, iw, rw2, rw3):
            n = ntk
            P.wait("vector", "sIN", sm_in2)
            P.dov(lambda E, t0=t0: E.tensor_scalar(rw[:, 0:n], ramp[:, 0:n],
                                                   1.0, float(t0), op0=ALU.mult,
                                                   op1=ALU.add))
            P.dov(lambda E: E.tensor_scalar(rw2[:, 0:n], rw[:, 0:n], invf2, None,
                                            op0=ALU.mult))
            P.dov(lambda E: E.tensor_scalar(rw[:, 0:n], rw2[:, 0:n], INV_2PI, 0.5,
                                            op0=ALU.mult, op1=ALU.add))
            P.dov(lambda E: E.tensor_copy(iw[:, 0:n], rw[:, 0:n]))
            P.dov(lambda E: E.tensor_copy(rw[:, 0:n], iw[:, 0:n]))
            P.dov(lambda E: E.tensor_scalar(rw3[:, 0:n], rw[:, 0:n], -TWO_PI_HI,
                                            None, op0=ALU.mult))
            P.dov(lambda E: E.tensor_add(rs[:, 0:n], rw2[:, 0:n], rw3[:, 0:n]))
            P.dov(lambda E: E.tensor_scalar(rw3[:, 0:n], rw[:, 0:n], -TWO_PI_LO,
                                            None, op0=ALU.mult))
            P.dov(lambda E: E.tensor_add(rs[:, 0:n], rs[:, 0:n], rw3[:, 0:n]))
            P.dov(lambda E: E.tensor_single_scalar(rw3[:, 0:n], rs[:, 0:n],
                                                   float(math.pi), op=ALU.is_gt))
            P.dov(lambda E: E.scalar_tensor_tensor(rs[:, 0:n], rw3[:, 0:n],
                                                   float(-2.0 * math.pi),
                                                   rs[:, 0:n], op0=ALU.mult,
                                                   op1=ALU.add))
            P.dov(lambda E: E.tensor_single_scalar(rw3[:, 0:n], rs[:, 0:n],
                                                   float(-math.pi), op=ALU.is_lt))
            P.do("vector", lambda E: E.scalar_tensor_tensor(
                rs[:, 0:n], rw3[:, 0:n], float(2.0 * math.pi), rs[:, 0:n],
                op0=ALU.mult, op1=ALU.add), inc=("sVE", 1))
            P.wait("scalar", "sVE")
            P.do("scalar", lambda E: E.activation(rs[:, 0:n], rs[:, 0:n], AF.Sin),
                 inc=("sSC", 1))
            P.dov(lambda E: E.tensor_scalar(rw[:, 0:n], rw2[:, 0:n], INV_2PI, 0.75,
                                            op0=ALU.mult, op1=ALU.add))
            P.dov(lambda E: E.tensor_copy(iw[:, 0:n], rw[:, 0:n]))
            P.dov(lambda E: E.tensor_copy(rw[:, 0:n], iw[:, 0:n]))
            P.dov(lambda E: E.tensor_scalar(rw3[:, 0:n], rw[:, 0:n], -TWO_PI_HI,
                                            None, op0=ALU.mult))
            P.dov(lambda E: E.tensor_add(rc[:, 0:n], rw2[:, 0:n], rw3[:, 0:n]))
            P.dov(lambda E: E.tensor_scalar(rw3[:, 0:n], rw[:, 0:n], -TWO_PI_LO,
                                            None, op0=ALU.mult))
            P.dov(lambda E: E.tensor_add(rc[:, 0:n], rc[:, 0:n], rw3[:, 0:n]))
            P.dov(lambda E: E.tensor_scalar(rc[:, 0:n], rc[:, 0:n], 1.0,
                                            float(math.pi / 2.0),
                                            op0=ALU.mult, op1=ALU.add))
            P.dov(lambda E: E.tensor_single_scalar(rw3[:, 0:n], rc[:, 0:n],
                                                   float(math.pi), op=ALU.is_gt))
            P.dov(lambda E: E.scalar_tensor_tensor(rc[:, 0:n], rw3[:, 0:n],
                                                   float(-2.0 * math.pi),
                                                   rc[:, 0:n], op0=ALU.mult,
                                                   op1=ALU.add))
            P.dov(lambda E: E.tensor_single_scalar(rw3[:, 0:n], rc[:, 0:n],
                                                   float(-math.pi), op=ALU.is_lt))
            P.do("vector", lambda E: E.scalar_tensor_tensor(
                rc[:, 0:n], rw3[:, 0:n], float(2.0 * math.pi), rc[:, 0:n],
                op0=ALU.mult, op1=ALU.add), inc=("sVE", 1))
            P.wait("scalar", "sVE")
            P.do("scalar", lambda E: E.activation(rc[:, 0:n], rc[:, 0:n], AF.Sin),
                 inc=("sSC", 1))
            P.wait("vector", "sSC")

        def projT_pass(lnT_src, wsrc, dstbuf, rope, ntok):
            arena.reset()
            wsb_ = arena.take(16 * D, BF16)
            lnT = [arena.take(16 * CH, BF16), arena.take(16 * CH, BF16)]
            outT = arena.take(16 * CH, BF16)
            rc = arena.take(CH, F32)
            rs = arena.take(CH, F32)
            rw = arena.take(CH, F32)
            rw2 = arena.take(CH, F32)
            rw3 = arena.take(CH, F32)
            iw = arena.take(CH, I32)
            qnb = arena.take(CH, BF16)
            load_w16(wsb_, wsrc)
            free_at = [0, 0]
            for i, (t0, nt) in enumerate(chunks_of(ntok)):
                ntk = nt * 128
                lt = lnT[i % 2]
                sd = "sD0" if i % 2 == 0 else "sD1"
                P.wait("sync", "sMM", free_at[i % 2])
                P.do("sync", lambda E, t0=t0, ntk=ntk, lt=lt, sd=sd: E.dma_start(
                    out=r_sb(lt, 16)[:, :, 0:ntk],
                    in_=r_in(lnT_src[:, t0:t0 + ntk], 16)), inc=(sd, 16))
                if rope:
                    rope_tables(t0, ntk, rc, rs, rw, iw, rw2, rw3)
                P.wait("tensor", sd)
                for h16 in range(16):
                    bank = psb[h16 % 2]
                    P.wait("tensor", "sVE")
                    for k16 in range(16):
                        P.do("tensor", lambda E, h16=h16, k16=k16, bank=bank, lt=lt,
                             ntk=ntk: E.matmul(
                                 bank[:, 0:ntk],
                                 wt(wsb_, k16)[:, h16 * 128:(h16 + 1) * 128],
                                 lt[:, k16 * CH:k16 * CH + ntk],
                                 start=(k16 == 0), stop=(k16 == 15)),
                             inc=("sMM", 1) if k16 == 15 else None)
                    P.wait("vector", "sMM")
                    if rope:
                        dsl = outT[:, h16 * CH:h16 * CH + ntk]
                        rb = psb[2 + h16 % 2]
                        P.wait("scalar", "sMM")
                        P.do("scalar", lambda E, bank=bank, ntk=ntk: E.copy(
                            qnb[:, 0:ntk], bank[:, 0:ntk]), inc=("sSC", 1))
                        P.wait("tensor", "sSC")
                        P.wait("tensor", "sVE")
                        P.do("tensor", lambda E, rb=rb, ntk=ntk: E.matmul(
                            rb[:, 0:ntk], pmat, qnb[:, 0:ntk], start=True,
                            stop=True), inc=("sMM", 1))
                        P.wait("vector", "sMM")
                        P.do("vector", lambda E, ntk=ntk: E.tensor_mul(
                            rw[:, 0:ntk], qnb[:, 0:ntk], rc[:, 0:ntk]))
                        P.do("vector", lambda E, rb=rb, ntk=ntk: E.tensor_mul(
                            rw2[:, 0:ntk], rb[:, 0:ntk], rs[:, 0:ntk]),
                            inc=("sVE", 1))
                        P.wait("vector", "sVE")
                        P.do("vector", lambda E, dsl=dsl, ntk=ntk: E.tensor_add(
                            dsl, rw[:, 0:ntk], rw2[:, 0:ntk]), inc=("sVE", 1))
                    else:
                        P.do("vector", lambda E, h16=h16, bank=bank, ntk=ntk:
                             E.tensor_copy(outT[:, h16 * CH:h16 * CH + ntk],
                                           bank[:, 0:ntk]), inc=("sVE", 1))
                free_at[i % 2] = P.count("sMM")
                P.wait("sync", "sVE")
                P.wait("sync", "sST")
                P.do("sync", lambda E, t0=t0, ntk=ntk: E.dma_start(
                    out=r_in(dstbuf[:, t0:t0 + ntk], 16),
                    in_=r_sb(outT, 16)[:, :, 0:ntk]), inc=("sST", 16))
                P.wait("vector", "sST")
            P.barrier()

        def projN_pass(lnT_src, wsrc, dstbuf, ntok):
            arena.reset()
            wsb_ = arena.take(16 * D, BF16)
            lnT = [arena.take(16 * CH, BF16), arena.take(16 * CH, BF16)]
            outN = arena.take(4 * D, BF16)
            load_w16(wsb_, wsrc)
            free_at = [0, 0]
            for i, (t0, nt) in enumerate(chunks_of(ntok)):
                ntk = nt * 128
                lt = lnT[i % 2]
                sd = "sD0" if i % 2 == 0 else "sD1"
                P.wait("sync", "sMM", free_at[i % 2])
                P.do("sync", lambda E, t0=t0, ntk=ntk, lt=lt, sd=sd: E.dma_start(
                    out=r_sb(lt, 16)[:, :, 0:ntk],
                    in_=r_in(lnT_src[:, t0:t0 + ntk], 16)), inc=(sd, 16))
                P.wait("tensor", sd)
                for t in range(nt):
                    for n4 in range(4):
                        bank = psb[(t * 4 + n4) % 2]
                        P.wait("tensor", "sVE")
                        for k16 in range(16):
                            P.do("tensor", lambda E, t=t, n4=n4, k16=k16, bank=bank,
                                 lt=lt: E.matmul(
                                     bank[:, 0:512],
                                     lt[:, k16 * CH + t * 128:k16 * CH + t * 128 + 128],
                                     wt(wsb_, k16)[:, n4 * 512:(n4 + 1) * 512],
                                     start=(k16 == 0), stop=(k16 == 15)),
                                 inc=("sMM", 1) if k16 == 15 else None)
                        P.wait("vector", "sMM")
                        P.do("vector", lambda E, t=t, n4=n4, bank=bank: E.tensor_copy(
                            outN[:, t * D + n4 * 512:t * D + (n4 + 1) * 512],
                            bank[:, 0:512]), inc=("sVE", 1))
                free_at[i % 2] = P.count("sMM")
                P.wait("sync", "sVE")
                P.wait("sync", "sST")
                P.do("sync", lambda E, t0=t0, nt=nt: E.dma_start(
                    out=r_in(dstbuf[t0:t0 + nt * 128, :], nt),
                    in_=r_sb(outN, 4)[:, 0:nt, :]), inc=("sST", 16))
                P.wait("vector", "sST")
            P.barrier()

        ln_pass(x0, ln1T_buf, S)
        if debug:
            P.wait("gpsimd", "sST")
            P.do("gpsimd", lambda E: E.dma_start(out=dbg["d_ln1T"], in_=ln1T_buf),
                 inc=("sGP", 16))
        P.ckpt()
        projT_pass(ln1T_buf, wbf["wqa"], qa_buf, True, S)
        P.ckpt()
        projT_pass(ln1T_buf, wbf["wka"], ka_buf, True, S)
        P.ckpt()
        projN_pass(ln1T_buf, wbf["wva"], va_buf, S)
        if debug:
            P.wait("gpsimd", "sST")
            P.do("gpsimd", lambda E: E.dma_start(out=dbg["d_qa"], in_=qa_buf),
                 inc=("sGP", 16))
            P.do("gpsimd", lambda E: E.dma_start(out=dbg["d_ka"], in_=ka_buf),
                 inc=("sGP", 16))
            P.do("gpsimd", lambda E: E.dma_start(out=dbg["d_va"], in_=va_buf),
                 inc=("sGP", 16))

        P.ckpt()
        arena.reset()
        wsb_o = arena.take(16 * D, BF16)
        qtc = arena.take(16 * CH, BF16)
        KW = WIN + CH
        ktc = arena.take(16 * KW, BF16)
        vc = arena.take(6 * D, BF16)
        otc = arena.take(16 * CH, BF16)
        xtile = arena.take(D, F32)
        pb = arena.take(4 * WIN, BF16)
        sqa = arena.take(CH + KW, BF16)
        rwa = arena.take(WIN, F32)
        denb = arena.take(2048, BF16)

        load_w16(wsb_o, wbf["woa"])
        den_copied = [0, 0]
        nm_count = [0, 0]

        def ktt(h16, c0, cn):
            return ktc[:, h16 * KW + c0:h16 * KW + c0 + cn]

        import os as _os
        _p4n = int(_os.environ.get("P4N", "99"))
        _p4h = int(_os.environ.get("P4H", "99"))
        _p4w = int(_os.environ.get("P4W", "99"))
        for ch in range(min(NCH, _p4n)):
            P.wait("sync", "sMM")
            P.do("sync", lambda E, ch=ch: E.dma_start(
                out=r_sb(qtc, 16), in_=r_in(qa_buf[:, ch * CH:(ch + 1) * CH], 16)),
                inc=("sD0", 16))
            lo = max(ch * CH - WIN, 0)
            halo = ch * CH - lo
            P.do("sync", lambda E, ch=ch, lo=lo, halo=halo: E.dma_start(
                out=r_sb(ktc, 16)[:, :, WIN - halo:KW],
                in_=r_in(ka_buf[:, lo:(ch + 1) * CH], 16)), inc=("sD0", 16))
            a0 = (WIN - halo) // 128
            P.do("sync", lambda E, ch=ch, lo=lo, a0=a0: E.dma_start(
                out=r_sb(vc, 6)[:, a0:6, :],
                in_=r_in(va_buf[lo:(ch + 1) * CH, :], 6 - a0)), inc=("sD0", 16))
            if ch == 0:
                P.do("gpsimd", lambda E: E.memset(r_sb(ktc, 16)[:, :, 0:WIN], 0.0),
                     inc=("sGC", 1))
                P.do("gpsimd", lambda E: E.memset(vc[:, 0:2 * D], 0.0),
                     inc=("sGC", 1))
                P.wait("tensor", "sGC")
                P.wait("vector", "sGC")
            P.wait("tensor", "sD0")
            P.wait("vector", "sD0")

            for h16 in range(min(HEADS, _p4h)):
                P.do("vector", lambda E, h16=h16: E.tensor_mul(
                    sqa[:, 0:CH], qtc[:, h16 * CH:(h16 + 1) * CH],
                    qtc[:, h16 * CH:(h16 + 1) * CH]))
                P.do("vector", lambda E, h16=h16: E.tensor_mul(
                    sqa[:, CH:CH + KW], ktt(h16, 0, KW), ktt(h16, 0, KW)),
                    inc=("sVE", 1))
                P.wait("tensor", "sVE")
                P.do("tensor", lambda E: E.matmul(psb[5][0:1, 0:CH], onesb,
                                                  sqa[:, 0:CH], start=True, stop=True))
                P.do("tensor", lambda E: E.matmul(psb[4][0:1, 0:512], onesb,
                                                  sqa[:, CH:CH + 512],
                                                  start=True, stop=True))
                P.do("tensor", lambda E: E.matmul(psb[3][0:1, 0:KW - 512], onesb,
                                                  sqa[:, CH + 512:CH + KW],
                                                  start=True, stop=True),
                     inc=("sMM", 1))
                P.wait("vector", "sMM")
                P.do("vector", lambda E: E.reduce_max(srow[0:1, 0:1],
                                                      psb[5][0:1, 0:CH], axis=AX))
                P.do("vector", lambda E: E.reduce_max(srow[0:1, 1:2],
                                                      psb[4][0:1, 0:512], axis=AX))
                P.dov(lambda E: E.reduce_max(srow[0:1, 3:4],
                                             psb[3][0:1, 0:KW - 512], axis=AX))
                P.dov(lambda E: E.tensor_max(srow[0:1, 1:2], srow[0:1, 1:2],
                                             srow[0:1, 3:4]))
                P.dov(lambda E: E.tensor_add(srow[0:1, 0:1], srow[0:1, 0:1],
                                             srow[0:1, 1:2]))
                P.do("vector", lambda E: E.tensor_scalar(
                    srow[0:1, 2:3], srow[0:1, 0:1], -0.5 * SCALE_A, None,
                    op0=ALU.mult), inc=("sVE", 1))
                P.wait("scalar", "sVE")
                P.do("scalar", lambda E: E.copy(srowb[0:1, 0:1], srow[0:1, 2:3]),
                     inc=("sSC", 1))
                P.wait("tensor", "sSC")
                P.do("tensor", lambda E: E.matmul(psb[5][:, 0:1], onerow,
                                                  srowb[0:1, 0:1], start=True,
                                                  stop=True), inc=("sMM", 1))
                P.wait("scalar", "sMM")
                P.do("scalar", lambda E: E.copy(stat[:, 8:9], psb[5][:, 0:1]),
                     inc=("sSC", 1))

                for w2 in range(2 if _p4w >= 2 else 0):
                    q0 = w2 * WIN
                    P.wait("tensor", "sSC")
                    P.wait("tensor", "sVE")
                    for kb in range(4):
                        bank = psb[2] if kb < 2 else psb[3]
                        P.do("tensor", lambda E, h16=h16, q0=q0, kb=kb, bank=bank:
                             E.matmul(bank[:, (kb % 2) * WIN:(kb % 2 + 1) * WIN],
                                      ktt(h16, q0 + kb * 128, 128),
                                      qtc[:, h16 * CH + q0:h16 * CH + q0 + WIN],
                                      start=True, stop=True),
                             inc=("sMM", 1) if kb == 3 else None)
                    P.wait("scalar", "sMM")
                    P.do("scalar", lambda E: E.activation(
                        pb[:, 0:2 * WIN], psb[2][:, 0:2 * WIN], AF.Exp,
                        bias=stat[:, 8:9], scale=SCALE_A))
                    P.do("scalar", lambda E: E.activation(
                        pb[:, 2 * WIN:4 * WIN], psb[3][:, 0:2 * WIN], AF.Exp,
                        bias=stat[:, 8:9], scale=SCALE_A), inc=("sSC", 1))
                    P.wait("vector", "sSC")
                    P.do("vector", lambda E: E.tensor_mul(
                        pb[:, 2 * WIN:3 * WIN], pb[:, 2 * WIN:3 * WIN],
                        maskw[:, 0:WIN]))
                    P.do("vector", lambda E: E.tensor_mul(
                        pb[:, 3 * WIN:4 * WIN], pb[:, 3 * WIN:4 * WIN],
                        maskw[:, WIN:2 * WIN]), inc=("sVE", 1))
                    if ch == 0 and w2 == 0:
                        P.do("vector", lambda E: E.memset(pb[:, 0:2 * WIN], 0.0),
                             inc=("sVE", 1))
                        P.wait("tensor", "sVE")
                    P.wait("tensor", "sVE")
                    if _p4w < 3:
                        continue
                    sbank = psb[5] if w2 == 0 else psb[6]
                    P.wait("tensor", "sVE", den_copied[w2])
                    for kb in range(4):
                        P.do("tensor", lambda E, kb=kb, sbank=sbank: E.matmul(
                            sbank[0:1, 0:WIN], onesb, pb[:, kb * WIN:(kb + 1) * WIN],
                            start=(kb == 0), stop=(kb == 3)),
                            inc=("sMM", 1) if kb == 3 else None)
                    for kb in range(4):
                        r0 = (q0 + kb * 128) // 128
                        P.do("tensor", lambda E, h16=h16, kb=kb, r0=r0: E.matmul(
                            psb[4][:, 0:WIN],
                            vc[:, r0 * D + h16 * 128:r0 * D + h16 * 128 + 128],
                            pb[:, kb * WIN:(kb + 1) * WIN],
                            start=(kb == 0), stop=(kb == 3)),
                            inc=("sMM", 1) if kb == 3 else None)
                    P.wait("vector", "sMM")
                    if _p4w < 4:
                        continue
                    P.do("vector", lambda E, h16=h16, w2=w2, sbank=sbank:
                         E.tensor_copy(
                             srow2[0:1, (h16 % 4) * 512 + w2 * WIN:
                                   (h16 % 4) * 512 + w2 * WIN + WIN],
                             sbank[0:1, 0:WIN]), inc=("sVE", 1))
                    den_copied[w2] = P.count("sVE")
                    P.do("vector", lambda E, h16=h16, q0=q0: E.tensor_copy(
                        otc[:, h16 * CH + q0:h16 * CH + q0 + WIN],
                        psb[4][:, 0:WIN]), inc=("sVE", 1))
                if _p4w >= 4 and _p4h >= 16 and h16 % 4 == 3:
                    g0 = h16 - 3
                    P.wait("vector", "sSC")
                    P.do("vector", lambda E: E.reciprocal(
                        srow[0:1, 0:2048], srow2[0:1, 0:2048]), inc=("sVE", 1))
                    P.wait("scalar", "sVE")
                    P.wait("scalar", "sMM")
                    P.do("scalar", lambda E: E.copy(denb[0:1, 0:2048],
                                                    srow[0:1, 0:2048]),
                         inc=("sSC", 1))
                    P.wait("tensor", "sSC")
                    for hl in range(4):
                        bank = psb[2 + hl % 2]
                        if hl >= 2:
                            P.wait("tensor", "sVE", nm_count[hl % 2])
                        P.do("tensor", lambda E, hl=hl, bank=bank: E.matmul(
                            bank[:, 0:512], onerow,
                            denb[0:1, hl * 512:(hl + 1) * 512],
                            start=True, stop=True), inc=("sMM", 1))
                        P.wait("vector", "sMM")
                        P.do("vector", lambda E, h=g0 + hl, bank=bank: E.tensor_mul(
                            otc[:, h * CH:(h + 1) * CH],
                            otc[:, h * CH:(h + 1) * CH],
                            bank[:, 0:512]), inc=("sVE", 1))
                        nm_count[hl % 2] = P.count("sVE")

            for t4 in range(4 if _p4w >= 5 else 0):
                P.wait("sync", "sST")
                P.wait("sync", "sVE")
                P.do("sync", lambda E, ch=ch, t4=t4: E.dma_start(
                    out=xtile,
                    in_=x0[ch * CH + t4 * 128:ch * CH + (t4 + 1) * 128, :]),
                    inc=("sXT", 16))
                P.wait("vector", "sXT")
                for n4 in range(4 if _p4w >= 6 else 0):
                    bank = psb[n4 % 2]
                    P.wait("tensor", "sVE")
                    for k16 in range(16):
                        P.do("tensor", lambda E, t4=t4, n4=n4, k16=k16, bank=bank:
                             E.matmul(bank[:, 0:512],
                                      otc[:, k16 * CH + t4 * 128:
                                          k16 * CH + t4 * 128 + 128],
                                      wt(wsb_o, k16)[:, n4 * 512:(n4 + 1) * 512],
                                      start=(k16 == 0), stop=(k16 == 15)),
                             inc=("sMM", 1) if k16 == 15 else None)
                    P.wait("vector", "sMM")
                    _p4x = _os.environ.get("P4X", "add")
                    if _p4x == "add":
                        P.do("vector", lambda E, n4=n4, bank=bank: E.tensor_add(
                            xtile[:, n4 * 512:(n4 + 1) * 512],
                            xtile[:, n4 * 512:(n4 + 1) * 512], bank[:, 0:512]),
                            inc=("sVE", 1))
                    elif _p4x == "cp":
                        P.do("vector", lambda E, n4=n4, bank=bank: E.tensor_copy(
                            xtile[:, n4 * 512:(n4 + 1) * 512], bank[:, 0:512]),
                            inc=("sVE", 1))
                    else:
                        P.do("vector", lambda E: E.engine_nop(), inc=("sVE", 1))
                if _p4w < 7:
                    continue
                P.wait("sync", "sVE")
                P.do("sync", lambda E, ch=ch, t4=t4: E.dma_start(
                    out=x1_buf[ch * CH + t4 * 128:ch * CH + (t4 + 1) * 128, :],
                    in_=xtile), inc=("sST", 16))
        P.barrier()
        if debug:
            P.wait("gpsimd", "sST")
            P.do("gpsimd", lambda E: E.dma_start(out=dbg["d_x1"], in_=x1_buf),
                 inc=("sGP", 16))

        P.ckpt()
        def ffn_pass(lnT_src, x_src, x_dst, ntok):
            arena.reset()
            wsl2 = [arena.take(16 * 512, BF16), arena.take(16 * 512, BF16)]
            lnT = [arena.take(16 * CH, BF16), arena.take(16 * CH, BF16)]
            hid = arena.take(64 * CH, BF16)
            xc = arena.take(4 * D, F32)
            ac = arena.take(512, F32)
            free_at = [0, 0]
            wsl_free = [0, 0]
            wsl_done = [0, 0]
            for i, (t0, nt) in enumerate(chunks_of(ntok)):
                ntk = nt * 128
                lt = lnT[i % 2]
                sd = "sD0" if i % 2 == 0 else "sD1"
                P.wait("sync", "sMM", free_at[i % 2])
                P.do("sync", lambda E, t0=t0, ntk=ntk, lt=lt, sd=sd: E.dma_start(
                    out=r_sb(lt, 16)[:, :, 0:ntk],
                    in_=r_in(lnT_src[:, t0:t0 + ntk], 16)), inc=(sd, 16))
                P.wait("sync", "sST")
                P.do("sync", lambda E, t0=t0, nt=nt: E.dma_start(
                    out=r_sb(xc, 4)[:, 0:nt, :],
                    in_=r_in(x_src[t0:t0 + nt * 128, :], nt)), inc=("sXT", 16))
                xc_in = P.count("sXT")
                P.wait("tensor", sd)
                P.wait("tensor", "sVE")

                def up_dma(s):
                    bi = s % 2
                    P.wait("sync", "sMM", wsl_free[bi])
                    P.do("sync", lambda E, s=s, b=wsl2[bi]: E.dma_start(
                        out=b.rearrange("p (a c) -> p a c", a=16),
                        in_=r_in(upT[:, s * 512:(s + 1) * 512], 16)),
                        inc=("sW", 16))
                    wsl_done[bi] = P.count("sW")
                up_dma(0)
                for s in range(16):
                    bi = s % 2
                    if s + 1 < 16:
                        up_dma(s + 1)
                    P.wait("tensor", "sW", wsl_done[bi])
                    for h4 in range(4):
                        hg = s * 4 + h4
                        bank = psb[h4 % 2]
                        P.wait("tensor", "sSC")
                        for k16 in range(16):
                            P.do("tensor", lambda E, h4=h4, k16=k16, bank=bank,
                                 lt=lt, ntk=ntk, bi=bi: E.matmul(
                                     bank[:, 0:ntk],
                                     wsl2[bi][:, k16 * 512 + h4 * 128:
                                              k16 * 512 + (h4 + 1) * 128],
                                     lt[:, k16 * CH:k16 * CH + ntk],
                                     start=(k16 == 0), stop=(k16 == 15)),
                                 inc=("sMM", 1) if k16 == 15 else None)
                        P.wait("scalar", "sMM")
                        P.do("scalar", lambda E, hg=hg, bank=bank, ntk=ntk:
                             E.activation(hid[:, hg * CH:hg * CH + ntk],
                                          bank[:, 0:ntk], AF.Relu,
                                          scale=upsc[:, hg:hg + 1]), inc=("sSC", 1))
                    wsl_free[bi] = P.count("sMM")
                P.wait("vector", "sXT", xc_in)
                for half in range(2):
                    def dn_dma(s, half=half):
                        bi = s % 2
                        P.wait("sync", "sMM", wsl_free[bi])
                        P.do("sync", lambda E, s=s, half=half, b=wsl2[bi]:
                             E.dma_start(
                                 out=b[:, 0:4 * 1024]
                                 .rearrange("p (a c) -> p a c", a=4),
                                 in_=r_in(dnT[s * 512:(s + 1) * 512,
                                              half * 1024:(half + 1) * 1024], 4)),
                             inc=("sW", 16))
                        wsl_done[bi] = P.count("sW")
                    dn_dma(0)
                    for s in range(16):
                        bi = s % 2
                        if s + 1 < 16:
                            dn_dma(s + 1)
                        P.wait("tensor", "sW", wsl_done[bi])
                        P.wait("tensor", "sSC")
                        if s == 0:
                            P.wait("tensor", "sVE")
                        for t in range(nt):
                            for n2 in range(2):
                                bank = psb[t * 2 + n2]
                                for k4 in range(4):
                                    hg = s * 4 + k4
                                    P.do("tensor", lambda E, t=t, n2=n2, k4=k4,
                                         hg=hg, bank=bank, s=s, bi=bi: E.matmul(
                                             bank[:, 0:512],
                                             hid[:, hg * CH + t * 128:
                                                 hg * CH + t * 128 + 128],
                                             wsl2[bi][:, k4 * 1024 + n2 * 512:
                                                 k4 * 1024 + (n2 + 1) * 512],
                                             start=(s == 0 and k4 == 0),
                                             stop=(s == 15 and k4 == 3)),
                                         inc=("sMM", 1)
                                         if (k4 == 3) else None)
                        wsl_free[bi] = P.count("sMM")
                    P.wait("vector", "sMM")
                    for t in range(nt):
                        for n2 in range(2):
                            bank = psb[t * 2 + n2]
                            d0 = half * 1024 + n2 * 512
                            P.wait("vector", "sVE")
                            P.dov(lambda E, bank=bank, d0=d0: E.tensor_mul(
                                ac, bank[:, 0:512], dsrow[:, d0:d0 + 512]))
                            P.do("vector", lambda E, t=t, d0=d0: E.tensor_add(
                                xc[:, t * D + d0:t * D + d0 + 512],
                                xc[:, t * D + d0:t * D + d0 + 512], ac),
                                inc=("sVE", 1))
                free_at[i % 2] = P.count("sMM")
                P.wait("sync", "sVE")
                P.do("sync", lambda E, t0=t0, nt=nt: E.dma_start(
                    out=r_in(x_dst[t0:t0 + nt * 128, :], nt),
                    in_=r_sb(xc, 4)[:, 0:nt, :]), inc=("sST", 16))
            P.barrier()

        ln_pass(x1_buf, ln2T_buf, S)
        ffn_pass(ln2T_buf, x1_buf, x2_buf, S)
        if debug:
            P.wait("gpsimd", "sST")
            P.do("gpsimd", lambda E: E.dma_start(out=dbg["d_x2"], in_=x2_buf),
                 inc=("sGP", 16))

        P.ckpt()
        ln3T_buf = ln1T_buf
        ln_pass(x2_buf, ln3T_buf, S)
        projT_pass(ln3T_buf, wbf["wkb"], ktb_buf, False, S)
        projN_pass(ln3T_buf, wbf["wvb"], vb_buf, S)

        arena.reset()
        ktc2 = arena.take(16 * CH, BF16)
        sqk = arena.take(16 * CH, BF16)
        P.do("gpsimd", lambda E: E.memset(bcolk, 0.0), inc=("sGC", 1))
        P.do("gpsimd", lambda E: E.memset(bcolq, 0.0), inc=("sGC", 1))
        P.wait("vector", "sGC")
        for ch in range(NCH):
            P.wait("sync", "sVE")
            P.do("sync", lambda E, ch=ch: E.dma_start(
                out=r_sb(ktc2, 16), in_=r_in(ktb_buf[:, ch * CH:(ch + 1) * CH], 16)),
                inc=("sD0", 16))
            P.wait("vector", "sD0")
            for h16 in range(16):
                P.do("vector", lambda E, h16=h16: E.tensor_mul(
                    sqk[:, h16 * CH:(h16 + 1) * CH],
                    ktc2[:, h16 * CH:(h16 + 1) * CH],
                    ktc2[:, h16 * CH:(h16 + 1) * CH]),
                    inc=("sVE", 1) if h16 == 15 else None)
            P.wait("tensor", "sVE")
            for h16 in range(16):
                bank = psb[h16 % 2]
                bank2 = psb[2 + h16 % 2]
                P.do("tensor", lambda E, h16=h16, bank=bank: E.matmul(
                    bank[0:1, 0:CH], onesb[0:64, :],
                    sqk[0:64, h16 * CH:(h16 + 1) * CH],
                    start=True, stop=True))
                P.do("tensor", lambda E, h16=h16, bank2=bank2: E.matmul(
                    bank2[0:1, 0:CH], onesb[64:128, :],
                    sqk[64:128, h16 * CH:(h16 + 1) * CH],
                    start=True, stop=True), inc=("sMM", 1))
                P.wait("vector", "sMM")
                P.wait("vector", "sVE")
                P.do("vector", lambda E, h16=h16, bank=bank: E.reduce_max(
                    srow[0:1, 4:5], bank[0:1, 0:CH], axis=AX))
                P.dov(lambda E, h16=h16, bank2=bank2: E.reduce_max(
                    srow[0:1, 5:6], bank2[0:1, 0:CH], axis=AX))
                P.dov(lambda E, h16=h16: E.tensor_max(
                    bcolk[0:1, 2 * h16:2 * h16 + 1],
                    bcolk[0:1, 2 * h16:2 * h16 + 1], srow[0:1, 4:5]))
                P.do("vector", lambda E, h16=h16: E.tensor_max(
                    bcolk[0:1, 2 * h16 + 1:2 * h16 + 2],
                    bcolk[0:1, 2 * h16 + 1:2 * h16 + 2], srow[0:1, 5:6]),
                    inc=("sVE", 1))
                if h16 % 2 == 1:
                    P.wait("tensor", "sVE")
        P.barrier()
        if debug:
            P.wait("gpsimd", "sST")
            P.do("gpsimd", lambda E: E.dma_start(out=dbg["d_ktb"], in_=ktb_buf),
                 inc=("sGP", 16))
            P.do("gpsimd", lambda E: E.dma_start(out=dbg["d_vb"], in_=vb_buf),
                 inc=("sGP", 16))
            P.wait("gpsimd", "sGP")

        P.ckpt()
        def gather_switch(E):
            v = bass.Bass.values_load(nc, core_sb, engines=[mybir.EngineType.Pool],
                                      skip_runtime_bounds_check=True,
                                      min_val=0, max_val=7)
            reg = E.snap(v)
            for j in nc.Switch(engines=[E], index=[reg], n=NC8):
                for mm_ in range(M):
                    T = NC8 * mm_ + j
                    E.dma_start(out=x2own_buf[mm_ * 128:(mm_ + 1) * 128, :],
                                in_=x2_buf[T * 128:(T + 1) * 128, :]
                                ).then_inc(P.sem("sGP"), 16)

        if not P.frozen:
            P.sems["sGP"][1] += 16 * M
            P.emit("gpsimd", gather_switch)
        P.barrier()
        if debug:
            P.wait("gpsimd", "sST")
            P.do("gpsimd", lambda E: E.dma_start(out=dbg["d_x2own"], in_=x2own_buf),
                 inc=("sGP", 16))

        ln_pass(x2own_buf, ln3oT_buf, OWN)

        arena.reset()
        wsb_q = arena.take(16 * D, BF16)
        qto = arena.take(16 * OWN, BF16)
        lnTq = arena.take(16 * CH, BF16)
        load_w16(wsb_q, wbf["wqb"])
        for i, (t0, nt) in enumerate(chunks_of(OWN)):
            ntk = nt * 128
            P.wait("sync", "sMM")
            P.do("sync", lambda E, t0=t0, ntk=ntk: E.dma_start(
                out=r_sb(lnTq, 16)[:, :, 0:ntk],
                in_=r_in(ln3oT_buf[:, t0:t0 + ntk], 16)), inc=("sD0", 16))
            P.wait("tensor", "sD0")
            for h16 in range(16):
                bank = psb[h16 % 2]
                P.wait("tensor", "sVE")
                for k16 in range(16):
                    P.do("tensor", lambda E, h16=h16, k16=k16, bank=bank, ntk=ntk:
                         E.matmul(bank[:, 0:ntk],
                                  wt(wsb_q, k16)[:, h16 * 128:(h16 + 1) * 128],
                                  lnTq[:, k16 * CH:k16 * CH + ntk],
                                  start=(k16 == 0), stop=(k16 == 15)),
                         inc=("sMM", 1) if k16 == 15 else None)
                P.wait("vector", "sMM")
                P.do("vector", lambda E, h16=h16, t0=t0, ntk=ntk, bank=bank:
                     E.tensor_copy(qto[:, h16 * OWN + t0:h16 * OWN + t0 + ntk],
                                   bank[:, 0:ntk]), inc=("sVE", 1))
        sqq = lnTq
        P.wait("vector", "sVE")
        for i, (t0, nt) in enumerate(chunks_of(OWN)):
            ntk = nt * 128
            for h16 in range(16):
                P.do("vector", lambda E, h16=h16, t0=t0, ntk=ntk: E.tensor_mul(
                    sqq[:, h16 * CH:h16 * CH + ntk],
                    qto[:, h16 * OWN + t0:h16 * OWN + t0 + ntk],
                    qto[:, h16 * OWN + t0:h16 * OWN + t0 + ntk]),
                    inc=("sVE", 1) if h16 == 15 else None)
            P.wait("tensor", "sVE")
            for h16 in range(16):
                bank = psb[h16 % 2]
                bank2 = psb[2 + h16 % 2]
                P.do("tensor", lambda E, h16=h16, ntk=ntk, bank=bank: E.matmul(
                    bank[0:1, 0:ntk], onesb[0:64, :],
                    sqq[0:64, h16 * CH:h16 * CH + ntk],
                    start=True, stop=True))
                P.do("tensor", lambda E, h16=h16, ntk=ntk, bank2=bank2: E.matmul(
                    bank2[0:1, 0:ntk], onesb[64:128, :],
                    sqq[64:128, h16 * CH:h16 * CH + ntk],
                    start=True, stop=True), inc=("sMM", 1))
                P.wait("vector", "sMM")
                P.wait("vector", "sVE")
                P.do("vector", lambda E, ntk=ntk, bank=bank: E.reduce_max(
                    srow[0:1, 4:5], bank[0:1, 0:ntk], axis=AX))
                P.dov(lambda E, ntk=ntk, bank2=bank2: E.reduce_max(
                    srow[0:1, 5:6], bank2[0:1, 0:ntk], axis=AX))
                P.dov(lambda E, h16=h16: E.tensor_max(
                    bcolq[0:1, 2 * h16:2 * h16 + 1],
                    bcolq[0:1, 2 * h16:2 * h16 + 1], srow[0:1, 4:5]))
                P.do("vector", lambda E, h16=h16: E.tensor_max(
                    bcolq[0:1, 2 * h16 + 1:2 * h16 + 2],
                    bcolq[0:1, 2 * h16 + 1:2 * h16 + 2], srow[0:1, 5:6]),
                    inc=("sVE", 1))
                if h16 % 2 == 1:
                    P.wait("tensor", "sVE")
        P.wait("vector", "sVE")
        P.dov(lambda E: E.tensor_add(srow[0:1, 8:40], bcolq[0:1, 0:32],
                                     bcolk[0:1, 0:32]))
        P.do("vector", lambda E: E.tensor_scalar(
            srow[0:1, 8:40], srow[0:1, 8:40], -0.5 * SCALE_B, None, op0=ALU.mult),
            inc=("sVE", 1))
        P.wait("scalar", "sVE")
        P.do("scalar", lambda E: E.copy(srowb[0:1, 0:32], srow[0:1, 8:40]),
             inc=("sSC", 1))
        P.wait("tensor", "sSC")
        P.do("tensor", lambda E: E.matmul(psb[2][:, 0:32], onerow, srowb[0:1, 0:32],
                                          start=True, stop=True), inc=("sMM", 1))
        P.wait("scalar", "sMM")
        P.do("scalar", lambda E: E.copy(bcols, psb[2][:, 0:32]), inc=("sSC", 1))
        P.wait("tensor", "sSC")
        P.wait("vector", "sSC")

        P.ckpt()
        ktk = [arena.take(4 * 512, BF16), arena.take(4 * 512, BF16)]
        vtk = [arena.take(4 * 512, BF16), arena.take(4 * 512, BF16)]
        pb2 = arena.take(512, BF16)
        pb3 = arena.take(512, BF16)
        otall = arena.take(16 * OWN, BF16)
        rr = arena.take(256, F32)
        rwa2 = arena.take(128, F32)
        rwa3 = arena.take(128, F32)

        kv_free = [0, 0]
        for m in range(M):
            for hg in range(4):
                P.wait("tensor", "sVE")
                P.wait("tensor", "sSC")
                for b in range(2, 8):
                    P.do("tensor", lambda E, b=b: E.matmul(
                        psb[b][:, 0:512], onerow, zrow, start=True, stop=False),
                        inc=("sMM", 1) if b == 7 else None)
                nkc = m + 1
                for ikc in range(nkc):
                    kc = m - ikc
                    for hf in range(2):
                        it = ikc * 2 + hf
                        bi = it % 2
                        kt = ktk[bi]
                        vt = vtk[bi]
                        skv = "sKV0" if bi == 0 else "sKV1"
                        P.do("sync", lambda E, v=kv_free[bi]: E.wait_ge(
                            P.sem("sMM"), v))
                        P.do("sync", lambda E, hg=hg, kc=kc, hf=hf, kt=kt:
                             E.dma_start(
                                 out=r_sb(kt, 4),
                                 in_=r_in(ktb_buf[hg * 512:(hg + 1) * 512,
                                                  kc * 1024 + hf * 512:
                                                  kc * 1024 + (hf + 1) * 512], 4)),
                             inc=(skv, 16))
                        P.do("sync", lambda E, hg=hg, kc=kc, hf=hf, vt=vt:
                             E.dma_start(
                                 out=r_sb(vt, 4),
                                 in_=r_in(vb_buf[kc * 1024 + hf * 512:
                                                 kc * 1024 + (hf + 1) * 512,
                                                 hg * 512:(hg + 1) * 512], 4)),
                             inc=(skv, 16))
                        P.wait("tensor", skv)
                        last_it = (ikc == nkc - 1 and hf == 1)
                        for hl in range(4):
                            h = hg * 4 + hl
                            P.wait("tensor", "sSC")
                            P.wait("tensor", "sVE")
                            for kb in range(4):
                                for st_ in range(2):
                                    P.do("tensor", lambda E, hl=hl, kb=kb, st_=st_,
                                         kt=kt, m=m, h=h: E.matmul(
                                             psb[st_][:, kb * 128:(kb + 1) * 128],
                                             kt[st_ * 64:(st_ + 1) * 64,
                                                hl * 512 + kb * 128:
                                                hl * 512 + (kb + 1) * 128],
                                             qto[st_ * 64:(st_ + 1) * 64,
                                                 h * OWN + m * 128:
                                                 h * OWN + (m + 1) * 128],
                                             start=True, stop=True,
                                             tile_position=(st_ * 64, 0)),
                                         inc=("sMM", 1)
                                         if (kb == 3 and st_ == 1) else None)
                            P.wait("scalar", "sMM")
                            P.do("scalar", lambda E, h=h: E.activation(
                                pb2, psb[0][:, 0:512], AF.Exp,
                                bias=bcols[:, 2 * h:2 * h + 1], scale=SCALE_B))
                            P.do("scalar", lambda E, h=h: E.activation(
                                pb3, psb[1][:, 0:512], AF.Exp,
                                bias=bcols[:, 2 * h + 1:2 * h + 2], scale=SCALE_B),
                                inc=("sSC", 1))
                            if kc == m:
                                P.wait("vector", "sSC")
                                P.do("vector", lambda E, hf=hf: E.tensor_mul(
                                    pb2, pb2, dmsk[:, hf * 512:(hf + 1) * 512]))
                                P.do("vector", lambda E, hf=hf: E.tensor_mul(
                                    pb3, pb3, dmsk[:, hf * 512:(hf + 1) * 512]),
                                    inc=("sVE", 1))
                                P.wait("tensor", "sVE")
                            else:
                                P.wait("tensor", "sSC")
                            P.do("tensor", lambda E, hl=hl: E.matmul(
                                psb[4 + hl][0:1, 0:512], onesb, pb2,
                                start=False, stop=False))
                            P.do("tensor", lambda E, hl=hl: E.matmul(
                                psb[4 + hl][32:33, 0:512], onesb, pb3,
                                start=False, stop=False, tile_position=(0, 32)))
                            ob = psb[2 + hl // 2]
                            oc = (hl % 2) * 256
                            for kb in range(4):
                                P.do("tensor", lambda E, hl=hl, kb=kb, vt=vt, ob=ob,
                                     oc=oc: E.matmul(
                                         ob[:, oc:oc + 128],
                                         vt[:, kb * 512 + hl * 128:
                                            kb * 512 + hl * 128 + 128],
                                         pb2[:, kb * 128:(kb + 1) * 128],
                                         start=False, stop=False))
                                P.do("tensor", lambda E, hl=hl, kb=kb, vt=vt, ob=ob,
                                     oc=oc: E.matmul(
                                         ob[:, oc + 128:oc + 256],
                                         vt[:, kb * 512 + hl * 128:
                                            kb * 512 + hl * 128 + 128],
                                         pb3[:, kb * 128:(kb + 1) * 128],
                                         start=False, stop=False),
                                     inc=("sMM", 1) if kb == 3 else None)
                        kv_free[bi] = P.count("sMM")
                for b in range(2, 8):
                    P.do("tensor", lambda E, b=b: E.matmul(
                        psb[b][:, 0:512], onerow, zrow, start=False, stop=True),
                        inc=("sMM", 1) if b == 7 else None)
                for hl in range(4):
                    h = hg * 4 + hl
                    P.wait("vector", "sMM")
                    P.do("vector", lambda E, hl=hl: E.tensor_copy(
                        srow[0:1, 512:1024], psb[4 + hl][0:1, 0:512]))
                    P.dov(lambda E, hl=hl: E.tensor_copy(
                        srow2[0:1, 512:1024], psb[4 + hl][32:33, 0:512]))
                    P.do("vector", lambda E: E.tensor_add(
                        srow[0:1, 0:128], srow[0:1, 512:640], srow[0:1, 640:768]))
                    P.dov(lambda E: E.tensor_add(
                        srow[0:1, 128:256], srow[0:1, 768:896],
                        srow[0:1, 896:1024]))
                    P.dov(lambda E: E.tensor_add(
                        srow[0:1, 0:128], srow[0:1, 0:128], srow[0:1, 128:256]))
                    P.do("vector", lambda E: E.tensor_add(
                        srow2[0:1, 0:128], srow2[0:1, 512:640], srow2[0:1, 640:768]))
                    P.dov(lambda E: E.tensor_add(
                        srow2[0:1, 128:256], srow2[0:1, 768:896],
                        srow2[0:1, 896:1024]))
                    P.dov(lambda E: E.tensor_add(
                        srow[0:1, 128:256], srow2[0:1, 0:128],
                        srow2[0:1, 128:256]))
                    P.dov(lambda E: E.reciprocal(srow2[0:1, 0:256],
                                                 srow[0:1, 0:256]))
                    P.do("vector", lambda E, h=h: E.tensor_scalar(
                        srow2[0:1, 128:256], srow2[0:1, 128:256],
                        lams[0:1, h:h + 1], None, op0=ALU.mult), inc=("sVE", 1))
                    P.wait("scalar", "sVE")
                    P.do("scalar", lambda E: E.copy(srowb[0:1, 0:256],
                                                    srow2[0:1, 0:256]),
                         inc=("sSC", 1))
                    P.wait("tensor", "sSC")
                    P.do("tensor", lambda E: E.matmul(
                        psb[0][:, 0:256], onerow, srowb[0:1, 0:256],
                        start=True, stop=True), inc=("sMM", 1))
                    P.wait("vector", "sMM")
                    P.dov(lambda E: E.tensor_copy(rr, psb[0][:, 0:256]))
                    ob = psb[2 + hl // 2]
                    oc = (hl % 2) * 256
                    P.do("vector", lambda E, ob=ob, oc=oc: E.tensor_mul(
                        rwa2, ob[:, oc:oc + 128], rr[:, 0:128]))
                    P.do("vector", lambda E, ob=ob, oc=oc: E.tensor_mul(
                        rwa3, ob[:, oc + 128:oc + 256], rr[:, 128:256]),
                        inc=("sVE", 1))
                    P.wait("vector", "sVE")
                    P.do("vector", lambda E, h=h, m=m: E.tensor_sub(
                        otall[:, h * OWN + m * 128:h * OWN + (m + 1) * 128],
                        rwa2, rwa3), inc=("sVE", 1))
                    P.wait("tensor", "sVE")
        P.barrier()

        P.ckpt()
        wsb_ob = wsb_q
        xt2 = lnTq.bitcast(F32)
        load_w16(wsb_ob, wbf["wob"])
        for m in range(M):
            P.wait("sync", "sST")
            P.wait("sync", "sVE")
            P.do("sync", lambda E, m=m: E.dma_start(
                out=xt2[:, 0:D], in_=x2own_buf[m * 128:(m + 1) * 128, :]),
                inc=("sXT", 16))
            P.wait("vector", "sXT")
            P.wait("tensor", "sXT")
            for n4 in range(4):
                bank = psb[n4 % 2]
                P.wait("tensor", "sVE")
                for k16 in range(16):
                    P.do("tensor", lambda E, m=m, n4=n4, k16=k16, bank=bank:
                         E.matmul(bank[:, 0:512],
                                  otall[:, k16 * OWN + m * 128:
                                        k16 * OWN + m * 128 + 128],
                                  wt(wsb_ob, k16)[:, n4 * 512:(n4 + 1) * 512],
                                  start=(k16 == 0), stop=(k16 == 15)),
                         inc=("sMM", 1) if k16 == 15 else None)
                P.wait("vector", "sMM")
                P.do("vector", lambda E, n4=n4, bank=bank: E.tensor_add(
                    xt2[:, n4 * 512:(n4 + 1) * 512],
                    xt2[:, n4 * 512:(n4 + 1) * 512], bank[:, 0:512]),
                    inc=("sVE", 1))
            P.wait("sync", "sVE")
            P.do("sync", lambda E, m=m: E.dma_start(
                out=x3own_buf[m * 128:(m + 1) * 128, :], in_=xt2[:, 0:D]),
                inc=("sST", 16))
        P.barrier()
        if debug:
            P.wait("gpsimd", "sST")
            P.do("gpsimd", lambda E: E.dma_start(out=dbg["d_x3own"], in_=x3own_buf),
                 inc=("sGP", 16))

        P.ckpt()
        ln_pass(x3own_buf, ln4T_buf, OWN)
        ffn_pass(ln4T_buf, x3own_buf, x4own_buf, OWN)
        P.do("gpsimd", lambda E: E.dma_start(out=y, in_=x4own_buf), inc=("sGP", 16))
        P.wait("gpsimd", "sGP")
        P.barrier()

        @blk.sync
        def _(E):
            for f in P.ops["sync"]:
                f(E)

        @blk.scalar
        def _(E):
            for f in P.ops["scalar"]:
                f(E)

        @blk.vector
        def _(E):
            for f in P.ops["vector"]:
                f(E)

        @blk.tensor
        def _(E):
            for f in P.ops["tensor"]:
                f(E)

        @blk.gpsimd
        def _(E):
            for f in P.ops["gpsimd"]:
                f(E)

    return nc


def make_dmask(j):
    import ml_dtypes
    kb = np.arange(8)[:, None, None]
    p = np.arange(128)[None, :, None]
    w = np.arange(128)[None, None, :]
    m = (kb * 128 + p <= 128 * j + w).astype(np.float32)
    return np.ascontiguousarray(m.reshape(8 * 128, 128).astype(ml_dtypes.bfloat16))


def _prep_inputs(kw):
    x = np.ascontiguousarray(np.asarray(kw["L0_attn_a_hidden_states"],
                                        np.float32).reshape(-1, D))
    base = {
        "x0": x,
        "wqa": np.ascontiguousarray(np.asarray(kw["L0_attn_a_Wq"], np.float32)),
        "wka": np.ascontiguousarray(np.asarray(kw["L0_attn_a_Wk"], np.float32)),
        "wva": np.ascontiguousarray(np.asarray(kw["L0_attn_a_Wv"], np.float32)),
        "woa": np.ascontiguousarray(np.asarray(kw["L0_attn_a_Wo"], np.float32)),
        "wqb": np.ascontiguousarray(np.asarray(kw["L0_attn_b_Wq"], np.float32)),
        "wkb": np.ascontiguousarray(np.asarray(kw["L0_attn_b_Wk"], np.float32)),
        "wvb": np.ascontiguousarray(np.asarray(kw["L0_attn_b_Wv"], np.float32)),
        "wob": np.ascontiguousarray(np.asarray(kw["L0_attn_b_Wo"], np.float32)),
        "lam0": np.ascontiguousarray(np.asarray(kw["L0_attn_b_lambda_init"],
                                                np.float32).reshape(1, HEADS)),
        "wup": np.ascontiguousarray(np.asarray(kw["L0_ffn_up_weight"], np.float32)),
        "wdn": np.ascontiguousarray(np.asarray(kw["L0_ffn_down_weight"],
                                               np.float32)),
        "scup": np.ascontiguousarray(np.asarray(kw["L0_ffn_up_scale"], np.float32)
                                     .reshape(FFN_H // 128, 128).T),
        "scdn": np.ascontiguousarray(np.asarray(kw["L0_ffn_down_scale"],
                                                np.float32).reshape(1, D)),
        "scout": np.ascontiguousarray(np.asarray(kw["L0_ffn_output_scale"],
                                                 np.float32).reshape(1, 1)),
    }
    return x.shape[0], base


def kernel(**kw):
    S, base = _prep_inputs(kw)
    M = S // 128 // NC8
    nc = build_program(S, debug=False)
    in_maps = []
    for j in range(NC8):
        mdict = dict(base)
        mdict["core_slot"] = np.array([[j]], np.int32)
        mdict["dmask"] = make_dmask(j)
        mdict["rampi"] = np.broadcast_to(np.arange(CH, dtype=np.float32),
                                         (128, CH)).copy()
        in_maps.append(mdict)
    res = run_bass_kernel_spmd(nc, in_maps, core_ids=list(range(NC8)))
    out = np.empty((S, D), np.float32)
    for j in range(NC8):
        yj = res.results[j]["y"]
        for mm_ in range(M):
            T = NC8 * mm_ + j
            out[T * 128:(T + 1) * 128, :] = yj[mm_ * 128:(mm_ + 1) * 128, :]
    return out.reshape(1, S, D)


# version 05 (final)
# speedup vs baseline: 1.1877x; 1.0259x over previous
import contextlib
import math
import numpy as np

import concourse.bass as bass
import concourse.mybir as mybir
from concourse.bass_utils import run_bass_kernel_spmd

F32 = mybir.dt.float32
BF16 = mybir.dt.bfloat16
I32 = mybir.dt.int32
AF = mybir.ActivationFunctionType
ALU = mybir.AluOpType
AX = mybir.AxisListType.X

D = 2048
HEADS = 16
WIN = 256
FFN_H = 8192
NC8 = 8
CH = 512

INV_2PI = float(1.0 / (2.0 * math.pi))
TWO_PI_HI = 6.28125
TWO_PI_LO = float(2.0 * math.pi - 6.28125)
LN_THETA = float(-math.log(10000.0) / 64.0)
SCALE_A = float(1.0 / math.sqrt(128.0))
SCALE_B = 0.125


class Prog:
    def __init__(self):
        self.ops = {e: [] for e in ("sync", "scalar", "vector", "tensor", "gpsimd")}
        self.sems = {}
        self.frozen = False
        self.phase = 0
        self.max_phase = 99
        self.relay = None

    def ckpt(self):
        self.phase += 1
        if self.phase >= self.max_phase:
            self.frozen = True

    def add_sem(self, name, handle):
        self.sems[name] = [handle, 0]

    def sem(self, name):
        return self.sems[name][0]

    def count(self, name):
        return self.sems[name][1]

    def emit(self, eng, fn):
        if self.frozen:
            return
        self.ops[eng].append(fn)

    def wait(self, eng, name, val=None):
        h, c = self.sems[name]
        v = c if val is None else val
        if v > 0:
            self.emit(eng, lambda E, h=h, v=v: E.wait_ge(h, v))

    def do(self, eng, fn, inc=None):
        if self.frozen:
            return
        if inc is None:
            self.emit(eng, fn)
        else:
            name, n = inc
            rec = self.sems[name]
            rec[1] += n
            h = rec[0]
            self.emit(eng, lambda E, fn=fn, h=h, n=n: fn(E).then_inc(h, n))

    def dov(self, fn):
        self.do("vector", fn, inc=("sVE", 1))
        self.wait("vector", "sVE")

    def barrier(self):
        for eng in ("sync", "scalar", "vector", "tensor"):
            for name in self.sems:
                self.wait(eng, name)
        if self.relay is not None:
            self.relay()
            for name in ("sIN", "sK", "sGP", "sW", "sWV", "sD0", "sD1", "sST",
                         "sXT", "sKV0", "sKV1"):
                self.wait("gpsimd", name)
        else:
            for name in self.sems:
                self.wait("gpsimd", name)


class Arena:
    def __init__(self, base_ap, nbytes):
        self.base = base_ap
        self.nbytes = nbytes
        self.cur = 0

    def reset(self):
        self.cur = 0

    def take(self, nfree, dtype):
        sz = mybir.dt.size(dtype)
        nb = nfree * sz
        self.cur = (self.cur + 63) // 64 * 64
        off = self.cur
        self.cur += nb
        assert self.cur <= self.nbytes, f"arena overflow: {self.cur} > {self.nbytes}"
        v = self.base[:, off // 4:(off + nb) // 4]
        if dtype == F32:
            return v
        return v.bitcast(dtype)


def chunks_of(ntok):
    out = []
    t = 0
    while t < ntok:
        n = min(4, (ntok - t) // 128)
        out.append((t, n))
        t += n * 128
    return out


def build_program(S, debug=False, max_phase=99):
    NCH = S // CH
    M = S // 128 // NC8
    OWN = M * 128
    nc = bass.Bass("TRN2", target_bir_lowering=False)

    x0 = nc.dram_tensor("x0", [S, D], F32, kind="ExternalInput").ap()
    win = {}
    for nm in ["wqa", "wka", "wva", "woa", "wqb", "wkb", "wvb", "wob"]:
        win[nm] = nc.dram_tensor(nm, [D, D], F32, kind="ExternalInput").ap()
    lam0 = nc.dram_tensor("lam0", [1, HEADS], F32, kind="ExternalInput").ap()
    wup = nc.dram_tensor("wup", [FFN_H, D], F32, kind="ExternalInput").ap()
    wdn = nc.dram_tensor("wdn", [D, FFN_H], F32, kind="ExternalInput").ap()
    scup = nc.dram_tensor("scup", [128, FFN_H // 128], F32,
                          kind="ExternalInput").ap()
    scdn = nc.dram_tensor("scdn", [1, D], F32, kind="ExternalInput").ap()
    scout = nc.dram_tensor("scout", [1, 1], F32, kind="ExternalInput").ap()
    core_slot = nc.dram_tensor("core_slot", [1, 1], I32, kind="ExternalInput").ap()
    dmask = nc.dram_tensor("dmask", [8 * 128, 128], BF16, kind="ExternalInput").ap()
    rampi = nc.dram_tensor("rampi", [128, CH], F32, kind="ExternalInput").ap()
    y = nc.dram_tensor("y", [OWN, D], F32, kind="ExternalOutput").ap()

    wbf = {}
    for nm in ["wqa", "wka", "wva", "woa", "wqb", "wkb", "wvb", "wob"]:
        wbf[nm] = nc.dram_tensor(nm + "_b", [D, D], BF16, kind="Internal").ap()
    upT = nc.dram_tensor("upT", [D, FFN_H], BF16, kind="Internal").ap()
    dnT = nc.dram_tensor("dnT", [FFN_H, D], BF16, kind="Internal").ap()
    ln1T_buf = nc.dram_tensor("ln1T_buf", [D, S], BF16, kind="Internal").ap()
    qa_buf = nc.dram_tensor("qa_buf", [D, S], BF16, kind="Internal").ap()
    ka_buf = nc.dram_tensor("ka_buf", [D, S], BF16, kind="Internal").ap()
    va_buf = nc.dram_tensor("va_buf", [S, D], BF16, kind="Internal").ap()
    x1_buf = nc.dram_tensor("x1_buf", [S, D], F32, kind="Internal").ap()
    ln2T_buf = nc.dram_tensor("ln2T_buf", [D, S], BF16, kind="Internal").ap()
    x2_buf = nc.dram_tensor("x2_buf", [S, D], F32, kind="Internal").ap()
    ktb_buf = nc.dram_tensor("ktb_buf", [D, S], BF16, kind="Internal").ap()
    vb_buf = nc.dram_tensor("vb_buf", [S, D], BF16, kind="Internal").ap()
    x2own_buf = nc.dram_tensor("x2own_buf", [OWN, D], F32, kind="Internal").ap()
    ln3oT_buf = nc.dram_tensor("ln3oT_buf", [D, OWN], BF16, kind="Internal").ap()
    x3own_buf = nc.dram_tensor("x3own_buf", [OWN, D], F32, kind="Internal").ap()
    ln4T_buf = nc.dram_tensor("ln4T_buf", [D, OWN], BF16, kind="Internal").ap()
    x4own_buf = nc.dram_tensor("x4own_buf", [OWN, D], F32, kind="Internal").ap()

    dbg = {}
    if debug:
        for nm, shp, dt in [
            ("d_ln1T", [D, S], BF16), ("d_qa", [D, S], BF16), ("d_ka", [D, S], BF16),
            ("d_va", [S, D], BF16), ("d_x1", [S, D], F32), ("d_x2", [S, D], F32),
            ("d_ktb", [D, S], BF16), ("d_vb", [S, D], BF16),
            ("d_upT", [D, FFN_H], BF16), ("d_dnT", [FFN_H, D], BF16),
            ("d_x2own", [OWN, D], F32), ("d_x3own", [OWN, D], F32),
        ]:
            dbg[nm] = nc.dram_tensor(nm, shp, dt, kind="ExternalOutput").ap()

    P = Prog()
    P.max_phase = max_phase

    with contextlib.ExitStack() as ctx:
        def sb(name, shape, dt):
            return ctx.enter_context(nc.sbuf_tensor(name, shape, dt))[:]

        ident = sb("ident", [128, 128], BF16)
        pmat = sb("pmat", [128, 128], BF16)
        onesb = sb("onesb", [128, 1], BF16)
        ones2 = sb("ones2", [128, 2], BF16)
        onerow = sb("onerow", [1, 128], BF16)
        zrow = sb("zrow", [1, 512], BF16)
        invf2 = sb("invf2", [128, 1], F32)
        dsrow = sb("dsrow", [128, D], F32)
        upsc = sb("upsc", [128, FFN_H // 128], F32)
        lams = sb("lams", [1, HEADS], F32)
        maskw = sb("maskw", [128, 2 * WIN], BF16)
        dmsk = sb("dmsk", [128, 8 * 128], BF16)
        core_sb = sb("core_sb", [1, 1], I32)
        ramp = sb("ramp", [128, CH], F32)
        stat = sb("stat", [128, 32], F32)
        srow = sb("srow", [1, 2048], F32)
        srow2 = sb("srow2", [1, 2048], F32)
        srowb = sb("srowb", [1, 2048], BF16)
        bcolq = sb("bcolq", [1, 2 * HEADS], F32)
        bcolk = sb("bcolk", [1, 2 * HEADS], F32)
        bcols = sb("bcols", [128, 2 * HEADS], F32)

        ARENA_B = 172 * 1024
        arena = Arena(sb("arena", [128, ARENA_B // 4], F32), ARENA_B)

        psb = [ctx.enter_context(nc.psum_tensor(f"ps{i}", [128, 512], F32))[:]
               for i in range(8)]
        pst = psb[7].bitcast(BF16)
        pst2 = psb[6].bitcast(BF16)

        for s_ in ["sIN", "sSC", "sVE", "sMM", "sGP", "sK", "sGC",
            "sW", "sWV", "sD0", "sD1", "sST", "sXT", "sKV0", "sKV1"]:
            P.add_sem(s_, ctx.enter_context(nc.semaphore(s_)))

        blk = ctx.enter_context(nc.Block())

        def r_in(dram2d, a):
            return dram2d.rearrange("(a p) c -> p a c", p=128)

        def r_sb(sb2d, a):
            return sb2d.rearrange("p (a c) -> p a c", a=a)

        for nm in ["wqa", "wka", "wva", "woa", "wqb", "wkb", "wvb", "wob"]:
            P.do("gpsimd", lambda E, s=win[nm], d=wbf[nm]: E.dma_start(
                out=d, in_=s, max_dma_last_dim=8192), inc=("sGP", 16))
        wcast_done = P.count("sGP")

        P.do("gpsimd", lambda E: E.memset(ident, 0.0), inc=("sGC", 1))
        P.wait("gpsimd", "sGC")
        P.do("gpsimd", lambda E: E.affine_select(
            out=ident, in_=ident, compare_op=ALU.not_equal, fill=1.0, base=0,
            pattern=[[-1, 128]], channel_multiplier=1))
        P.do("gpsimd", lambda E: E.memset(pmat, 0.0), inc=("sGC", 1))
        P.wait("gpsimd", "sGC")
        P.do("gpsimd", lambda E: E.affine_select(
            out=pmat, in_=pmat, compare_op=ALU.not_equal, fill=-1.0, base=-64,
            pattern=[[-1, 128]], channel_multiplier=1), inc=("sGC", 1))
        P.wait("gpsimd", "sGC")
        P.do("gpsimd", lambda E: E.affine_select(
            out=pmat, in_=pmat, compare_op=ALU.not_equal, fill=1.0, base=64,
            pattern=[[-1, 128]], channel_multiplier=1), inc=("sGC", 1))
        P.wait("gpsimd", "sGC")
        P.do("gpsimd", lambda E: E.memset(onesb, 1.0))
        P.do("gpsimd", lambda E: E.memset(onerow, 1.0))
        P.do("gpsimd", lambda E: E.memset(zrow, 0.0))
        P.do("gpsimd", lambda E: E.memset(ones2, 0.0), inc=("sGC", 1))
        P.wait("gpsimd", "sGC")
        P.do("gpsimd", lambda E: E.memset(ones2[0:64, 0:1], 1.0))
        P.do("gpsimd", lambda E: E.memset(ones2[64:128, 1:2], 1.0))
        P.do("gpsimd", lambda E: E.memset(maskw, 1.0), inc=("sGC", 1))
        P.wait("gpsimd", "sGC")
        P.do("gpsimd", lambda E: E.affine_select(
            out=maskw[:, 0:WIN], in_=maskw[:, 0:WIN], compare_op=ALU.is_ge,
            fill=0.0, base=0, pattern=[[1, WIN]], channel_multiplier=-1))
        P.do("gpsimd", lambda E: E.affine_select(
            out=maskw[:, WIN:2 * WIN], in_=maskw[:, WIN:2 * WIN],
            compare_op=ALU.is_ge, fill=0.0, base=-128, pattern=[[1, WIN]],
            channel_multiplier=-1), inc=("sGC", 1))
        P.do("gpsimd", lambda E: E.iota(stat[:, 0:1], pattern=[[0, 1]], base=0,
                                        channel_multiplier=1,
                                        allow_small_or_imprecise_dtypes=True),
             inc=("sGC", 1))
        P.wait("vector", "sGC")
        P.dov(lambda E: E.tensor_single_scalar(invf2, stat[:, 0:1], 63.5,
                                               op=ALU.is_gt))
        P.dov(lambda E: E.tensor_scalar(invf2, invf2, -64.0, None, op0=ALU.mult))
        P.do("vector", lambda E: E.tensor_add(invf2, invf2, stat[:, 0:1]),
             inc=("sVE", 1))
        P.wait("scalar", "sVE")
        P.do("scalar", lambda E: E.activation(invf2, invf2, AF.Exp, scale=LN_THETA),
             inc=("sSC", 1))
        P.do("sync", lambda E: E.dma_start(out=r_sb(dmsk, 8),
                                           in_=r_in(dmask, 8)), inc=("sIN", 16))
        P.do("sync", lambda E: E.dma_start(out=core_sb, in_=core_slot), inc=("sIN", 16))

        def _relay():
            P.wait("sync", "sK")
            P.do("sync", lambda E: E.dma_start(out=core_sb, in_=core_slot),
                 inc=("sK", 16))
        P.relay = _relay
        P.do("sync", lambda E: E.dma_start(out=upsc, in_=scup), inc=("sIN", 16))
        P.do("sync", lambda E: E.dma_start(out=srow[0:1, 0:D], in_=scdn),
             inc=("sIN", 16))
        P.do("sync", lambda E: E.dma_start(out=srow2[0:1, 0:1], in_=scout),
             inc=("sIN", 16))
        P.do("sync", lambda E: E.dma_start(out=lams, in_=lam0), inc=("sIN", 16))
        P.do("sync", lambda E: E.dma_start(out=ramp, in_=rampi), inc=("sIN", 16))
        sm_in = P.count("sIN")
        sm_in2 = sm_in
        P.wait("scalar", "sIN", sm_in)
        P.do("scalar", lambda E: E.activation(lams, lams, AF.Sigmoid), inc=("sSC", 1))
        P.wait("vector", "sIN", sm_in)
        P.do("vector", lambda E: E.tensor_scalar(srow[0:1, 0:D], srow[0:1, 0:D],
                                                 srow2[0:1, 0:1], None, op0=ALU.mult),
             inc=("sVE", 1))
        P.wait("scalar", "sVE")
        P.do("scalar", lambda E: E.copy(srowb[0:1, 0:D], srow[0:1, 0:D]),
             inc=("sSC", 1))
        P.wait("tensor", "sSC")
        P.wait("tensor", "sGC")
        for i in range(4):
            P.do("tensor", lambda E, i=i: E.matmul(
                psb[5][:, 0:512], onerow, srowb[0:1, i * 512:(i + 1) * 512],
                start=True, stop=True), inc=("sMM", 1))
            P.wait("vector", "sMM")
            P.do("vector", lambda E, i=i: E.tensor_copy(
                dsrow[:, i * 512:(i + 1) * 512], psb[5][:, 0:512]), inc=("sVE", 1))
            P.wait("tensor", "sVE")

        arena.reset()
        wst32 = arena.take(2048, F32)
        wstbf = arena.take(2048, BF16)
        upTs = arena.take(16 * 1024, BF16)
        dnTs = arena.take(8 * 2048, BF16)

        for sup in range(FFN_H // 1024):
            for strip in range(8):
                h0 = sup * 1024 + strip * 128
                P.wait("sync", "sSC")
                P.do("sync", lambda E, h0=h0: E.dma_start(
                    out=wst32, in_=wup[h0:h0 + 128, :]), inc=("sW", 16))
                P.wait("scalar", "sW")
                P.wait("scalar", "sMM")
                P.do("scalar", lambda E: E.sign(wstbf, wst32), inc=("sSC", 1))
                P.wait("tensor", "sSC")
                for g in range(2):
                    bank = pst if g == 0 else pst2
                    P.wait("tensor", "sVE")
                    for t8 in range(8):
                        d16 = g * 8 + t8
                        P.do("tensor", lambda E, d16=d16, t8=t8, bank=bank:
                             E.transpose(bank[:, t8 * 128:(t8 + 1) * 128],
                                         wstbf[:, d16 * 128:(d16 + 1) * 128], ident),
                             inc=("sMM", 1) if t8 == 7 else None)
                    P.wait("vector", "sMM")
                    for t8 in range(8):
                        d16 = g * 8 + t8
                        P.do("vector", lambda E, d16=d16, t8=t8, strip=strip,
                             bank=bank: E.tensor_copy(
                                 upTs[:, d16 * 1024 + strip * 128:
                                      d16 * 1024 + strip * 128 + 128],
                                 bank[:, t8 * 128:(t8 + 1) * 128]),
                             inc=("sVE", 1) if t8 == 7 else None)
            P.wait("sync", "sVE")
            P.wait("sync", "sST")
            P.do("sync", lambda E, sup=sup: E.dma_start(
                out=r_in(upT[:, sup * 1024:(sup + 1) * 1024], 16),
                in_=r_sb(upTs, 16)), inc=("sST", 16))
            P.wait("vector", "sST")
        up_staged = P.count("sST")

        for hw in range(FFN_H // 1024):
            for dstrip in range(16):
                P.wait("sync", "sSC")
                P.do("sync", lambda E, hw=hw, dstrip=dstrip: E.dma_start(
                    out=wst32[:, 0:1024],
                    in_=wdn[dstrip * 128:(dstrip + 1) * 128,
                            hw * 1024:(hw + 1) * 1024]), inc=("sW", 16))
                P.wait("scalar", "sW")
                P.wait("scalar", "sMM")
                P.do("scalar", lambda E: E.sign(wstbf[:, 0:1024], wst32[:, 0:1024]),
                     inc=("sSC", 1))
                P.wait("tensor", "sSC")
                P.wait("tensor", "sVE")
                for h8 in range(8):
                    P.do("tensor", lambda E, h8=h8: E.transpose(
                        pst[:, h8 * 128:(h8 + 1) * 128],
                        wstbf[:, h8 * 128:(h8 + 1) * 128], ident),
                        inc=("sMM", 1) if h8 == 7 else None)
                P.wait("vector", "sMM")
                for h8 in range(8):
                    P.do("vector", lambda E, h8=h8, dstrip=dstrip: E.tensor_copy(
                        dnTs[:, h8 * 2048 + dstrip * 128:
                             h8 * 2048 + dstrip * 128 + 128],
                        pst[:, h8 * 128:(h8 + 1) * 128]),
                        inc=("sVE", 1) if h8 == 7 else None)
            P.wait("sync", "sVE")
            P.wait("sync", "sST")
            P.do("sync", lambda E, hw=hw: E.dma_start(
                out=r_in(dnT[hw * 1024:(hw + 1) * 1024, :], 8),
                in_=r_sb(dnTs, 8)), inc=("sST", 16))
            P.wait("vector", "sST")
        if debug:
            for s_ in ("sIN", "sST", "sW", "sD0", "sD1"):
                P.wait("gpsimd", s_)
            P.do("gpsimd", lambda E: E.dma_start(out=dbg["d_upT"], in_=upT),
                 inc=("sGP", 16))
            P.do("gpsimd", lambda E: E.dma_start(out=dbg["d_dnT"], in_=dnT),
                 inc=("sGP", 16))
        P.barrier()
        P.ckpt()

        def emit_ln_chunk(xc, lnc, sq, nt):
            for t in range(nt):
                P.do("vector", lambda E, t=t: E.reduce_sum(
                    stat[:, t:t + 1], xc[:, t * D:(t + 1) * D], axis=AX),
                    inc=("sVE", 1) if t == nt - 1 else None)
            for t in range(nt):
                P.do("scalar", lambda E, t=t: E.activation(
                    sq[:, t * D:(t + 1) * D], xc[:, t * D:(t + 1) * D], AF.Square),
                    inc=("sSC", 1) if t == nt - 1 else None)
            P.wait("vector", "sSC")
            for t in range(nt):
                P.do("vector", lambda E, t=t: E.reduce_sum(
                    stat[:, 4 + t:5 + t], sq[:, t * D:(t + 1) * D], axis=AX),
                    inc=("sVE", 1) if t == nt - 1 else None)
            P.wait("vector", "sVE")
            P.dov(lambda E, nt=nt: E.tensor_scalar(
                stat[:, 8:8 + nt], stat[:, 0:nt], 1.0 / D, None, op0=ALU.mult))
            P.dov(lambda E, nt=nt: E.tensor_mul(
                stat[:, 12:12 + nt], stat[:, 8:8 + nt], stat[:, 8:8 + nt]))
            P.dov(lambda E, nt=nt: E.tensor_scalar(
                stat[:, 16:16 + nt], stat[:, 4:4 + nt], 1.0 / D, None, op0=ALU.mult))
            P.dov(lambda E, nt=nt: E.tensor_sub(
                stat[:, 16:16 + nt], stat[:, 16:16 + nt], stat[:, 12:12 + nt]))
            P.do("vector", lambda E, nt=nt: E.tensor_scalar(
                stat[:, 16:16 + nt], stat[:, 16:16 + nt], 1.0, 1e-5, op0=ALU.mult,
                op1=ALU.add), inc=("sVE", 1))
            P.wait("scalar", "sVE")
            P.do("scalar", lambda E, nt=nt: E.activation(
                stat[:, 20:20 + nt], stat[:, 16:16 + nt], AF.Sqrt), inc=("sSC", 1))
            P.wait("vector", "sSC")
            P.dov(lambda E, nt=nt: E.reciprocal(
                stat[:, 24:24 + nt], stat[:, 20:20 + nt]))
            P.dov(lambda E, nt=nt: E.tensor_mul(
                stat[:, 28:28 + nt], stat[:, 8:8 + nt], stat[:, 24:24 + nt]))
            P.dov(lambda E, nt=nt: E.tensor_scalar(
                stat[:, 28:28 + nt], stat[:, 28:28 + nt], -1.0, None, op0=ALU.mult))
            for t in range(nt):
                P.do("vector", lambda E, t=t: E.tensor_scalar(
                    lnc[:, t * D:(t + 1) * D], xc[:, t * D:(t + 1) * D],
                    stat[:, 24 + t:25 + t], stat[:, 28 + t:29 + t],
                    op0=ALU.mult, op1=ALU.add), inc=("sVE", 1))

        def emit_tr_chunk(src, dst, nt):
            C = nt * 128
            P.wait("tensor", "sSC")
            P.wait("tensor", "sVE")
            for d16 in range(16):
                bank = pst if d16 % 2 == 0 else pst2
                P.wait("tensor", "sVE")
                for t in range(nt):
                    P.do("tensor", lambda E, d16=d16, t=t, bank=bank: E.transpose(
                        bank[:, t * 128:(t + 1) * 128],
                        src[:, t * D + d16 * 128:t * D + d16 * 128 + 128], ident),
                        inc=("sMM", 1) if t == nt - 1 else None)
                P.wait("vector", "sMM")
                P.do("vector", lambda E, d16=d16, C=C, bank=bank: E.tensor_copy(
                    dst[:, d16 * CH:d16 * CH + C], bank[:, 0:C]), inc=("sVE", 1))
            P.wait("tensor", "sVE")

        def load_w16(wdst, wsrc):
            P.wait("sync", "sGP")
            P.do("sync", lambda E, wdst=wdst, wsrc=wsrc: E.dma_start(
                out=r_sb(wdst, 16), in_=r_in(wsrc, 16)), inc=("sW", 16))
            P.wait("tensor", "sW")

        def wt(wsb_, k16):
            return wsb_[:, k16 * D:(k16 + 1) * D]

        def ln_pass(x_src, lnT_dst, ntok):
            arena.reset()
            xc = [arena.take(4 * D, F32), arena.take(4 * D, F32)]
            lnc = arena.take(4 * D, BF16)
            lnT = arena.take(16 * CH, BF16)
            sq = arena.take(4 * D, F32)
            free_at = [0, 0]
            for i, (t0, nt) in enumerate(chunks_of(ntok)):
                buf = xc[i % 2]
                sd = "sD0" if i % 2 == 0 else "sD1"
                P.wait("sync", "sVE", free_at[i % 2])
                P.do("sync", lambda E, t0=t0, nt=nt, buf=buf: E.dma_start(
                    out=r_sb(buf, 4)[:, 0:nt, :],
                    in_=r_in(x_src[t0:t0 + nt * 128, :], nt)), inc=(sd, 16))
                P.wait("vector", sd)
                P.wait("scalar", sd)
                emit_ln_chunk(buf, lnc, sq, nt)
                free_at[i % 2] = P.count("sVE")
                emit_tr_chunk(lnc, lnT, nt)
                P.wait("sync", "sVE")
                P.wait("sync", "sST")
                P.do("sync", lambda E, t0=t0, nt=nt: E.dma_start(
                    out=r_in(lnT_dst[:, t0:t0 + nt * 128], 16),
                    in_=r_sb(lnT, 16)[:, :, 0:nt * 128]), inc=("sST", 16))
                P.wait("vector", "sST")
            P.barrier()

        def rope_tables(t0, ntk, rc, rs, rw, iw, rw2, rw3):
            n = ntk
            P.wait("vector", "sIN", sm_in2)
            P.dov(lambda E, t0=t0: E.tensor_scalar(rw[:, 0:n], ramp[:, 0:n],
                                                   1.0, float(t0), op0=ALU.mult,
                                                   op1=ALU.add))
            P.dov(lambda E: E.tensor_scalar(rw2[:, 0:n], rw[:, 0:n], invf2, None,
                                            op0=ALU.mult))
            P.dov(lambda E: E.tensor_scalar(rw[:, 0:n], rw2[:, 0:n], INV_2PI, 0.5,
                                            op0=ALU.mult, op1=ALU.add))
            P.dov(lambda E: E.tensor_copy(iw[:, 0:n], rw[:, 0:n]))
            P.dov(lambda E: E.tensor_copy(rw[:, 0:n], iw[:, 0:n]))
            P.dov(lambda E: E.tensor_scalar(rw3[:, 0:n], rw[:, 0:n], -TWO_PI_HI,
                                            None, op0=ALU.mult))
            P.dov(lambda E: E.tensor_add(rs[:, 0:n], rw2[:, 0:n], rw3[:, 0:n]))
            P.dov(lambda E: E.tensor_scalar(rw3[:, 0:n], rw[:, 0:n], -TWO_PI_LO,
                                            None, op0=ALU.mult))
            P.dov(lambda E: E.tensor_add(rs[:, 0:n], rs[:, 0:n], rw3[:, 0:n]))
            P.dov(lambda E: E.tensor_single_scalar(rw3[:, 0:n], rs[:, 0:n],
                                                   float(math.pi), op=ALU.is_gt))
            P.dov(lambda E: E.scalar_tensor_tensor(rs[:, 0:n], rw3[:, 0:n],
                                                   float(-2.0 * math.pi),
                                                   rs[:, 0:n], op0=ALU.mult,
                                                   op1=ALU.add))
            P.dov(lambda E: E.tensor_single_scalar(rw3[:, 0:n], rs[:, 0:n],
                                                   float(-math.pi), op=ALU.is_lt))
            P.do("vector", lambda E: E.scalar_tensor_tensor(
                rs[:, 0:n], rw3[:, 0:n], float(2.0 * math.pi), rs[:, 0:n],
                op0=ALU.mult, op1=ALU.add), inc=("sVE", 1))
            P.wait("scalar", "sVE")
            P.do("scalar", lambda E: E.activation(rs[:, 0:n], rs[:, 0:n], AF.Sin),
                 inc=("sSC", 1))
            P.dov(lambda E: E.tensor_scalar(rw[:, 0:n], rw2[:, 0:n], INV_2PI, 0.75,
                                            op0=ALU.mult, op1=ALU.add))
            P.dov(lambda E: E.tensor_copy(iw[:, 0:n], rw[:, 0:n]))
            P.dov(lambda E: E.tensor_copy(rw[:, 0:n], iw[:, 0:n]))
            P.dov(lambda E: E.tensor_scalar(rw3[:, 0:n], rw[:, 0:n], -TWO_PI_HI,
                                            None, op0=ALU.mult))
            P.dov(lambda E: E.tensor_add(rc[:, 0:n], rw2[:, 0:n], rw3[:, 0:n]))
            P.dov(lambda E: E.tensor_scalar(rw3[:, 0:n], rw[:, 0:n], -TWO_PI_LO,
                                            None, op0=ALU.mult))
            P.dov(lambda E: E.tensor_add(rc[:, 0:n], rc[:, 0:n], rw3[:, 0:n]))
            P.dov(lambda E: E.tensor_scalar(rc[:, 0:n], rc[:, 0:n], 1.0,
                                            float(math.pi / 2.0),
                                            op0=ALU.mult, op1=ALU.add))
            P.dov(lambda E: E.tensor_single_scalar(rw3[:, 0:n], rc[:, 0:n],
                                                   float(math.pi), op=ALU.is_gt))
            P.dov(lambda E: E.scalar_tensor_tensor(rc[:, 0:n], rw3[:, 0:n],
                                                   float(-2.0 * math.pi),
                                                   rc[:, 0:n], op0=ALU.mult,
                                                   op1=ALU.add))
            P.dov(lambda E: E.tensor_single_scalar(rw3[:, 0:n], rc[:, 0:n],
                                                   float(-math.pi), op=ALU.is_lt))
            P.do("vector", lambda E: E.scalar_tensor_tensor(
                rc[:, 0:n], rw3[:, 0:n], float(2.0 * math.pi), rc[:, 0:n],
                op0=ALU.mult, op1=ALU.add), inc=("sVE", 1))
            P.wait("scalar", "sVE")
            P.do("scalar", lambda E: E.activation(rc[:, 0:n], rc[:, 0:n], AF.Sin),
                 inc=("sSC", 1))
            P.wait("vector", "sSC")

        def projT_pass(lnT_src, wsrc, dstbuf, rope, ntok):
            arena.reset()
            wsb_ = arena.take(16 * D, BF16)
            lnT = [arena.take(16 * CH, BF16), arena.take(16 * CH, BF16)]
            outT = arena.take(16 * CH, BF16)
            rc = arena.take(CH, F32)
            rs = arena.take(CH, F32)
            rw = arena.take(CH, F32)
            rw2 = arena.take(CH, F32)
            rw3 = arena.take(CH, F32)
            iw = arena.take(CH, I32)
            qnb = arena.take(CH, BF16)
            load_w16(wsb_, wsrc)
            free_at = [0, 0]
            for i, (t0, nt) in enumerate(chunks_of(ntok)):
                ntk = nt * 128
                lt = lnT[i % 2]
                sd = "sD0" if i % 2 == 0 else "sD1"
                P.wait("sync", "sMM", free_at[i % 2])
                P.do("sync", lambda E, t0=t0, ntk=ntk, lt=lt, sd=sd: E.dma_start(
                    out=r_sb(lt, 16)[:, :, 0:ntk],
                    in_=r_in(lnT_src[:, t0:t0 + ntk], 16)), inc=(sd, 16))
                if rope:
                    rope_tables(t0, ntk, rc, rs, rw, iw, rw2, rw3)
                P.wait("tensor", sd)
                for h16 in range(16):
                    bank = psb[h16 % 2]
                    P.wait("tensor", "sVE")
                    for k16 in range(16):
                        P.do("tensor", lambda E, h16=h16, k16=k16, bank=bank, lt=lt,
                             ntk=ntk: E.matmul(
                                 bank[:, 0:ntk],
                                 wt(wsb_, k16)[:, h16 * 128:(h16 + 1) * 128],
                                 lt[:, k16 * CH:k16 * CH + ntk],
                                 start=(k16 == 0), stop=(k16 == 15)),
                             inc=("sMM", 1) if k16 == 15 else None)
                    P.wait("vector", "sMM")
                    if rope:
                        dsl = outT[:, h16 * CH:h16 * CH + ntk]
                        rb = psb[2 + h16 % 2]
                        P.wait("scalar", "sMM")
                        P.do("scalar", lambda E, bank=bank, ntk=ntk: E.copy(
                            qnb[:, 0:ntk], bank[:, 0:ntk]), inc=("sSC", 1))
                        P.wait("tensor", "sSC")
                        P.wait("tensor", "sVE")
                        P.do("tensor", lambda E, rb=rb, ntk=ntk: E.matmul(
                            rb[:, 0:ntk], pmat, qnb[:, 0:ntk], start=True,
                            stop=True), inc=("sMM", 1))
                        P.wait("vector", "sMM")
                        P.do("vector", lambda E, ntk=ntk: E.tensor_mul(
                            rw[:, 0:ntk], qnb[:, 0:ntk], rc[:, 0:ntk]))
                        P.do("vector", lambda E, rb=rb, ntk=ntk: E.tensor_mul(
                            rw2[:, 0:ntk], rb[:, 0:ntk], rs[:, 0:ntk]),
                            inc=("sVE", 1))
                        P.wait("vector", "sVE")
                        P.do("vector", lambda E, dsl=dsl, ntk=ntk: E.tensor_add(
                            dsl, rw[:, 0:ntk], rw2[:, 0:ntk]), inc=("sVE", 1))
                    else:
                        P.do("vector", lambda E, h16=h16, bank=bank, ntk=ntk:
                             E.tensor_copy(outT[:, h16 * CH:h16 * CH + ntk],
                                           bank[:, 0:ntk]), inc=("sVE", 1))
                free_at[i % 2] = P.count("sMM")
                P.wait("sync", "sVE")
                P.wait("sync", "sST")
                P.do("sync", lambda E, t0=t0, ntk=ntk: E.dma_start(
                    out=r_in(dstbuf[:, t0:t0 + ntk], 16),
                    in_=r_sb(outT, 16)[:, :, 0:ntk]), inc=("sST", 16))
                P.wait("vector", "sST")
            P.barrier()

        def projN_pass(lnT_src, wsrc, dstbuf, ntok):
            arena.reset()
            wsb_ = arena.take(16 * D, BF16)
            lnT = [arena.take(16 * CH, BF16), arena.take(16 * CH, BF16)]
            outN = arena.take(4 * D, BF16)
            load_w16(wsb_, wsrc)
            free_at = [0, 0]
            for i, (t0, nt) in enumerate(chunks_of(ntok)):
                ntk = nt * 128
                lt = lnT[i % 2]
                sd = "sD0" if i % 2 == 0 else "sD1"
                P.wait("sync", "sMM", free_at[i % 2])
                P.do("sync", lambda E, t0=t0, ntk=ntk, lt=lt, sd=sd: E.dma_start(
                    out=r_sb(lt, 16)[:, :, 0:ntk],
                    in_=r_in(lnT_src[:, t0:t0 + ntk], 16)), inc=(sd, 16))
                P.wait("tensor", sd)
                for t in range(nt):
                    for n4 in range(4):
                        bank = psb[(t * 4 + n4) % 2]
                        P.wait("tensor", "sVE")
                        for k16 in range(16):
                            P.do("tensor", lambda E, t=t, n4=n4, k16=k16, bank=bank,
                                 lt=lt: E.matmul(
                                     bank[:, 0:512],
                                     lt[:, k16 * CH + t * 128:k16 * CH + t * 128 + 128],
                                     wt(wsb_, k16)[:, n4 * 512:(n4 + 1) * 512],
                                     start=(k16 == 0), stop=(k16 == 15)),
                                 inc=("sMM", 1) if k16 == 15 else None)
                        P.wait("vector", "sMM")
                        P.do("vector", lambda E, t=t, n4=n4, bank=bank: E.tensor_copy(
                            outN[:, t * D + n4 * 512:t * D + (n4 + 1) * 512],
                            bank[:, 0:512]), inc=("sVE", 1))
                free_at[i % 2] = P.count("sMM")
                P.wait("sync", "sVE")
                P.wait("sync", "sST")
                P.do("sync", lambda E, t0=t0, nt=nt: E.dma_start(
                    out=r_in(dstbuf[t0:t0 + nt * 128, :], nt),
                    in_=r_sb(outN, 4)[:, 0:nt, :]), inc=("sST", 16))
                P.wait("vector", "sST")
            P.barrier()

        ln_pass(x0, ln1T_buf, S)
        if debug:
            P.wait("gpsimd", "sST")
            P.do("gpsimd", lambda E: E.dma_start(out=dbg["d_ln1T"], in_=ln1T_buf),
                 inc=("sGP", 16))
        P.ckpt()
        projT_pass(ln1T_buf, wbf["wqa"], qa_buf, True, S)
        P.ckpt()
        projT_pass(ln1T_buf, wbf["wka"], ka_buf, True, S)
        P.ckpt()
        projN_pass(ln1T_buf, wbf["wva"], va_buf, S)
        if debug:
            P.wait("gpsimd", "sST")
            P.do("gpsimd", lambda E: E.dma_start(out=dbg["d_qa"], in_=qa_buf),
                 inc=("sGP", 16))
            P.do("gpsimd", lambda E: E.dma_start(out=dbg["d_ka"], in_=ka_buf),
                 inc=("sGP", 16))
            P.do("gpsimd", lambda E: E.dma_start(out=dbg["d_va"], in_=va_buf),
                 inc=("sGP", 16))

        P.ckpt()
        arena.reset()
        wsb_o = arena.take(16 * D, BF16)
        qtc = arena.take(16 * CH, BF16)
        KW = WIN + CH
        ktc = arena.take(16 * KW, BF16)
        vc = arena.take(6 * D, BF16)
        otc = arena.take(16 * CH, BF16)
        xtile = arena.take(D, F32)
        pb = arena.take(4 * WIN, BF16)
        sqa = arena.take(CH + KW, BF16)
        rwa = arena.take(WIN, F32)
        denb = arena.take(2048, BF16)

        load_w16(wsb_o, wbf["woa"])
        den_copied = [0, 0]
        nm_count = [0, 0]

        def ktt(h16, c0, cn):
            return ktc[:, h16 * KW + c0:h16 * KW + c0 + cn]

        import os as _os
        _p4n = int(_os.environ.get("P4N", "99"))
        _p4h = int(_os.environ.get("P4H", "99"))
        _p4w = int(_os.environ.get("P4W", "99"))
        for ch in range(min(NCH, _p4n)):
            P.wait("sync", "sMM")
            P.do("sync", lambda E, ch=ch: E.dma_start(
                out=r_sb(qtc, 16), in_=r_in(qa_buf[:, ch * CH:(ch + 1) * CH], 16)),
                inc=("sD0", 16))
            lo = max(ch * CH - WIN, 0)
            halo = ch * CH - lo
            P.do("sync", lambda E, ch=ch, lo=lo, halo=halo: E.dma_start(
                out=r_sb(ktc, 16)[:, :, WIN - halo:KW],
                in_=r_in(ka_buf[:, lo:(ch + 1) * CH], 16)), inc=("sD0", 16))
            a0 = (WIN - halo) // 128
            P.do("sync", lambda E, ch=ch, lo=lo, a0=a0: E.dma_start(
                out=r_sb(vc, 6)[:, a0:6, :],
                in_=r_in(va_buf[lo:(ch + 1) * CH, :], 6 - a0)), inc=("sD0", 16))
            if ch == 0:
                P.do("gpsimd", lambda E: E.memset(r_sb(ktc, 16)[:, :, 0:WIN], 0.0),
                     inc=("sGC", 1))
                P.do("gpsimd", lambda E: E.memset(vc[:, 0:2 * D], 0.0),
                     inc=("sGC", 1))
                P.wait("tensor", "sGC")
                P.wait("vector", "sGC")
            P.wait("tensor", "sD0")
            P.wait("vector", "sD0")

            for h16 in range(min(HEADS, _p4h)):
                P.do("vector", lambda E, h16=h16: E.tensor_mul(
                    sqa[:, 0:CH], qtc[:, h16 * CH:(h16 + 1) * CH],
                    qtc[:, h16 * CH:(h16 + 1) * CH]))
                P.do("vector", lambda E, h16=h16: E.tensor_mul(
                    sqa[:, CH:CH + KW], ktt(h16, 0, KW), ktt(h16, 0, KW)),
                    inc=("sVE", 1))
                P.wait("tensor", "sVE")
                P.do("tensor", lambda E: E.matmul(psb[5][0:1, 0:CH], onesb,
                                                  sqa[:, 0:CH], start=True, stop=True))
                P.do("tensor", lambda E: E.matmul(psb[4][0:1, 0:512], onesb,
                                                  sqa[:, CH:CH + 512],
                                                  start=True, stop=True))
                P.do("tensor", lambda E: E.matmul(psb[3][0:1, 0:KW - 512], onesb,
                                                  sqa[:, CH + 512:CH + KW],
                                                  start=True, stop=True),
                     inc=("sMM", 1))
                P.wait("vector", "sMM")
                P.do("vector", lambda E: E.reduce_max(srow[0:1, 0:1],
                                                      psb[5][0:1, 0:CH], axis=AX))
                P.do("vector", lambda E: E.reduce_max(srow[0:1, 1:2],
                                                      psb[4][0:1, 0:512], axis=AX))
                P.dov(lambda E: E.reduce_max(srow[0:1, 3:4],
                                             psb[3][0:1, 0:KW - 512], axis=AX))
                P.dov(lambda E: E.tensor_max(srow[0:1, 1:2], srow[0:1, 1:2],
                                             srow[0:1, 3:4]))
                P.dov(lambda E: E.tensor_add(srow[0:1, 0:1], srow[0:1, 0:1],
                                             srow[0:1, 1:2]))
                P.do("vector", lambda E: E.tensor_scalar(
                    srow[0:1, 2:3], srow[0:1, 0:1], -0.5 * SCALE_A, None,
                    op0=ALU.mult), inc=("sVE", 1))
                P.wait("scalar", "sVE")
                P.do("scalar", lambda E: E.copy(srowb[0:1, 0:1], srow[0:1, 2:3]),
                     inc=("sSC", 1))
                P.wait("tensor", "sSC")
                P.do("tensor", lambda E: E.matmul(psb[5][:, 0:1], onerow,
                                                  srowb[0:1, 0:1], start=True,
                                                  stop=True), inc=("sMM", 1))
                P.wait("scalar", "sMM")
                P.do("scalar", lambda E: E.copy(stat[:, 8:9], psb[5][:, 0:1]),
                     inc=("sSC", 1))

                for w2 in range(2 if _p4w >= 2 else 0):
                    q0 = w2 * WIN
                    P.wait("tensor", "sSC")
                    P.wait("tensor", "sVE")
                    for kb in range(4):
                        bank = psb[2] if kb < 2 else psb[3]
                        P.do("tensor", lambda E, h16=h16, q0=q0, kb=kb, bank=bank:
                             E.matmul(bank[:, (kb % 2) * WIN:(kb % 2 + 1) * WIN],
                                      ktt(h16, q0 + kb * 128, 128),
                                      qtc[:, h16 * CH + q0:h16 * CH + q0 + WIN],
                                      start=True, stop=True),
                             inc=("sMM", 1) if kb == 3 else None)
                    P.wait("scalar", "sMM")
                    P.do("scalar", lambda E: E.activation(
                        pb[:, 0:2 * WIN], psb[2][:, 0:2 * WIN], AF.Exp,
                        bias=stat[:, 8:9], scale=SCALE_A))
                    P.do("scalar", lambda E: E.activation(
                        pb[:, 2 * WIN:4 * WIN], psb[3][:, 0:2 * WIN], AF.Exp,
                        bias=stat[:, 8:9], scale=SCALE_A), inc=("sSC", 1))
                    P.wait("vector", "sSC")
                    P.do("vector", lambda E: E.tensor_mul(
                        pb[:, 2 * WIN:3 * WIN], pb[:, 2 * WIN:3 * WIN],
                        maskw[:, 0:WIN]))
                    P.do("vector", lambda E: E.tensor_mul(
                        pb[:, 3 * WIN:4 * WIN], pb[:, 3 * WIN:4 * WIN],
                        maskw[:, WIN:2 * WIN]), inc=("sVE", 1))
                    if ch == 0 and w2 == 0:
                        P.do("vector", lambda E: E.memset(pb[:, 0:2 * WIN], 0.0),
                             inc=("sVE", 1))
                        P.wait("tensor", "sVE")
                    P.wait("tensor", "sVE")
                    if _p4w < 3:
                        continue
                    sbank = psb[5] if w2 == 0 else psb[6]
                    P.wait("tensor", "sVE", den_copied[w2])
                    for kb in range(4):
                        P.do("tensor", lambda E, kb=kb, sbank=sbank: E.matmul(
                            sbank[0:1, 0:WIN], onesb, pb[:, kb * WIN:(kb + 1) * WIN],
                            start=(kb == 0), stop=(kb == 3)),
                            inc=("sMM", 1) if kb == 3 else None)
                    for kb in range(4):
                        r0 = (q0 + kb * 128) // 128
                        P.do("tensor", lambda E, h16=h16, kb=kb, r0=r0: E.matmul(
                            psb[4][:, 0:WIN],
                            vc[:, r0 * D + h16 * 128:r0 * D + h16 * 128 + 128],
                            pb[:, kb * WIN:(kb + 1) * WIN],
                            start=(kb == 0), stop=(kb == 3)),
                            inc=("sMM", 1) if kb == 3 else None)
                    P.wait("vector", "sMM")
                    if _p4w < 4:
                        continue
                    P.do("vector", lambda E, h16=h16, w2=w2, sbank=sbank:
                         E.tensor_copy(
                             srow2[0:1, (h16 % 4) * 512 + w2 * WIN:
                                   (h16 % 4) * 512 + w2 * WIN + WIN],
                             sbank[0:1, 0:WIN]), inc=("sVE", 1))
                    den_copied[w2] = P.count("sVE")
                    P.do("vector", lambda E, h16=h16, q0=q0: E.tensor_copy(
                        otc[:, h16 * CH + q0:h16 * CH + q0 + WIN],
                        psb[4][:, 0:WIN]), inc=("sVE", 1))
                if _p4w >= 4 and _p4h >= 16 and h16 % 4 == 3:
                    g0 = h16 - 3
                    P.wait("vector", "sSC")
                    P.do("vector", lambda E: E.reciprocal(
                        srow[0:1, 0:2048], srow2[0:1, 0:2048]), inc=("sVE", 1))
                    P.wait("scalar", "sVE")
                    P.wait("scalar", "sMM")
                    P.do("scalar", lambda E: E.copy(denb[0:1, 0:2048],
                                                    srow[0:1, 0:2048]),
                         inc=("sSC", 1))
                    P.wait("tensor", "sSC")
                    for hl in range(4):
                        bank = psb[2 + hl % 2]
                        if hl >= 2:
                            P.wait("tensor", "sVE", nm_count[hl % 2])
                        P.do("tensor", lambda E, hl=hl, bank=bank: E.matmul(
                            bank[:, 0:512], onerow,
                            denb[0:1, hl * 512:(hl + 1) * 512],
                            start=True, stop=True), inc=("sMM", 1))
                        P.wait("vector", "sMM")
                        P.do("vector", lambda E, h=g0 + hl, bank=bank: E.tensor_mul(
                            otc[:, h * CH:(h + 1) * CH],
                            otc[:, h * CH:(h + 1) * CH],
                            bank[:, 0:512]), inc=("sVE", 1))
                        nm_count[hl % 2] = P.count("sVE")

            for t4 in range(4 if _p4w >= 5 else 0):
                P.wait("sync", "sST")
                P.wait("sync", "sVE")
                P.do("sync", lambda E, ch=ch, t4=t4: E.dma_start(
                    out=xtile,
                    in_=x0[ch * CH + t4 * 128:ch * CH + (t4 + 1) * 128, :]),
                    inc=("sXT", 16))
                P.wait("vector", "sXT")
                for n4 in range(4 if _p4w >= 6 else 0):
                    bank = psb[n4 % 2]
                    P.wait("tensor", "sVE")
                    for k16 in range(16):
                        P.do("tensor", lambda E, t4=t4, n4=n4, k16=k16, bank=bank:
                             E.matmul(bank[:, 0:512],
                                      otc[:, k16 * CH + t4 * 128:
                                          k16 * CH + t4 * 128 + 128],
                                      wt(wsb_o, k16)[:, n4 * 512:(n4 + 1) * 512],
                                      start=(k16 == 0), stop=(k16 == 15)),
                             inc=("sMM", 1) if k16 == 15 else None)
                    P.wait("vector", "sMM")
                    _p4x = _os.environ.get("P4X", "add")
                    if _p4x == "add":
                        P.do("vector", lambda E, n4=n4, bank=bank: E.tensor_add(
                            xtile[:, n4 * 512:(n4 + 1) * 512],
                            xtile[:, n4 * 512:(n4 + 1) * 512], bank[:, 0:512]),
                            inc=("sVE", 1))
                    elif _p4x == "cp":
                        P.do("vector", lambda E, n4=n4, bank=bank: E.tensor_copy(
                            xtile[:, n4 * 512:(n4 + 1) * 512], bank[:, 0:512]),
                            inc=("sVE", 1))
                    else:
                        P.do("vector", lambda E: E.engine_nop(), inc=("sVE", 1))
                if _p4w < 7:
                    continue
                P.wait("sync", "sVE")
                P.do("sync", lambda E, ch=ch, t4=t4: E.dma_start(
                    out=x1_buf[ch * CH + t4 * 128:ch * CH + (t4 + 1) * 128, :],
                    in_=xtile), inc=("sST", 16))
        P.barrier()
        if debug:
            P.wait("gpsimd", "sST")
            P.do("gpsimd", lambda E: E.dma_start(out=dbg["d_x1"], in_=x1_buf),
                 inc=("sGP", 16))

        P.ckpt()
        def ffn_pass(lnT_src, x_src, x_dst, ntok):
            arena.reset()
            wsl2 = [arena.take(16 * 512, BF16), arena.take(16 * 512, BF16)]
            lnT = [arena.take(16 * CH, BF16), arena.take(16 * CH, BF16)]
            hid = arena.take(64 * CH, BF16)
            xc = arena.take(4 * D, F32)
            ac = arena.take(512, F32)
            free_at = [0, 0]
            wsl_free = [0, 0]
            wsl_done = [0, 0]
            for i, (t0, nt) in enumerate(chunks_of(ntok)):
                ntk = nt * 128
                lt = lnT[i % 2]
                sd = "sD0" if i % 2 == 0 else "sD1"
                P.wait("sync", "sMM", free_at[i % 2])
                P.do("sync", lambda E, t0=t0, ntk=ntk, lt=lt, sd=sd: E.dma_start(
                    out=r_sb(lt, 16)[:, :, 0:ntk],
                    in_=r_in(lnT_src[:, t0:t0 + ntk], 16)), inc=(sd, 16))
                P.wait("sync", "sST")
                P.do("sync", lambda E, t0=t0, nt=nt: E.dma_start(
                    out=r_sb(xc, 4)[:, 0:nt, :],
                    in_=r_in(x_src[t0:t0 + nt * 128, :], nt)), inc=("sXT", 16))
                xc_in = P.count("sXT")
                P.wait("tensor", sd)
                P.wait("tensor", "sVE")

                def up_dma(s):
                    bi = s % 2
                    wq = "sync" if bi == 0 else "scalar"
                    ws = "sW" if bi == 0 else "sWV"
                    P.wait(wq, "sMM", wsl_free[bi])
                    P.do(wq, lambda E, s=s, b=wsl2[bi]: E.dma_start(
                        out=b.rearrange("p (a c) -> p a c", a=16),
                        in_=r_in(upT[:, s * 512:(s + 1) * 512], 16)),
                        inc=(ws, 16))
                    wsl_done[bi] = P.count(ws)
                up_dma(0)
                up_act = [0, 0]
                for s in range(16):
                    bi = s % 2
                    if s + 1 < 16:
                        up_dma(s + 1)
                    P.wait("tensor", "sW" if bi == 0 else "sWV", wsl_done[bi])
                    for h4 in range(4):
                        hg = s * 4 + h4
                        bank = psb[h4 % 2]
                        P.wait("tensor", "sSC", up_act[h4 % 2])
                        for k16 in range(16):
                            P.do("tensor", lambda E, h4=h4, k16=k16, bank=bank,
                                 lt=lt, ntk=ntk, bi=bi: E.matmul(
                                     bank[:, 0:ntk],
                                     wsl2[bi][:, k16 * 512 + h4 * 128:
                                              k16 * 512 + (h4 + 1) * 128],
                                     lt[:, k16 * CH:k16 * CH + ntk],
                                     start=(k16 == 0), stop=(k16 == 15)),
                                 inc=("sMM", 1) if k16 == 15 else None)
                        P.wait("scalar", "sMM")
                        P.do("scalar", lambda E, hg=hg, bank=bank, ntk=ntk:
                             E.activation(hid[:, hg * CH:hg * CH + ntk],
                                          bank[:, 0:ntk], AF.Relu,
                                          scale=upsc[:, hg:hg + 1]), inc=("sSC", 1))
                        up_act[h4 % 2] = P.count("sSC")
                    wsl_free[bi] = P.count("sMM")
                P.wait("vector", "sXT", xc_in)
                for half in range(2):
                    def dn_dma(s, half=half):
                        bi = s % 2
                        wq = "sync" if bi == 0 else "scalar"
                        ws = "sW" if bi == 0 else "sWV"
                        P.wait(wq, "sMM", wsl_free[bi])
                        P.do(wq, lambda E, s=s, half=half, b=wsl2[bi]:
                             E.dma_start(
                                 out=b[:, 0:4 * 1024]
                                 .rearrange("p (a c) -> p a c", a=4),
                                 in_=r_in(dnT[s * 512:(s + 1) * 512,
                                              half * 1024:(half + 1) * 1024], 4)),
                             inc=(ws, 16))
                        wsl_done[bi] = P.count(ws)
                    dn_dma(0)
                    for s in range(16):
                        bi = s % 2
                        if s + 1 < 16:
                            dn_dma(s + 1)
                        P.wait("tensor", "sW" if bi == 0 else "sWV", wsl_done[bi])
                        P.wait("tensor", "sSC")
                        if s == 0:
                            P.wait("tensor", "sVE")
                        for t in range(nt):
                            for n2 in range(2):
                                bank = psb[t * 2 + n2]
                                for k4 in range(4):
                                    hg = s * 4 + k4
                                    P.do("tensor", lambda E, t=t, n2=n2, k4=k4,
                                         hg=hg, bank=bank, s=s, bi=bi: E.matmul(
                                             bank[:, 0:512],
                                             hid[:, hg * CH + t * 128:
                                                 hg * CH + t * 128 + 128],
                                             wsl2[bi][:, k4 * 1024 + n2 * 512:
                                                 k4 * 1024 + (n2 + 1) * 512],
                                             start=(s == 0 and k4 == 0),
                                             stop=(s == 15 and k4 == 3)),
                                         inc=("sMM", 1)
                                         if (k4 == 3) else None)
                        wsl_free[bi] = P.count("sMM")
                    P.wait("vector", "sMM")
                    for t in range(nt):
                        for n2 in range(2):
                            bank = psb[t * 2 + n2]
                            d0 = half * 1024 + n2 * 512
                            P.wait("vector", "sVE")
                            P.dov(lambda E, bank=bank, d0=d0: E.tensor_mul(
                                ac, bank[:, 0:512], dsrow[:, d0:d0 + 512]))
                            P.do("vector", lambda E, t=t, d0=d0: E.tensor_add(
                                xc[:, t * D + d0:t * D + d0 + 512],
                                xc[:, t * D + d0:t * D + d0 + 512], ac),
                                inc=("sVE", 1))
                free_at[i % 2] = P.count("sMM")
                P.wait("sync", "sVE")
                P.do("sync", lambda E, t0=t0, nt=nt: E.dma_start(
                    out=r_in(x_dst[t0:t0 + nt * 128, :], nt),
                    in_=r_sb(xc, 4)[:, 0:nt, :]), inc=("sST", 16))
            P.barrier()

        ln_pass(x1_buf, ln2T_buf, S)
        ffn_pass(ln2T_buf, x1_buf, x2_buf, S)
        if debug:
            P.wait("gpsimd", "sST")
            P.do("gpsimd", lambda E: E.dma_start(out=dbg["d_x2"], in_=x2_buf),
                 inc=("sGP", 16))

        P.ckpt()
        ln3T_buf = ln1T_buf
        ln_pass(x2_buf, ln3T_buf, S)
        projT_pass(ln3T_buf, wbf["wkb"], ktb_buf, False, S)
        projN_pass(ln3T_buf, wbf["wvb"], vb_buf, S)

        arena.reset()
        ktc2 = arena.take(16 * CH, BF16)
        sqk = arena.take(16 * CH, BF16)
        P.do("gpsimd", lambda E: E.memset(bcolk, 0.0), inc=("sGC", 1))
        P.do("gpsimd", lambda E: E.memset(bcolq, 0.0), inc=("sGC", 1))
        P.wait("vector", "sGC")
        for ch in range(NCH):
            P.wait("sync", "sVE")
            P.do("sync", lambda E, ch=ch: E.dma_start(
                out=r_sb(ktc2, 16), in_=r_in(ktb_buf[:, ch * CH:(ch + 1) * CH], 16)),
                inc=("sD0", 16))
            P.wait("vector", "sD0")
            for h16 in range(16):
                P.do("vector", lambda E, h16=h16: E.tensor_mul(
                    sqk[:, h16 * CH:(h16 + 1) * CH],
                    ktc2[:, h16 * CH:(h16 + 1) * CH],
                    ktc2[:, h16 * CH:(h16 + 1) * CH]),
                    inc=("sVE", 1) if h16 == 15 else None)
            P.wait("tensor", "sVE")
            for h16 in range(16):
                bank = psb[h16 % 2]
                bank2 = psb[2 + h16 % 2]
                P.do("tensor", lambda E, h16=h16, bank=bank: E.matmul(
                    bank[0:1, 0:CH], onesb[0:64, :],
                    sqk[0:64, h16 * CH:(h16 + 1) * CH],
                    start=True, stop=True))
                P.do("tensor", lambda E, h16=h16, bank2=bank2: E.matmul(
                    bank2[0:1, 0:CH], onesb[64:128, :],
                    sqk[64:128, h16 * CH:(h16 + 1) * CH],
                    start=True, stop=True), inc=("sMM", 1))
                P.wait("vector", "sMM")
                P.wait("vector", "sVE")
                P.do("vector", lambda E, h16=h16, bank=bank: E.reduce_max(
                    srow[0:1, 4:5], bank[0:1, 0:CH], axis=AX))
                P.dov(lambda E, h16=h16, bank2=bank2: E.reduce_max(
                    srow[0:1, 5:6], bank2[0:1, 0:CH], axis=AX))
                P.dov(lambda E, h16=h16: E.tensor_max(
                    bcolk[0:1, 2 * h16:2 * h16 + 1],
                    bcolk[0:1, 2 * h16:2 * h16 + 1], srow[0:1, 4:5]))
                P.do("vector", lambda E, h16=h16: E.tensor_max(
                    bcolk[0:1, 2 * h16 + 1:2 * h16 + 2],
                    bcolk[0:1, 2 * h16 + 1:2 * h16 + 2], srow[0:1, 5:6]),
                    inc=("sVE", 1))
                if h16 % 2 == 1:
                    P.wait("tensor", "sVE")
        P.barrier()
        if debug:
            P.wait("gpsimd", "sST")
            P.do("gpsimd", lambda E: E.dma_start(out=dbg["d_ktb"], in_=ktb_buf),
                 inc=("sGP", 16))
            P.do("gpsimd", lambda E: E.dma_start(out=dbg["d_vb"], in_=vb_buf),
                 inc=("sGP", 16))
            P.wait("gpsimd", "sGP")

        P.ckpt()
        def gather_switch(E):
            v = bass.Bass.values_load(nc, core_sb, engines=[mybir.EngineType.Pool],
                                      skip_runtime_bounds_check=True,
                                      min_val=0, max_val=7)
            reg = E.snap(v)
            for j in nc.Switch(engines=[E], index=[reg], n=NC8):
                for mm_ in range(M):
                    T = NC8 * mm_ + j
                    E.dma_start(out=x2own_buf[mm_ * 128:(mm_ + 1) * 128, :],
                                in_=x2_buf[T * 128:(T + 1) * 128, :]
                                ).then_inc(P.sem("sGP"), 16)

        if not P.frozen:
            P.sems["sGP"][1] += 16 * M
            P.emit("gpsimd", gather_switch)
        P.barrier()
        if debug:
            P.wait("gpsimd", "sST")
            P.do("gpsimd", lambda E: E.dma_start(out=dbg["d_x2own"], in_=x2own_buf),
                 inc=("sGP", 16))

        ln_pass(x2own_buf, ln3oT_buf, OWN)

        arena.reset()
        wsb_q = arena.take(16 * D, BF16)
        qto = arena.take(16 * OWN, BF16)
        lnTq = arena.take(16 * CH, BF16)
        load_w16(wsb_q, wbf["wqb"])
        for i, (t0, nt) in enumerate(chunks_of(OWN)):
            ntk = nt * 128
            P.wait("sync", "sMM")
            P.do("sync", lambda E, t0=t0, ntk=ntk: E.dma_start(
                out=r_sb(lnTq, 16)[:, :, 0:ntk],
                in_=r_in(ln3oT_buf[:, t0:t0 + ntk], 16)), inc=("sD0", 16))
            P.wait("tensor", "sD0")
            for h16 in range(16):
                bank = psb[h16 % 2]
                P.wait("tensor", "sVE")
                for k16 in range(16):
                    P.do("tensor", lambda E, h16=h16, k16=k16, bank=bank, ntk=ntk:
                         E.matmul(bank[:, 0:ntk],
                                  wt(wsb_q, k16)[:, h16 * 128:(h16 + 1) * 128],
                                  lnTq[:, k16 * CH:k16 * CH + ntk],
                                  start=(k16 == 0), stop=(k16 == 15)),
                         inc=("sMM", 1) if k16 == 15 else None)
                P.wait("vector", "sMM")
                P.do("vector", lambda E, h16=h16, t0=t0, ntk=ntk, bank=bank:
                     E.tensor_copy(qto[:, h16 * OWN + t0:h16 * OWN + t0 + ntk],
                                   bank[:, 0:ntk]), inc=("sVE", 1))
        sqq = lnTq
        P.wait("vector", "sVE")
        for i, (t0, nt) in enumerate(chunks_of(OWN)):
            ntk = nt * 128
            for h16 in range(16):
                P.do("vector", lambda E, h16=h16, t0=t0, ntk=ntk: E.tensor_mul(
                    sqq[:, h16 * CH:h16 * CH + ntk],
                    qto[:, h16 * OWN + t0:h16 * OWN + t0 + ntk],
                    qto[:, h16 * OWN + t0:h16 * OWN + t0 + ntk]),
                    inc=("sVE", 1) if h16 == 15 else None)
            P.wait("tensor", "sVE")
            for h16 in range(16):
                bank = psb[h16 % 2]
                bank2 = psb[2 + h16 % 2]
                P.do("tensor", lambda E, h16=h16, ntk=ntk, bank=bank: E.matmul(
                    bank[0:1, 0:ntk], onesb[0:64, :],
                    sqq[0:64, h16 * CH:h16 * CH + ntk],
                    start=True, stop=True))
                P.do("tensor", lambda E, h16=h16, ntk=ntk, bank2=bank2: E.matmul(
                    bank2[0:1, 0:ntk], onesb[64:128, :],
                    sqq[64:128, h16 * CH:h16 * CH + ntk],
                    start=True, stop=True), inc=("sMM", 1))
                P.wait("vector", "sMM")
                P.wait("vector", "sVE")
                P.do("vector", lambda E, ntk=ntk, bank=bank: E.reduce_max(
                    srow[0:1, 4:5], bank[0:1, 0:ntk], axis=AX))
                P.dov(lambda E, ntk=ntk, bank2=bank2: E.reduce_max(
                    srow[0:1, 5:6], bank2[0:1, 0:ntk], axis=AX))
                P.dov(lambda E, h16=h16: E.tensor_max(
                    bcolq[0:1, 2 * h16:2 * h16 + 1],
                    bcolq[0:1, 2 * h16:2 * h16 + 1], srow[0:1, 4:5]))
                P.do("vector", lambda E, h16=h16: E.tensor_max(
                    bcolq[0:1, 2 * h16 + 1:2 * h16 + 2],
                    bcolq[0:1, 2 * h16 + 1:2 * h16 + 2], srow[0:1, 5:6]),
                    inc=("sVE", 1))
                if h16 % 2 == 1:
                    P.wait("tensor", "sVE")
        P.wait("vector", "sVE")
        P.dov(lambda E: E.tensor_add(srow[0:1, 8:40], bcolq[0:1, 0:32],
                                     bcolk[0:1, 0:32]))
        P.do("vector", lambda E: E.tensor_scalar(
            srow[0:1, 8:40], srow[0:1, 8:40], -0.5 * SCALE_B, None, op0=ALU.mult),
            inc=("sVE", 1))
        P.wait("scalar", "sVE")
        P.do("scalar", lambda E: E.copy(srowb[0:1, 0:32], srow[0:1, 8:40]),
             inc=("sSC", 1))
        P.wait("tensor", "sSC")
        P.do("tensor", lambda E: E.matmul(psb[2][:, 0:32], onerow, srowb[0:1, 0:32],
                                          start=True, stop=True), inc=("sMM", 1))
        P.wait("scalar", "sMM")
        P.do("scalar", lambda E: E.copy(bcols, psb[2][:, 0:32]), inc=("sSC", 1))
        P.wait("tensor", "sSC")
        P.wait("vector", "sSC")

        P.ckpt()
        ktk = [arena.take(4 * 512, BF16), arena.take(4 * 512, BF16)]
        vtk = [arena.take(4 * 512, BF16), arena.take(4 * 512, BF16)]
        pb2 = arena.take(512, BF16)
        pb3 = arena.take(512, BF16)
        otall = arena.take(16 * OWN, BF16)
        rr = arena.take(256, F32)
        rwa2 = arena.take(128, F32)
        rwa3 = arena.take(128, F32)

        kv_free = [0, 0]
        for m in range(M):
            for hg in range(4):
                P.wait("tensor", "sVE")
                P.wait("tensor", "sSC")
                for b in range(2, 8):
                    P.do("tensor", lambda E, b=b: E.matmul(
                        psb[b][:, 0:512], onerow, zrow, start=True, stop=False),
                        inc=("sMM", 1) if b == 7 else None)
                nkc = m + 1
                for ikc in range(nkc):
                    kc = m - ikc
                    for hf in range(2):
                        it = ikc * 2 + hf
                        bi = it % 2
                        kt = ktk[bi]
                        vt = vtk[bi]
                        skv = "sKV0" if bi == 0 else "sKV1"
                        P.do("sync", lambda E, v=kv_free[bi]: E.wait_ge(
                            P.sem("sMM"), v))
                        P.do("sync", lambda E, hg=hg, kc=kc, hf=hf, kt=kt:
                             E.dma_start(
                                 out=r_sb(kt, 4),
                                 in_=r_in(ktb_buf[hg * 512:(hg + 1) * 512,
                                                  kc * 1024 + hf * 512:
                                                  kc * 1024 + (hf + 1) * 512], 4)),
                             inc=(skv, 16))
                        P.do("sync", lambda E, hg=hg, kc=kc, hf=hf, vt=vt:
                             E.dma_start(
                                 out=r_sb(vt, 4),
                                 in_=r_in(vb_buf[kc * 1024 + hf * 512:
                                                 kc * 1024 + (hf + 1) * 512,
                                                 hg * 512:(hg + 1) * 512], 4)),
                             inc=(skv, 16))
                        P.wait("tensor", skv)
                        last_it = (ikc == nkc - 1 and hf == 1)
                        for hl in range(4):
                            h = hg * 4 + hl
                            P.wait("tensor", "sSC")
                            P.wait("tensor", "sVE")
                            for kb in range(4):
                                for st_ in range(2):
                                    P.do("tensor", lambda E, hl=hl, kb=kb, st_=st_,
                                         kt=kt, m=m, h=h: E.matmul(
                                             psb[st_][:, kb * 128:(kb + 1) * 128],
                                             kt[st_ * 64:(st_ + 1) * 64,
                                                hl * 512 + kb * 128:
                                                hl * 512 + (kb + 1) * 128],
                                             qto[st_ * 64:(st_ + 1) * 64,
                                                 h * OWN + m * 128:
                                                 h * OWN + (m + 1) * 128],
                                             start=True, stop=True,
                                             tile_position=(st_ * 64, 0)),
                                         inc=("sMM", 1)
                                         if (kb == 3 and st_ == 1) else None)
                            P.wait("scalar", "sMM")
                            P.do("scalar", lambda E, h=h: E.activation(
                                pb2, psb[0][:, 0:512], AF.Exp,
                                bias=bcols[:, 2 * h:2 * h + 1], scale=SCALE_B))
                            P.do("scalar", lambda E, h=h: E.activation(
                                pb3, psb[1][:, 0:512], AF.Exp,
                                bias=bcols[:, 2 * h + 1:2 * h + 2], scale=SCALE_B),
                                inc=("sSC", 1))
                            if kc == m:
                                P.wait("vector", "sSC")
                                P.do("vector", lambda E, hf=hf: E.tensor_mul(
                                    pb2, pb2, dmsk[:, hf * 512:(hf + 1) * 512]))
                                P.do("vector", lambda E, hf=hf: E.tensor_mul(
                                    pb3, pb3, dmsk[:, hf * 512:(hf + 1) * 512]),
                                    inc=("sVE", 1))
                                P.wait("tensor", "sVE")
                            else:
                                P.wait("tensor", "sSC")
                            P.do("tensor", lambda E, hl=hl: E.matmul(
                                psb[4 + hl][0:1, 0:512], onesb, pb2,
                                start=False, stop=False))
                            P.do("tensor", lambda E, hl=hl: E.matmul(
                                psb[4 + hl][32:33, 0:512], onesb, pb3,
                                start=False, stop=False, tile_position=(0, 32)))
                            ob = psb[2 + hl // 2]
                            oc = (hl % 2) * 256
                            for kb in range(4):
                                P.do("tensor", lambda E, hl=hl, kb=kb, vt=vt, ob=ob,
                                     oc=oc: E.matmul(
                                         ob[:, oc:oc + 128],
                                         vt[:, kb * 512 + hl * 128:
                                            kb * 512 + hl * 128 + 128],
                                         pb2[:, kb * 128:(kb + 1) * 128],
                                         start=False, stop=False))
                                P.do("tensor", lambda E, hl=hl, kb=kb, vt=vt, ob=ob,
                                     oc=oc: E.matmul(
                                         ob[:, oc + 128:oc + 256],
                                         vt[:, kb * 512 + hl * 128:
                                            kb * 512 + hl * 128 + 128],
                                         pb3[:, kb * 128:(kb + 1) * 128],
                                         start=False, stop=False),
                                     inc=("sMM", 1) if kb == 3 else None)
                        kv_free[bi] = P.count("sMM")
                for b in range(2, 8):
                    P.do("tensor", lambda E, b=b: E.matmul(
                        psb[b][:, 0:512], onerow, zrow, start=False, stop=True),
                        inc=("sMM", 1) if b == 7 else None)
                for hl in range(4):
                    h = hg * 4 + hl
                    P.wait("vector", "sMM")
                    P.do("vector", lambda E, hl=hl: E.tensor_copy(
                        srow[0:1, 512:1024], psb[4 + hl][0:1, 0:512]))
                    P.dov(lambda E, hl=hl: E.tensor_copy(
                        srow2[0:1, 512:1024], psb[4 + hl][32:33, 0:512]))
                    P.do("vector", lambda E: E.tensor_add(
                        srow[0:1, 0:128], srow[0:1, 512:640], srow[0:1, 640:768]))
                    P.dov(lambda E: E.tensor_add(
                        srow[0:1, 128:256], srow[0:1, 768:896],
                        srow[0:1, 896:1024]))
                    P.dov(lambda E: E.tensor_add(
                        srow[0:1, 0:128], srow[0:1, 0:128], srow[0:1, 128:256]))
                    P.do("vector", lambda E: E.tensor_add(
                        srow2[0:1, 0:128], srow2[0:1, 512:640], srow2[0:1, 640:768]))
                    P.dov(lambda E: E.tensor_add(
                        srow2[0:1, 128:256], srow2[0:1, 768:896],
                        srow2[0:1, 896:1024]))
                    P.dov(lambda E: E.tensor_add(
                        srow[0:1, 128:256], srow2[0:1, 0:128],
                        srow2[0:1, 128:256]))
                    P.dov(lambda E: E.reciprocal(srow2[0:1, 0:256],
                                                 srow[0:1, 0:256]))
                    P.do("vector", lambda E, h=h: E.tensor_scalar(
                        srow2[0:1, 128:256], srow2[0:1, 128:256],
                        lams[0:1, h:h + 1], None, op0=ALU.mult), inc=("sVE", 1))
                    P.wait("scalar", "sVE")
                    P.do("scalar", lambda E: E.copy(srowb[0:1, 0:256],
                                                    srow2[0:1, 0:256]),
                         inc=("sSC", 1))
                    P.wait("tensor", "sSC")
                    P.do("tensor", lambda E: E.matmul(
                        psb[0][:, 0:256], onerow, srowb[0:1, 0:256],
                        start=True, stop=True), inc=("sMM", 1))
                    P.wait("vector", "sMM")
                    P.dov(lambda E: E.tensor_copy(rr, psb[0][:, 0:256]))
                    ob = psb[2 + hl // 2]
                    oc = (hl % 2) * 256
                    P.do("vector", lambda E, ob=ob, oc=oc: E.tensor_mul(
                        rwa2, ob[:, oc:oc + 128], rr[:, 0:128]))
                    P.do("vector", lambda E, ob=ob, oc=oc: E.tensor_mul(
                        rwa3, ob[:, oc + 128:oc + 256], rr[:, 128:256]),
                        inc=("sVE", 1))
                    P.wait("vector", "sVE")
                    P.do("vector", lambda E, h=h, m=m: E.tensor_sub(
                        otall[:, h * OWN + m * 128:h * OWN + (m + 1) * 128],
                        rwa2, rwa3), inc=("sVE", 1))
                    P.wait("tensor", "sVE")
        P.barrier()

        P.ckpt()
        wsb_ob = wsb_q
        xt2 = lnTq.bitcast(F32)
        load_w16(wsb_ob, wbf["wob"])
        for m in range(M):
            P.wait("sync", "sST")
            P.wait("sync", "sVE")
            P.do("sync", lambda E, m=m: E.dma_start(
                out=xt2[:, 0:D], in_=x2own_buf[m * 128:(m + 1) * 128, :]),
                inc=("sXT", 16))
            P.wait("vector", "sXT")
            P.wait("tensor", "sXT")
            for n4 in range(4):
                bank = psb[n4 % 2]
                P.wait("tensor", "sVE")
                for k16 in range(16):
                    P.do("tensor", lambda E, m=m, n4=n4, k16=k16, bank=bank:
                         E.matmul(bank[:, 0:512],
                                  otall[:, k16 * OWN + m * 128:
                                        k16 * OWN + m * 128 + 128],
                                  wt(wsb_ob, k16)[:, n4 * 512:(n4 + 1) * 512],
                                  start=(k16 == 0), stop=(k16 == 15)),
                         inc=("sMM", 1) if k16 == 15 else None)
                P.wait("vector", "sMM")
                P.do("vector", lambda E, n4=n4, bank=bank: E.tensor_add(
                    xt2[:, n4 * 512:(n4 + 1) * 512],
                    xt2[:, n4 * 512:(n4 + 1) * 512], bank[:, 0:512]),
                    inc=("sVE", 1))
            P.wait("sync", "sVE")
            P.do("sync", lambda E, m=m: E.dma_start(
                out=x3own_buf[m * 128:(m + 1) * 128, :], in_=xt2[:, 0:D]),
                inc=("sST", 16))
        P.barrier()
        if debug:
            P.wait("gpsimd", "sST")
            P.do("gpsimd", lambda E: E.dma_start(out=dbg["d_x3own"], in_=x3own_buf),
                 inc=("sGP", 16))

        P.ckpt()
        ln_pass(x3own_buf, ln4T_buf, OWN)
        ffn_pass(ln4T_buf, x3own_buf, x4own_buf, OWN)
        P.do("gpsimd", lambda E: E.dma_start(out=y, in_=x4own_buf), inc=("sGP", 16))
        P.wait("gpsimd", "sGP")
        P.barrier()

        @blk.sync
        def _(E):
            for f in P.ops["sync"]:
                f(E)

        @blk.scalar
        def _(E):
            for f in P.ops["scalar"]:
                f(E)

        @blk.vector
        def _(E):
            for f in P.ops["vector"]:
                f(E)

        @blk.tensor
        def _(E):
            for f in P.ops["tensor"]:
                f(E)

        @blk.gpsimd
        def _(E):
            for f in P.ops["gpsimd"]:
                f(E)

    return nc


def make_dmask(j):
    import ml_dtypes
    kb = np.arange(8)[:, None, None]
    p = np.arange(128)[None, :, None]
    w = np.arange(128)[None, None, :]
    m = (kb * 128 + p <= 128 * j + w).astype(np.float32)
    return np.ascontiguousarray(m.reshape(8 * 128, 128).astype(ml_dtypes.bfloat16))


def _prep_inputs(kw):
    x = np.ascontiguousarray(np.asarray(kw["L0_attn_a_hidden_states"],
                                        np.float32).reshape(-1, D))
    base = {
        "x0": x,
        "wqa": np.ascontiguousarray(np.asarray(kw["L0_attn_a_Wq"], np.float32)),
        "wka": np.ascontiguousarray(np.asarray(kw["L0_attn_a_Wk"], np.float32)),
        "wva": np.ascontiguousarray(np.asarray(kw["L0_attn_a_Wv"], np.float32)),
        "woa": np.ascontiguousarray(np.asarray(kw["L0_attn_a_Wo"], np.float32)),
        "wqb": np.ascontiguousarray(np.asarray(kw["L0_attn_b_Wq"], np.float32)),
        "wkb": np.ascontiguousarray(np.asarray(kw["L0_attn_b_Wk"], np.float32)),
        "wvb": np.ascontiguousarray(np.asarray(kw["L0_attn_b_Wv"], np.float32)),
        "wob": np.ascontiguousarray(np.asarray(kw["L0_attn_b_Wo"], np.float32)),
        "lam0": np.ascontiguousarray(np.asarray(kw["L0_attn_b_lambda_init"],
                                                np.float32).reshape(1, HEADS)),
        "wup": np.ascontiguousarray(np.asarray(kw["L0_ffn_up_weight"], np.float32)),
        "wdn": np.ascontiguousarray(np.asarray(kw["L0_ffn_down_weight"],
                                               np.float32)),
        "scup": np.ascontiguousarray(np.asarray(kw["L0_ffn_up_scale"], np.float32)
                                     .reshape(FFN_H // 128, 128).T),
        "scdn": np.ascontiguousarray(np.asarray(kw["L0_ffn_down_scale"],
                                                np.float32).reshape(1, D)),
        "scout": np.ascontiguousarray(np.asarray(kw["L0_ffn_output_scale"],
                                                 np.float32).reshape(1, 1)),
    }
    return x.shape[0], base


def kernel(**kw):
    S, base = _prep_inputs(kw)
    M = S // 128 // NC8
    nc = build_program(S, debug=False)
    in_maps = []
    for j in range(NC8):
        mdict = dict(base)
        mdict["core_slot"] = np.array([[j]], np.int32)
        mdict["dmask"] = make_dmask(j)
        mdict["rampi"] = np.broadcast_to(np.arange(CH, dtype=np.float32),
                                         (128, CH)).copy()
        in_maps.append(mdict)
    res = run_bass_kernel_spmd(nc, in_maps, core_ids=list(range(NC8)))
    out = np.empty((S, D), np.float32)
    for j in range(NC8):
        yj = res.results[j]["y"]
        for mm_ in range(M):
            T = NC8 * mm_ + j
            out[T * 128:(T + 1) * 128, :] = yj[mm_ * 128:(mm_ + 1) * 128, :]
    return out.reshape(1, S, D)
